# version 03
# baseline (speedup 1.0000x reference)
import numpy as np
import ml_dtypes
from contextlib import ExitStack

import concourse.bass as bass
import concourse.mybir as mybir
from concourse.bass_utils import run_bass_kernel_spmd

F32 = mybir.dt.float32
BF16 = mybir.dt.bfloat16
I32 = mybir.dt.int32
AF = mybir.ActivationFunctionType
ALU = mybir.AluOpType
AX = mybir.AxisListType

NCORES = 8
BPC = 2
S = 2048
D = 1024
T = BPC * S
NT = T // 128
NTB = S // 128

ENGINES = ("tensor", "vector", "scalar", "gpsimd", "sync")
BF = ml_dtypes.bfloat16


class Ctr:
    def __init__(self, handle, name):
        self.h = handle
        self.name = name
        self.n = 0


class Build:
    def __init__(self, nc, ctx):
        self.nc = nc
        self.ctx = ctx
        self.q = {e: [] for e in ENGINES}
        self.sems = {}
        self.dram = {}
        self.inputs = {}
        self.sb = {}
        self.uid = 0
        self.last_inc = {e: None for e in ENGINES}
        self.dma_ctrs = {"sync": {}, "gpsimd": {}}

    def do(self, eng, fn, inc=None, waits=()):
        waits = [(c, (c.n if v is None else v)) for (c, v) in waits]
        if inc is not None:
            ctr, k = inc
            ctr.n += k
            tgt = (ctr, k)
            self.last_inc[eng] = (ctr, ctr.n)
            if eng in self.dma_ctrs:
                self.dma_ctrs[eng][ctr.name] = ctr
        else:
            tgt = None

        def closure(E, fn=fn, waits=waits, tgt=tgt):
            for c, v in waits:
                if v > 0:
                    E.wait_ge(c.h, v)
            r = fn(E)
            if tgt is not None:
                c, k = tgt
                r.then_inc(c.h, k)
        self.q[eng].append(closure)

    def wait(self, eng, ctr, val=None):
        v = ctr.n if val is None else val
        if v > 0:
            self.q[eng].append(lambda E, h=ctr.h, v=v: E.wait_ge(h, v))

    def emit(self, block):
        for eng in ENGINES:
            dec = getattr(block, eng)

            def make(eng_):
                def _(E):
                    for fn in self.q[eng_]:
                        fn(E)
                return _
            dec(make(eng_=eng))

    def sem(self, name):
        if name not in self.sems:
            h = self.ctx.enter_context(self.nc.semaphore(name))
            self.sems[name] = Ctr(h, name)
        return self.sems[name]

    def din(self, name, shape, dtype=F32):
        if name not in self.inputs:
            self.inputs[name] = self.nc.dram_tensor(name, list(shape), dtype, kind="ExternalInput").ap()
        return self.inputs[name]

    def dout(self, name, shape, dtype=F32):
        self.inputs[name] = self.nc.dram_tensor(name, list(shape), dtype, kind="ExternalOutput").ap()
        return self.inputs[name]

    def dscratch(self, name, shape, dtype=F32):
        if name not in self.dram:
            self.dram[name] = self.nc.dram_tensor(name, list(shape), dtype, kind="Internal").ap()
        return self.dram[name]

    def sbuf(self, name, shape, dtype=F32):
        if name not in self.sb:
            self.sb[name] = self.ctx.enter_context(self.nc.sbuf_tensor(name, list(shape), dtype))
        return self.sb[name]

    def psum(self, name, shape, dtype=F32):
        if name not in self.sb:
            self.sb[name] = self.ctx.enter_context(self.nc.psum_tensor(name, list(shape), dtype))
        return self.sb[name]

    def barrier(self, tag):
        beats = {e: self.sem(f"beat_{e}") for e in ENGINES}
        PC, IDB = self.sb["PC"], self.sb["IDB"]
        BT = self.sb["BEAT"]
        btd = self.dram["beat_dram"]
        self.do("vector", lambda E: E.memset(BT[0:1, 0:64], 0.0), inc=(beats["vector"], 1),
                waits=[(beats["vector"], None)])
        self.do("scalar", lambda E: E.memzero(BT[0:1, 64:128]), inc=(beats["scalar"], 1),
                waits=[(beats["scalar"], None)])
        for eng, dst in (("sync", slice(128, 256)), ("gpsimd", slice(256, 384))):
            for c in list(self.dma_ctrs[eng].values()):
                self.wait(eng, c)
            self.do(eng, lambda E, dst=dst: E.dma_start(out=BT[0:1, dst], in_=btd),
                    inc=(beats[eng], 16), waits=[(beats[eng], None)])
        self.do("tensor", lambda E: E.matmul(PC[0:1, 508:512], IDB[0:1, 0:1], IDB[0:1, 0:4],
                start=True, stop=True), inc=(beats["tensor"], 1),
                waits=[(beats["tensor"], None), (beats["vector"], None), (beats["scalar"], None),
                       (beats["sync"], None), (beats["gpsimd"], None)])
        for e in ENGINES:
            for e2 in ENGINES:
                if e2 != e:
                    self.wait(e, beats[e2])


def ceil_div(a, b):
    return (a + b - 1) // b



def make_constants():
    c = {}
    c["identb"] = np.eye(128, dtype=BF)
    c["identf"] = np.eye(128, dtype=np.float32)
    inv_freq = 10000.0 ** (-(np.arange(32, dtype=np.float64) / 32.0))
    ang = np.arange(S, dtype=np.float64)[:, None] * inv_freq[None, :]
    c["ropeC"] = np.concatenate([np.cos(ang), np.cos(ang)], 1).astype(np.float32)
    c["ropeS"] = np.concatenate([-np.sin(ang), np.sin(ang)], 1).astype(np.float32)
    idx = np.arange(128)
    c["utrib"] = (idx[:, None] <= idx[None, :]).astype(BF)
    c["utrif"] = (idx[:, None] <= idx[None, :]).astype(np.float32)
    c["onescolb"] = np.ones((128, 1), dtype=BF)
    for off in (-384, -256, -128, 0, 128, 256, 384):
        m = ((off + idx[:, None]) <= np.arange(512)[None, :]).astype(BF)
        c[f"cm{off}"] = m
    return c


CONSTS = make_constants()



def emit_ln(bd: Build, layer, x_src):
    xn_dram = bd.dscratch("xn_dram", (T, D), BF16)
    quant = (layer == 3)
    if quant:
        xdt_dram = bd.dscratch("xdt_dram", (D, T), F32)
        PC = bd.sb["PC"]
        identf = bd.sb["IDF"]

    nc = bd.nc
    lctx = ExitStack()
    LNX = lctx.enter_context(nc.sbuf_tensor(f"ln_x{layer}", [128, 2 * D], F32))
    SCR = lctx.enter_context(nc.sbuf_tensor(f"ln_scr{layer}", [128, 2 * D], F32))
    XNB = lctx.enter_context(nc.sbuf_tensor(f"ln_xnb{layer}", [128, 4 * D], BF16))
    ST = lctx.enter_context(nc.sbuf_tensor(f"ln_st{layer}", [128, 16], F32))
    s_ld = [bd.sem("lnld0"), bd.sem("lnld1")]
    s_sq = bd.sem("lnsq")
    s_v = bd.sem("lnv")
    s_st = [bd.sem("lnst0"), bd.sem("lnst1")]
    s_t = bd.sem("lnt")
    if quant:
        QS = lctx.enter_context(nc.sbuf_tensor("q_scr", [128, 4 * D], F32))
        QI = lctx.enter_context(nc.sbuf_tensor("q_int", [128, 2 * D], I32))
        QM = lctx.enter_context(nc.sbuf_tensor("q_small", [128, 2 * 3 * 32], F32))
        QXD = lctx.enter_context(nc.sbuf_tensor("q_xd", [128, 2 * D], F32))
        QSG = lctx.enter_context(nc.sbuf_tensor("q_sgn", [128, 2 * D], F32))
        QDT = lctx.enter_context(nc.sbuf_tensor("q_xdt", [128, 2 * D], F32))

    vdone, sqdone, stdone = [], [], []
    for i in range(NT):
        bf = i % 2
        xs = slice(bf * D, (bf + 1) * D)
        c0 = bf * 8
        guards = [(s_ld, None)] if False else []
        lw = []
        if i >= 2:
            lw = [(s_v, vdone[i - 2]), (s_sq, sqdone[i - 2])]
        bd.do("sync", lambda E, i=i, xs=xs: E.dma_start(out=LNX[:, xs], in_=x_src[i * 128:(i + 1) * 128, :]),
              inc=(s_ld[bf], 16), waits=lw)
        ldn = s_ld[bf].n
        bd.do("scalar", lambda E, xs=xs, c0=c0: E.activation(SCR[:, xs], LNX[:, xs], AF.Square,
              accum_out=ST[:, c0 + 1:c0 + 2]), inc=(s_sq, 1), waits=[(s_ld[bf], ldn)])
        bd.do("vector", lambda E, xs=xs, c0=c0: E.tensor_reduce(ST[:, c0:c0 + 1], LNX[:, xs], axis=AX.X, op=ALU.add),
              inc=(s_v, 1), waits=[(s_ld[bf], ldn)])
        bd.do("vector", lambda E, c0=c0: E.tensor_scalar(ST[:, c0 + 2:c0 + 3], ST[:, c0:c0 + 1], 1.0 / D, None, op0=ALU.mult),
              inc=(s_v, 1), waits=[(s_v, None)])
        bd.do("vector", lambda E, c0=c0: E.tensor_tensor(ST[:, c0 + 3:c0 + 4], ST[:, c0 + 2:c0 + 3], ST[:, c0 + 2:c0 + 3], op=ALU.mult),
              inc=(s_v, 1), waits=[(s_v, None)])
        bd.do("vector", lambda E, c0=c0: E.scalar_tensor_tensor(ST[:, c0 + 4:c0 + 5], ST[:, c0 + 1:c0 + 2], 1.0 / D,
              ST[:, c0 + 3:c0 + 4], op0=ALU.mult, op1=ALU.subtract), inc=(s_v, 1), waits=[(s_sq, None), (s_v, None)])
        bd.do("vector", lambda E, c0=c0: E.tensor_scalar(ST[:, c0 + 5:c0 + 6], ST[:, c0 + 4:c0 + 5], 1e-5, None, op0=ALU.add),
              inc=(s_v, 1), waits=[(s_v, None)])
        bd.do("vector", lambda E, c0=c0: E.reciprocal(ST[:, c0 + 6:c0 + 7], ST[:, c0 + 5:c0 + 6]),
              inc=(s_v, 1), waits=[(s_v, None)])
        bd.do("scalar", lambda E, c0=c0: E.activation(ST[:, c0 + 7:c0 + 8], ST[:, c0 + 6:c0 + 7], AF.Sqrt),
              inc=(s_sq, 1), waits=[(s_v, None)])
        xw = [(s_sq, None)] + ([(s_st[bf], None)] if i >= 2 else [])
        bd.do("vector", lambda E, xs=xs, c0=c0: E.tensor_scalar(SCR[:, xs], LNX[:, xs], ST[:, c0 + 2:c0 + 3],
              ST[:, c0 + 7:c0 + 8], op0=ALU.subtract, op1=ALU.mult), inc=(s_v, 1), waits=xw)
        bd.do("vector", lambda E, xs=xs: E.tensor_copy(XNB[:, xs], SCR[:, xs]),
              inc=(s_v, 1), waits=[(s_v, None)])
        los = slice(2 * D + xs.start, 2 * D + xs.stop)
        bd.do("vector", lambda E, xs=xs, los=los: E.tensor_tensor(XNB[:, los], SCR[:, xs], XNB[:, xs], op=ALU.subtract),
              inc=(s_v, 1), waits=[(s_v, None)])
        bd.do("sync", lambda E, i=i, xs=xs: E.dma_start(out=xn_dram[i * 128:(i + 1) * 128, :], in_=XNB[:, xs]),
              inc=(s_st[bf], 16), waits=[(s_v, None)])
        bd.do("sync", lambda E, i=i, los=los: E.dma_start(out=bd.dram["xnlo_dram"][i * 128:(i + 1) * 128, :], in_=XNB[:, los]),
              inc=(s_st[bf], 16), waits=[(s_v, None)])

        if quant:
            ys = slice(bf * 2 * D, bf * 2 * D + D)
            ds = slice(bf * 2 * D + D, (bf + 1) * 2 * D)
            m0 = bf * 96
            bd.do("vector", lambda E, xs=xs, ys=ys, c0=c0: E.tensor_scalar(QS[:, ys], LNX[:, xs], ST[:, c0 + 2:c0 + 3],
                  ST[:, c0 + 7:c0 + 8], op0=ALU.subtract, op1=ALU.mult), inc=(s_v, 1), waits=[(s_sq, None)])
            bd.do("vector", lambda E, ys=ys, m0=m0: E.tensor_reduce(QM[:, m0:m0 + 32],
                  QS[:, ys].rearrange("p (b c) -> p b c", c=32), axis=AX.X, op=ALU.max, apply_absolute_value=True),
                  inc=(s_v, 1), waits=[(s_v, None)])
            bd.do("vector", lambda E, m0=m0: E.tensor_scalar(QM[:, m0 + 32:m0 + 64], QM[:, m0:m0 + 32], 1.0 / 7.5, 1e-30,
                  op0=ALU.mult, op1=ALU.max), inc=(s_v, 1), waits=[(s_v, None)])
            bd.do("vector", lambda E, m0=m0: E.reciprocal(QM[:, m0 + 64:m0 + 96], QM[:, m0 + 32:m0 + 64]),
                  inc=(s_v, 1), waits=[(s_v, None)])
            bd.do("vector", lambda E, ys=ys, m0=m0: E.tensor_tensor(QS[:, ys].rearrange("p (b c) -> p b c", c=32),
                  QS[:, ys].rearrange("p (b c) -> p b c", c=32),
                  QM[:, m0 + 64:m0 + 96].unsqueeze(2).to_broadcast([128, 32, 32]), op=ALU.mult),
                  inc=(s_v, 1), waits=[(s_v, None)])
            bd.do("scalar", lambda E, ys=ys, xs=xs: E.activation(QSG[:, xs], QS[:, ys], AF.Sign),
                  inc=(s_sq, 1), waits=[(s_v, None)])
            bd.do("scalar", lambda E, ys=ys, xs=xs: E.activation(QXD[:, xs], QS[:, ys], AF.Abs),
                  inc=(s_sq, 1), waits=[(s_v, None)])
            bd.do("vector", lambda E, xs=xs, ds=ds: E.tensor_scalar(QS[:, ds], QXD[:, xs], 2.0, None, op0=ALU.is_ge),
                  inc=(s_v, 1), waits=[(s_sq, None)])
            bd.do("vector", lambda E, ds=ds: E.tensor_scalar(QS[:, ds], QS[:, ds], -4.0, 8.0, op0=ALU.mult, op1=ALU.add),
                  inc=(s_v, 1), waits=[(s_v, None)])
            bd.do("vector", lambda E, xs=xs: E.tensor_scalar(QDT[:, xs], QXD[:, xs], 4.0, None, op0=ALU.is_ge),
                  inc=(s_v, 1), waits=[(s_sq, None)])
            bd.do("vector", lambda E, xs=xs, ds=ds: E.scalar_tensor_tensor(QS[:, ds], QDT[:, xs], -2.0, QS[:, ds],
                  op0=ALU.mult, op1=ALU.add), inc=(s_v, 1), waits=[(s_v, None)])
            bd.do("vector", lambda E, xs=xs, ds=ds: E.tensor_tensor(QXD[:, xs], QXD[:, xs], QS[:, ds], op=ALU.mult),
                  inc=(s_v, 1), waits=[(s_v, None)])
            bd.do("vector", lambda E, xs=xs: E.tensor_copy(QI[:, xs], QXD[:, xs]),
                  inc=(s_v, 1), waits=[(s_v, None)])
            bd.do("vector", lambda E, xs=xs: E.tensor_copy(QXD[:, xs], QI[:, xs]),
                  inc=(s_v, 1), waits=[(s_v, None)])
            bd.do("vector", lambda E, ds=ds: E.reciprocal(QS[:, ds], QS[:, ds]),
                  inc=(s_v, 1), waits=[(s_v, None)])
            bd.do("vector", lambda E, xs=xs, ds=ds: E.tensor_tensor(QXD[:, xs], QXD[:, xs], QS[:, ds], op=ALU.mult),
                  inc=(s_v, 1), waits=[(s_v, None)])
            bd.do("vector", lambda E, xs=xs: E.tensor_tensor(QXD[:, xs], QXD[:, xs], QSG[:, xs], op=ALU.mult),
                  inc=(s_v, 1), waits=[(s_v, None)])
            bd.do("vector", lambda E, xs=xs, m0=m0: E.tensor_tensor(QXD[:, xs].rearrange("p (b c) -> p b c", c=32),
                  QXD[:, xs].rearrange("p (b c) -> p b c", c=32),
                  QM[:, m0 + 32:m0 + 64].unsqueeze(2).to_broadcast([128, 32, 32]), op=ALU.mult),
                  inc=(s_v, 1), waits=[(s_v, None)])
            for g2 in range(2):
                for cc in range(4):
                    c = g2 * 4 + cc
                    tw = [(s_v, None)] if cc == 0 else []
                    bd.do("tensor", lambda E, xs=xs, c=c, cc=cc: E.transpose(PC[:, cc * 128:(cc + 1) * 128],
                          QXD[:, xs.start + c * 128: xs.start + (c + 1) * 128], identf[:]),
                          inc=(s_t, 1), waits=tw)
                bd.do("vector", lambda E, g2=g2, xs=xs: E.tensor_copy(QDT[:, xs.start + g2 * 512: xs.start + (g2 + 1) * 512], PC[:]),
                      inc=(s_v, 1), waits=[(s_t, None), (s_st[bf], None)])
                if g2 == 0:
                    pass
            bd.do("sync", lambda E, i=i, xs=xs: E.dma_start(
                  out=xdt_dram.rearrange("(c p) t -> p c t", p=128)[:, :, i * 128:(i + 1) * 128],
                  in_=QDT[:, xs].rearrange("p (c m) -> p c m", c=8)),
                  inc=(s_st[bf], 16), waits=[(s_v, None)])
        vdone.append(s_v.n)
        sqdone.append(s_sq.n)
        stdone.append(0)
    bd.wait("sync", s_st[0])
    bd.wait("sync", s_st[1])
    lctx.close()
    bd.barrier(f"ln{layer}")



def emit_ffn(bd: Build, layer, cfg, x_src, x_dst):
    E, H = cfg["E"], cfg["H"]
    NHC = H // 128
    G = cfg.get("G", 1024)
    NG = T // G
    NTG = G // 128
    xn_dram = bd.dram["xn_dram"]
    s_dram = bd.dram["s_dram"]
    w1, w2 = bd.wprep[layer]
    has_b = cfg.get("b1") is not None
    if has_b:
        b1c = bd.din(cfg["b1"] + "_c", (E, 128, NHC))
        b2 = bd.din(cfg["b2"], (E, D))
    l2 = cfg.get("l2_mode", False)

    PA, PB, PC = bd.sb["PA"], bd.sb["PB"], bd.sb["PC"]
    IDB = bd.sb["IDB"]
    C_sb = bd.sb["C_sb"]
    CT_sb = bd.sb["CT_sb"]

    nc = bd.nc
    fctx = ExitStack()
    XG = fctx.enter_context(nc.sbuf_tensor(f"ff_xg{layer}", [128, NTG * D], BF16))
    ACC = fctx.enter_context(nc.sbuf_tensor(f"ff_acc{layer}", [128, NTG * D], F32))
    XINT = fctx.enter_context(nc.sbuf_tensor(f"ff_xint{layer}", [128, 2 * 8 * G], BF16))
    HT = fctx.enter_context(nc.sbuf_tensor(f"ff_ht{layer}", [128, NHC * G], BF16))
    W1C = fctx.enter_context(nc.sbuf_tensor(f"ff_w1{layer}", [128, 4 * 1024], BF16))
    W2SB = fctx.enter_context(nc.sbuf_tensor(f"ff_w2{layer}", [128, 2 * NHC * D], BF16))
    B1SB = fctx.enter_context(nc.sbuf_tensor(f"ff_b1{layer}", [128, 2 * 16], F32))
    B2SB = fctx.enter_context(nc.sbuf_tensor(f"ff_b2{layer}", [16, D], BF16))
    DG = fctx.enter_context(nc.sbuf_tensor(f"ff_dg{layer}", [128, 2 * 128], BF16))
    CMBX = fctx.enter_context(nc.sbuf_tensor(f"cmb_x{layer}", [128, D], F32))
    CMBS = fctx.enter_context(nc.sbuf_tensor(f"cmb_s{layer}", [128, D], F32))
    CMBG = fctx.enter_context(nc.sbuf_tensor(f"cmb_g{layer}", [128, D], F32))
    SQS = CMBG
    LST = fctx.enter_context(nc.sbuf_tensor(f"ff_lst{layer}", [128, 16], F32))

    s_w = bd.sem("ffw")
    s_w1 = [bd.sem("ffw1a"), bd.sem("ffw1b"), bd.sem("ffw1c"), bd.sem("ffw1d")]
    s_w2 = [bd.sem("ffw2a"), bd.sem("ffw2b")]
    s_b1 = [bd.sem("ffb1a"), bd.sem("ffb1b")]
    s_cl2 = [bd.sem("ffcla"), bd.sem("ffclb")]
    s_ot = bd.sem("ffot")
    s_x = bd.sem("ffx")
    s_t = bd.sem("fft")
    s_g = bd.sem("ffg")
    s_v = bd.sem("ffv")
    s_st = bd.sem("ffst")
    s_cl = bd.sem("ffcl")

    w1_last_mm = [0, 0, 0, 0]
    w2_last_mm = [0, 0]
    b1_last_g = [0, 0]


    s_b2l = bd.sem("ffb2l")
    if has_b:
        bd.do("gpsimd", lambda E_: E_.dma_start(out=B2SB[:E, :], in_=b2), inc=(s_b2l, 16))
    if l2:
        gam = bd.din(cfg["gamma"], (1, D))
        bet = bd.din(cfg["beta"], (1, D))
        GB = fctx.enter_context(nc.sbuf_tensor("l2_gb_rows", [1, 2 * D], F32))
        GMB = fctx.enter_context(nc.sbuf_tensor("l2_gamma_b", [128, 2 * D], BF16))
        ONESRF = bd.sb["ONESRF"]
        bd.do("sync", lambda E_: E_.dma_start(out=GB[:, 0:D], in_=gam), inc=(s_ot, 16))
        bd.do("sync", lambda E_: E_.dma_start(out=GB[:, D:2 * D], in_=bet), inc=(s_ot, 16))
        xw = s_ot.n
        for hb in range(2):
            bd.do("tensor", lambda E_, hb=hb: E_.matmul(PB[:, hb * 512:(hb + 1) * 512], ONESRF[:],
                  GB[:, hb * 512:(hb + 1) * 512], start=True, stop=True),
                  inc=(s_t, 1), waits=([(s_ot, xw)] if hb == 0 else []))
            bd.do("tensor", lambda E_, hb=hb: E_.matmul(PA[:, hb * 512:(hb + 1) * 512], ONESRF[:],
                  GB[:, D + hb * 512: D + (hb + 1) * 512], start=True, stop=True), inc=(s_t, 1))
        bd.do("vector", lambda E_: E_.tensor_copy(GMB[:, 0:D], PB[:]), inc=(s_v, 1), waits=[(s_t, None)])
        bd.do("vector", lambda E_: E_.tensor_copy(GMB[:, D:2 * D], PA[:, 0:D]), inc=(s_v, 1), waits=[(s_t, None)])

    for g in range(NG):
        t0 = g * NTG
        ldw = [(s_t, None)] if g > 0 else []
        bd.do("sync", lambda E_, g=g: E_.dma_start(out=XG[:].rearrange("p (i d) -> p i d", d=D),
              in_=xn_dram.rearrange("(i p) d -> p i d", p=128)[:, g * NTG:(g + 1) * NTG, :]),
              inc=(s_x, 16), waits=ldw)
        xg_ready = s_x.n
        for tt in range(NTG):
            ti = t0 + tt
            if has_b:
                for db in range(2):
                    bd.do("tensor", lambda E_, ti=ti, db=db: E_.matmul(PB[:, db * 512:(db + 1) * 512],
                          CT_sb[0:E, ti * 128:(ti + 1) * 128],
                          B2SB[0:E, db * 512:(db + 1) * 512], start=True, stop=True), inc=(s_t, 1),
                          waits=([(s_x, xg_ready), (s_b2l, None), (s_v, None)] if db == 0 else []))
                bd.do("vector", lambda E_, tt=tt: E_.tensor_copy(ACC[:, tt * D:(tt + 1) * D], PB[:]),
                      inc=(s_v, 1), waits=[(s_t, None)])
            else:
                bd.do("vector", lambda E_, tt=tt: E_.memset(ACC[:, tt * D:(tt + 1) * D], 0.0), inc=(s_v, 1))

        xin_ready = {}
        mm1_done = {}

        def emit_xin(e):
            xb = (e % 2) * 8 * G
            for tt in range(NTG):
                ti = t0 + tt
                dbf = tt % 2
                xw2 = [(s_x, xg_ready)]
                if e >= 2:
                    xw2.append((s_t, mm1_done[e - 2]))
                bd.do("vector", lambda E_, ti=ti, e=e, dbf=dbf: E_.tensor_scalar(
                      DG[:, dbf * 128:(dbf + 1) * 128], IDB[:], C_sb[:, ti * E + e: ti * E + e + 1], None, op0=ALU.mult),
                      inc=(s_v, 1), waits=xw2 + [(s_v, None)])
                for r2 in range(2):
                    for c4 in range(4):
                        c = r2 * 4 + c4
                        tw = [(s_v, None), (s_g, None)] if c4 == 0 else []
                        bd.do("tensor", lambda E_, tt=tt, c=c, c4=c4, dbf=dbf: E_.matmul(
                              PC[:, c4 * 128:(c4 + 1) * 128], XG[:, tt * D + c * 128: tt * D + (c + 1) * 128],
                              DG[:, dbf * 128:(dbf + 1) * 128], start=True, stop=True),
                              inc=(s_t, 1), waits=tw)
                    eng = "vector" if (tt + r2) % 2 == 0 else "scalar"
                    ctrx = s_v if eng == "vector" else s_g

                    def evac_fn(E_, tt=tt, r2=r2, xb=xb, eng=eng):
                        src2 = PC[:, 0:512].rearrange("p (c m) -> p c m", c=4)
                        dv = XINT[:, xb:xb + 8 * G].rearrange("p (c t) -> p c t", c=8)[:, r2 * 4:(r2 + 1) * 4, tt * 128:(tt + 1) * 128]
                        if eng == "vector":
                            return E_.tensor_copy(dv, src2)
                        return E_.copy(dv, src2)
                    bd.do(eng, evac_fn, inc=(ctrx, 1), waits=[(s_t, None)])
            xin_ready[e] = (s_v.n, s_g.n)

        emit_xin(0)
        if E > 1:
            emit_xin(1)
        for e in range(E):
            ebuf = e % 2
            xb = (e % 2) * 8 * G
            bd.do("gpsimd", lambda E_, e=e, ebuf=ebuf: E_.dma_start(
                  out=W2SB[:, ebuf * NHC * D: (ebuf + 1) * NHC * D].rearrange("p (c d) -> p c d", d=D),
                  in_=w2[e].rearrange("(c p) d -> p c d", p=128)),
                  inc=(s_w2[ebuf], 16), waits=[(s_t, w2_last_mm[ebuf]), (bd.s_wp, None)])
            if has_b:
                bd.do("gpsimd", lambda E_, e=e, ebuf=ebuf: E_.dma_start(
                      out=B1SB[:, ebuf * 16: ebuf * 16 + NHC], in_=b1c[e]),
                      inc=(s_b1[ebuf], 16), waits=[(s_g, b1_last_g[ebuf])])
            w2_ready = s_w2[ebuf].n
            b1_ready = s_b1[ebuf].n
            xin_v, xin_g = xin_ready[e]
            GB512 = G // 512

            def load_w1(hc, e=e):
                wbuf = hc % 4
                bd.do("gpsimd", lambda E_, e=e, hc=hc, wbuf=wbuf: E_.dma_start(
                      out=W1C[:, wbuf * 1024: (wbuf + 1) * 1024].rearrange("p (c h) -> p c h", c=8),
                      in_=w1[e].rearrange("(c p) h -> p c h", p=128)[:, :, hc * 128:(hc + 1) * 128]),
                      inc=(s_w1[wbuf], 16), waits=[(s_t, w1_last_mm[wbuf]), (bd.s_wp, None)])
                return s_w1[wbuf].n

            w1rdys = {}
            for hc0 in range(min(3, NHC)):
                w1rdys[hc0] = load_w1(hc0)
            for hc in range(NHC):
                pp = hc % 2
                wbuf = hc % 4
                if hc + 3 < NHC:
                    w1rdys[hc + 3] = load_w1(hc + 3)
                w1rdy = w1rdys[hc]
                mmw = [(s_w1[wbuf], w1rdy), (s_v, xin_v), (s_g, xin_g)]
                if hc >= 2:
                    mmw.append((s_g, None))
                for tb in range(GB512):
                    for dc in range(8):
                        w_ = mmw if (tb == 0 and dc == 0) else []
                        bd.do("tensor", lambda E_, pp=pp, tb=tb, dc=dc, wbuf=wbuf, xb=xb: E_.matmul(
                              PA[:, pp * G + tb * 512: pp * G + tb * 512 + 512],
                              W1C[:, wbuf * 1024 + dc * 128: wbuf * 1024 + (dc + 1) * 128],
                              XINT[:, xb + dc * G + tb * 512: xb + dc * G + tb * 512 + 512],
                              start=(dc == 0), stop=(dc == 7)), inc=(s_t, 1), waits=w_)
                w1_last_mm[wbuf] = s_t.n

                def gelu_fn(E_, hc=hc, pp=pp, ebuf=ebuf):
                    bias = B1SB[:, ebuf * 16 + hc: ebuf * 16 + hc + 1] if has_b else 0.0
                    return E_.activation(HT[:, hc * G:(hc + 1) * G], PA[:, pp * G:(pp + 1) * G],
                                         AF.Gelu, bias=bias)
                bd.do("scalar", gelu_fn, inc=(s_g, 1), waits=[(s_t, None), (s_b1[ebuf], b1_ready) if has_b else (s_t, 0)])
                if has_b:
                    b1_last_g[ebuf] = s_g.n
                g0w = [(s_g, None)]
                if hc == 0:
                    g0w += [(s_w2[ebuf], w2_ready), (s_v, None)]
                for db in range(2):
                    bd.do("tensor", lambda E_, db=db, hc=hc, ebuf=ebuf: E_.matmul(
                          PB[:, db * 512:(db + 1) * 512],
                          HT[:, hc * G + 0: hc * G + 128],
                          W2SB[:, ebuf * NHC * D + hc * D + db * 512: ebuf * NHC * D + hc * D + db * 512 + 512],
                          start=(hc == 0), stop=(hc == NHC - 1)), inc=(s_t, 1),
                          waits=(g0w if db == 0 else []))
            ht_ready = s_g.n
            mm1_done[e] = s_t.n
            if e + 2 < E:
                emit_xin(e + 2)
            bd.do("vector", lambda E_, ti=t0, e=e: E_.scalar_tensor_tensor(
                  ACC[:, 0:D], PB[:], C_sb[:, ti * E + e: ti * E + e + 1],
                  ACC[:, 0:D], op0=ALU.mult, op1=ALU.add),
                  inc=(s_v, 1), waits=[(s_t, None)])
            for tt in range(1, NTG):
                ti = t0 + tt
                mmw = [(s_g, ht_ready), (s_v, None)]
                for db in range(2):
                    for hc in range(NHC):
                        w_ = mmw if (db == 0 and hc == 0) else []
                        bd.do("tensor", lambda E_, tt=tt, db=db, hc=hc, ebuf=ebuf: E_.matmul(
                              PB[:, db * 512:(db + 1) * 512],
                              HT[:, hc * G + tt * 128: hc * G + (tt + 1) * 128],
                              W2SB[:, ebuf * NHC * D + hc * D + db * 512: ebuf * NHC * D + hc * D + db * 512 + 512],
                              start=(hc == 0), stop=(hc == NHC - 1)), inc=(s_t, 1), waits=w_)
                bd.do("vector", lambda E_, tt=tt, ti=ti, e=e: E_.scalar_tensor_tensor(
                      ACC[:, tt * D:(tt + 1) * D], PB[:], C_sb[:, ti * E + e: ti * E + e + 1],
                      ACC[:, tt * D:(tt + 1) * D], op0=ALU.mult, op1=ALU.add),
                      inc=(s_v, 1), waits=[(s_t, None)])
            w2_last_mm[ebuf] = s_t.n

        for tt in range(NTG):
            ti = t0 + tt
            bf = 0
            xs = slice(0, D)
            clw = ([(s_v, None), (bd.sem("ffxo0"), None)] if (g > 0 or tt >= 1) else [])
            bd.do("sync", lambda E_, ti=ti, xs=xs, bf=bf: E_.dma_start(out=CMBX[:, xs], in_=x_src[ti * 128:(ti + 1) * 128, :]),
                  inc=(s_cl2[bf], 16), waits=clw)
            bd.do("sync", lambda E_, ti=ti, xs=xs, bf=bf: E_.dma_start(out=CMBS[:, xs], in_=s_dram[ti * 128:(ti + 1) * 128, :]),
                  inc=(s_cl2[bf], 16))
            clr = s_cl2[bf].n
            accs = slice(tt * D, (tt + 1) * D)
            if l2:
                TVc = bd.sb["TV_sb"]
                c0 = bf * 8
                bd.do("vector", lambda E_, accs=accs, ti=ti: E_.tensor_scalar(ACC[:, accs], ACC[:, accs],
                      TVc[:, ti:ti + 1], None, op0=ALU.mult), inc=(s_v, 1), waits=[(s_v, None), (s_g, None)])
                bd.do("vector", lambda E_, accs=accs, c0=c0: E_.tensor_reduce(LST[:, c0:c0 + 1], ACC[:, accs], axis=AX.X, op=ALU.add),
                      inc=(s_v, 1), waits=[(s_v, None)])
                bd.do("scalar", lambda E_, accs=accs, c0=c0: E_.activation(SQS[:, 0:D], ACC[:, accs], AF.Square,
                      accum_out=LST[:, c0 + 1:c0 + 2]), inc=(s_g, 1), waits=[(s_v, None)])
                bd.do("vector", lambda E_, c0=c0: E_.tensor_scalar(LST[:, c0 + 2:c0 + 3], LST[:, c0:c0 + 1], 1.0 / D, None, op0=ALU.mult),
                      inc=(s_v, 1), waits=[(s_v, None)])
                bd.do("vector", lambda E_, c0=c0: E_.tensor_tensor(LST[:, c0 + 3:c0 + 4], LST[:, c0 + 2:c0 + 3], LST[:, c0 + 2:c0 + 3], op=ALU.mult),
                      inc=(s_v, 1), waits=[(s_v, None)])
                bd.do("vector", lambda E_, c0=c0: E_.scalar_tensor_tensor(LST[:, c0 + 4:c0 + 5], LST[:, c0 + 1:c0 + 2], 1.0 / D,
                      LST[:, c0 + 3:c0 + 4], op0=ALU.mult, op1=ALU.subtract), inc=(s_v, 1), waits=[(s_g, None), (s_v, None)])
                bd.do("vector", lambda E_, c0=c0: E_.tensor_scalar(LST[:, c0 + 5:c0 + 6], LST[:, c0 + 4:c0 + 5], 1e-5, None, op0=ALU.add),
                      inc=(s_v, 1), waits=[(s_v, None)])
                bd.do("vector", lambda E_, c0=c0: E_.reciprocal(LST[:, c0 + 6:c0 + 7], LST[:, c0 + 5:c0 + 6]),
                      inc=(s_v, 1), waits=[(s_v, None)])
                bd.do("scalar", lambda E_, c0=c0: E_.activation(LST[:, c0 + 7:c0 + 8], LST[:, c0 + 6:c0 + 7], AF.Sqrt),
                      inc=(s_g, 1), waits=[(s_v, None)])
                bd.do("vector", lambda E_, accs=accs, c0=c0: E_.tensor_scalar(ACC[:, accs], ACC[:, accs], LST[:, c0 + 2:c0 + 3],
                      LST[:, c0 + 7:c0 + 8], op0=ALU.subtract, op1=ALU.mult), inc=(s_v, 1), waits=[(s_g, None)])
                bd.do("vector", lambda E_, accs=accs: E_.tensor_tensor(ACC[:, accs], ACC[:, accs], GMB[:, 0:D], op=ALU.mult),
                      inc=(s_v, 1), waits=[(s_v, None)])
                bd.do("vector", lambda E_, accs=accs: E_.tensor_tensor(ACC[:, accs], ACC[:, accs], GMB[:, D:2 * D], op=ALU.add),
                      inc=(s_v, 1), waits=[(s_v, None)])
            bd.do("scalar", lambda E_, accs=accs: E_.activation(CMBG[:, 0:D], ACC[:, accs], AF.Silu),
                  inc=(s_g, 1), waits=[(s_v, None)])
            bd.do("vector", lambda E_, xs=xs, bf=bf: E_.tensor_tensor(CMBS[:, xs], CMBS[:, xs], CMBG[:, 0:D], op=ALU.mult),
                  inc=(s_v, 1), waits=[(s_g, None), (s_cl2[bf], clr)])
            bd.do("vector", lambda E_, xs=xs: E_.tensor_tensor(CMBX[:, xs], CMBX[:, xs], CMBS[:, xs], op=ALU.add),
                  inc=(s_v, 1), waits=[(s_v, None)])
            s_xo = bd.sem(f"ffxo{bf}")
            bd.do("sync", lambda E_, ti=ti, xs=xs, s_xo=s_xo: E_.dma_start(out=x_dst[ti * 128:(ti + 1) * 128, :], in_=CMBX[:, xs]),
                  inc=(s_xo, 16), waits=[(s_v, None)])
    bd.wait("sync", bd.sem("ffxo0"))
    bd.wait("sync", bd.sem("ffxo1"))
    fctx.close()
    bd.barrier(f"ffn{layer}")



def emit_router(bd, layer, b, cfg, XNT, XNTL, sems):
    E = cfg["E"]
    K = cfg["topk"]
    l2 = cfg.get("l2_mode", False)
    l0 = (layer == 0)
    PC = bd.sb["PC"]
    IDF = bd.sb["IDF"]
    C_sb, CT_sb = bd.sb["C_sb"], bd.sb["CT_sb"]
    RWsb, RBB = bd.sb["RWsb"], bd.sb["RBB"]
    s_t, s_v, s_g, s_x = sems["t"], sems["v"], sems["g"], sems["x"]
    s_w = sems["w"]
    RL = bd.sb["RL"]
    RP = bd.sb["RP"]
    RT8 = bd.sb["RT8"]
    RC = bd.sb["RC"]
    RS = bd.sb["RS"]
    ROH = bd.sb["ROH"]
    RTP = bd.sb["RTP"]
    SSUM = bd.sb["SSUM"]
    xnt_ready = s_x.n
    ct_evac_done = s_v.n
    for tt in range(NTB):
        ti = b * NTB + tt
        bf = tt % 2
        ls = slice(bf * 16, bf * 16 + E)
        c0 = bf * 4
        first = True
        RWLO = bd.sb["RWLO"]
        passes = ((XNT, RWsb), (XNT, RWLO), (XNTL, RWsb))
        for pi, (XX, WW) in enumerate(passes):
            for dc in range(8):
                w_ = [(s_x, xnt_ready), (s_v, None), (s_w, None)] if first else []
                st = (pi == 0 and dc == 0)
                sp = (pi == 2 and dc == 7)
                bd.do("tensor", lambda E_, tt=tt, dc=dc, XX=XX, WW=WW, st=st, sp=sp: E_.matmul(
                      PC[:, 0:E], XX[:, dc * S + tt * 128: dc * S + tt * 128 + 128],
                      WW[:, dc * 16: dc * 16 + E], start=st, stop=sp), inc=(s_t, 1), waits=w_)
                first = False
        lg = s_t.n
        bd.do("vector", lambda E_, ls=ls: E_.tensor_tensor(RL[:, ls], PC[:, 0:E], RBB[:, 0:E], op=ALU.add),
              inc=(s_v, 1), waits=[(s_t, lg)])
        bd.do("vector", lambda E_, ls=ls, c0=c0: E_.tensor_reduce(RS[:, c0:c0 + 1], RL[:, ls], axis=AX.X, op=ALU.max, negate=True),
              inc=(s_v, 1), waits=[(s_v, None)])
        bd.do("scalar", lambda E_, ls=ls, c0=c0: E_.activation(RP[:, ls], RL[:, ls], AF.Exp,
              bias=RS[:, c0:c0 + 1], accum_out=RS[:, c0 + 1:c0 + 2]), inc=(s_g, 1), waits=[(s_v, None)])
        bd.do("vector", lambda E_, c0=c0: E_.reciprocal(RS[:, c0 + 2:c0 + 3], RS[:, c0 + 1:c0 + 2]),
              inc=(s_v, 1), waits=[(s_g, None)])
        bd.do("vector", lambda E_, ls=ls, c0=c0: E_.tensor_scalar(RP[:, ls], RP[:, ls], RS[:, c0 + 2:c0 + 3], None, op0=ALU.mult),
              inc=(s_v, 1), waits=[(s_v, None)])
        t8 = slice(bf * 8, bf * 8 + 8)
        bd.do("vector", lambda E_, ls=ls, t8=t8: E_.max(out=RT8[:, t8], in_=RP[:, ls]),
              inc=(s_v, 1), waits=[(s_v, None)])
        if l2:
            TV = bd.sb["TV_sb"]
            bd.do("vector", lambda E_, t8=t8, c0=c0: E_.tensor_scalar(RS[:, c0 + 3:c0 + 4], RT8[:, t8.start:t8.start + 1], 1e-8, None, op0=ALU.add),
                  inc=(s_v, 1), waits=[(s_v, None)])
            bd.do("vector", lambda E_, c0=c0: E_.reciprocal(RS[:, c0:c0 + 1], RS[:, c0 + 3:c0 + 4]),
                  inc=(s_v, 1), waits=[(s_v, None)])
            bd.do("vector", lambda E_, ti=ti, t8=t8, c0=c0: E_.scalar_tensor_tensor(TV[:, ti:ti + 1],
                  RT8[:, t8.start:t8.start + 1], 1.0, RS[:, c0:c0 + 1], op0=ALU.mult, op1=ALU.mult),
                  inc=(s_v, 1), waits=[(s_v, None)])
            bd.do("vector", lambda E_, ls=ls, t8=t8: E_.tensor_scalar(RC[:, ls], RP[:, ls],
                  RT8[:, t8.start:t8.start + 1], None, op0=ALU.is_ge), inc=(s_v, 1), waits=[(s_v, None)])
        else:
            bd.do("vector", lambda E_, t8=t8, c0=c0: E_.tensor_reduce(RS[:, c0:c0 + 1], RT8[:, t8.start:t8.start + K], axis=AX.X, op=ALU.add),
                  inc=(s_v, 1), waits=[(s_v, None)])
            bd.do("vector", lambda E_, c0=c0: E_.reciprocal(RS[:, c0 + 3:c0 + 4], RS[:, c0:c0 + 1]),
                  inc=(s_v, 1), waits=[(s_v, None)])
            bd.do("vector", lambda E_, ls=ls, t8=t8, c0=c0: E_.tensor_scalar(RC[:, ls], RP[:, ls],
                  RT8[:, t8.start + K - 1:t8.start + K], None, op0=ALU.is_ge), inc=(s_v, 1), waits=[(s_v, None)])
            bd.do("vector", lambda E_, ls=ls, c0=c0: E_.scalar_tensor_tensor(RC[:, ls], RP[:, ls], RS[:, c0 + 3:c0 + 4],
                  RC[:, ls], op0=ALU.mult, op1=ALU.mult), inc=(s_v, 1), waits=[(s_v, None)])
        bd.do("vector", lambda E_, ti=ti, ls=ls: E_.tensor_copy(C_sb[:, ti * E:(ti + 1) * E], RC[:, ls]),
              inc=(s_v, 1), waits=[(s_v, None)])
        cwr = s_v.n
        if l0:
            for k in range(2):
                bd.do("vector", lambda E_, ls=ls, t8=t8, k=k: E_.tensor_scalar(ROH[:, ls], RP[:, ls],
                      RT8[:, t8.start + k:t8.start + k + 1], None, op0=ALU.is_equal),
                      inc=(s_v, 1), waits=[(s_v, None), (s_t, None)])
                bd.do("vector", lambda E_, t8=t8, k=k, c0=c0, bf=bf: E_.scalar_tensor_tensor(RTP[:, bf * 2 + k:bf * 2 + k + 1],
                      RT8[:, t8.start + k:t8.start + k + 1], 1.0, RS[:, c0 + 3:c0 + 4], op0=ALU.mult, op1=ALU.mult),
                      inc=(s_v, 1), waits=[(s_v, None)])
                bd.do("tensor", lambda E_, ls=ls, k=k, bf=bf: E_.matmul(PC[0:8, 24 + k:25 + k], ROH[:, ls],
                      RTP[:, bf * 2 + k:bf * 2 + k + 1], start=True, stop=True), inc=(s_t, 1), waits=[(s_v, None)])
            bd.do("vector", lambda E_: E_.scalar_tensor_tensor(SSUM[:, 0:2], PC[0:8, 24:26], 1.0, SSUM[:, 0:2],
                  op0=ALU.mult, op1=ALU.add), inc=(s_v, 1), waits=[(s_t, None)])
        bd.do("tensor", lambda E_, ls=ls: E_.transpose(PC[0:E, 128:256], RC[:, ls], IDF[:]),
              inc=(s_t, 1), waits=[(s_v, None)])
        bd.do("vector", lambda E_, ti=ti: E_.tensor_copy(CT_sb[0:E, ti * 128:(ti + 1) * 128], PC[0:E, 128:256]),
              inc=(s_v, 1), waits=[(s_t, None)])
        ct_evac_done = s_v.n



def emit_l0_scatter_fix(bd, E, sems):
    nc = bd.nc
    s_t, s_v, s_g, s_st = sems["t"], sems["v"], sems["g"], sems["st"]
    s_cc = bd.sem("l0cc")
    PC = bd.sb["PC"]
    IDF = bd.sb["IDF"]
    SSUM = bd.sb["SSUM"]
    C_sb, CT_sb = bd.sb["C_sb"], bd.sb["CT_sb"]
    C0G = bd.sb["C0G"]
    ADDT = bd.sb["ADDT"]
    ssum_d = bd.dscratch("ssum_d", (8, 2), F32)
    ssum_o = bd.dscratch("ssum_o", (8, 2), F32)
    s_fx = bd.sem("l0fx")
    bd.do("sync", lambda E_: E_.dma_start(out=ssum_d, in_=SSUM[:, 0:2]), inc=(s_fx, 16), waits=[(s_v, None)])
    bd.wait("gpsimd", s_fx)
    import os as _os
    if _os.environ.get("KNOCC"):
        bd.do("gpsimd", lambda E_: E_.dma_start(out=ssum_o, in_=ssum_d), inc=(s_cc, 16))
    else:
        bd.do("gpsimd", lambda E_: E_.collective_compute("AllReduce", ALU.add,
              replica_groups=[list(range(NCORES))], ins=[ssum_d.opt()], outs=[ssum_o.opt()]), inc=(s_cc, 1))
    bd.do("vector", lambda E_: E_.memset(ADDT[:], 0.0), inc=(s_v, 1))
    bd.do("sync", lambda E_: E_.dma_start(out=ADDT[0:8, 0:2], in_=ssum_o), inc=(s_fx, 16),
          waits=[(s_cc, None), (s_v, None)])
    ld = s_fx.n
    bd.do("vector", lambda E_: E_.tensor_tensor(ADDT[:], ADDT[:], C0G[:], op=ALU.mult),
          inc=(s_v, 1), waits=[(s_fx, ld)])
    bd.do("vector", lambda E_: E_.tensor_tensor(C_sb[:, 0:E], C_sb[:, 0:E], ADDT[:, 0:E], op=ALU.add),
          inc=(s_v, 1), waits=[(s_v, None)])
    bd.do("vector", lambda E_: E_.tensor_scalar(C_sb[:, 0:E], C_sb[:, 0:E], 5120.0, None, op0=ALU.min),
          inc=(s_v, 1), waits=[(s_v, None)])
    bd.do("tensor", lambda E_: E_.transpose(PC[0:E, 128:256], C_sb[:, 0:E], IDF[:]), inc=(s_t, 1),
          waits=[(s_v, None)])
    bd.do("vector", lambda E_: E_.tensor_copy(CT_sb[0:E, 0:128], PC[0:E, 128:256]), inc=(s_v, 1),
          waits=[(s_t, None)])



def _attn_sample(bd, layer, b, ffn_cfg, nc, sems, shared):
    NH = 16
    (PA, PB, PC, PD, IDB, IDF, ONESRF, ONESRB, SHIFTC, RWsb, RBB,
     wqkv, WPRJ, ROPEC, ROPES, ETsb, BPRJ, xn_dram, xnlo_dram, s_dram,
     qkt_dram, zdram, pprime, rope, band,
     s_w, s_x, s_t, s_v, s_g, s_st, s_bd, s_qp, s_c, w_ready, x_ready) = shared
    with ExitStack() as bctx:
        XNT = bctx.enter_context(nc.sbuf_tensor(f"at_xnt{layer}{b}", [128, 8 * S], BF16))
        for c in range(8):
            bd.do("sync", lambda E_, b=b, c=c: E_.dma_start(out=XNT[:, c * S:(c + 1) * S],
                  in_=xn_dram[b * S:(b + 1) * S, c * 128:(c + 1) * 128], transpose=True),
                  inc=(s_x, 16), waits=([(s_t, None), (s_v, None), (s_g, None)] if c == 0 else []))
        with ExitStack() as rctx:
            XNTL = rctx.enter_context(nc.sbuf_tensor(f"at_xntl{layer}{b}", [128, 8 * S], BF16))
            for c in range(8):
                bd.do("sync", lambda E_, b=b, c=c: E_.dma_start(out=XNTL[:, c * S:(c + 1) * S],
                      in_=xnlo_dram[b * S:(b + 1) * S, c * 128:(c + 1) * 128], transpose=True),
                      inc=(s_x, 16))
            emit_router(bd, layer, b, ffn_cfg, XNT, XNTL, sems)
            bd.barrier(f"rt{layer}{b}")
        VAUG = bctx.enter_context(nc.sbuf_tensor(f"at_vaug{layer}{b}", [128, NTB * NH * 64], BF16))
        with ExitStack() as pctx:
            WQKV = pctx.enter_context(nc.sbuf_tensor(f"at_wqkv{layer}{b}", [128, 8 * 3 * D], BF16))
            QRS = pctx.enter_context(nc.sbuf_tensor(f"at_qrs{layer}{b}", [128, 2 * 2048], BF16))
            RTMP = pctx.enter_context(nc.sbuf_tensor(f"at_rtmp{layer}{b}", [128, 2048], F32))
            PDS = pctx.enter_context(nc.sbuf_tensor(f"at_pds{layer}{b}", [128, 2 * 1024], BF16))
            bd.do("gpsimd", lambda E_: E_.dma_start(out=WQKV[:].rearrange("p (c n) -> p c n", c=8),
                  in_=wqkv.rearrange("(c p) n -> p c n", p=128)), inc=(s_w, 16),
                  waits=[(s_t, None), (s_v, None), (s_g, None)])
            w_ready2 = s_w.n
            bd.do("vector", lambda E_: E_.memset(VAUG[:], 1.0), inc=(s_v, 1))
            vaug_ones = s_v.n
            for tt in range(NTB):
                tw = [(s_w, w_ready2), (s_c, x_ready), (s_x, None)]
                if tt > 0:
                    tw += [(s_v, None), (s_g, None)]
                first = True
                for nb in range(4):
                    for dc in range(8):
                        bd.do("tensor", lambda E_, tt=tt, nb=nb, dc=dc: E_.matmul(
                              PA[:, nb * 512:(nb + 1) * 512],
                              XNT[:, dc * S + tt * 128: dc * S + tt * 128 + 128],
                              WQKV[:, dc * 3 * D + nb * 512: dc * 3 * D + nb * 512 + 512],
                              start=(dc == 0), stop=(dc == 7)), inc=(s_t, 1), waits=(tw if first else []))
                        first = False
                for nb in range(2):
                    for dc in range(8):
                        bd.do("tensor", lambda E_, tt=tt, nb=nb, dc=dc: E_.matmul(
                              PB[:, nb * 512:(nb + 1) * 512],
                              XNT[:, dc * S + tt * 128: dc * S + tt * 128 + 128],
                              WQKV[:, dc * 3 * D + 2048 + nb * 512: dc * 3 * D + 2048 + nb * 512 + 512],
                              start=(dc == 0), stop=(dc == 7)), inc=(s_t, 1))
                mm_done = s_t.n
                qb = (tt % 2) * 2048
                if rope:
                    rc = tt * 64
                    for half, srcoff in ((0, 0), (1, 1024)):
                        bd.do("vector", lambda E_, srcoff=srcoff, rc=rc: E_.tensor_tensor(
                              RTMP[:, 0:1024].rearrange("p (h x) -> p h x", x=64),
                              PA[:, srcoff:srcoff + 1024].rearrange("p (h x) -> p h x", x=64),
                              ROPEC[:, rc:rc + 64].unsqueeze(1).to_broadcast([128, 16, 64]), op=ALU.mult),
                              inc=(s_v, 1), waits=([(s_t, mm_done), (s_v, None)] if half == 0 else [(s_v, None)]))
                        bd.do("vector", lambda E_, srcoff=srcoff: E_.tensor_copy(
                              RTMP[:, 1024:2048].rearrange("p (h x) -> p h x", x=64)[:, :, 0:32],
                              PA[:, srcoff:srcoff + 1024].rearrange("p (h x) -> p h x", x=64)[:, :, 32:64]),
                              inc=(s_v, 1), waits=[(s_v, None)])
                        bd.do("vector", lambda E_, srcoff=srcoff: E_.tensor_copy(
                              RTMP[:, 1024:2048].rearrange("p (h x) -> p h x", x=64)[:, :, 32:64],
                              PA[:, srcoff:srcoff + 1024].rearrange("p (h x) -> p h x", x=64)[:, :, 0:32]),
                              inc=(s_v, 1), waits=[(s_v, None)])
                        bd.do("vector", lambda E_, rc=rc: E_.tensor_tensor(
                              RTMP[:, 1024:2048].rearrange("p (h x) -> p h x", x=64),
                              RTMP[:, 1024:2048].rearrange("p (h x) -> p h x", x=64),
                              ROPES[:, rc:rc + 64].unsqueeze(1).to_broadcast([128, 16, 64]), op=ALU.mult),
                              inc=(s_v, 1), waits=[(s_v, None)])
                        bd.do("vector", lambda E_, qb=qb, half=half: E_.tensor_tensor(
                              QRS[:, qb + half * 1024: qb + half * 1024 + 1024],
                              RTMP[:, 0:1024], RTMP[:, 1024:2048], op=ALU.add),
                              inc=(s_v, 1), waits=[(s_v, None)])
                else:
                    bd.do("vector", lambda E_, qb=qb: E_.tensor_copy(QRS[:, qb:qb + 2048], PA[:]),
                          inc=(s_v, 1), waits=[(s_t, mm_done), (s_v, None)])
                qk_ready = s_v.n
                bd.do("scalar", lambda E_, tt=tt: E_.copy(
                      VAUG[:, tt * 1024: tt * 1024 + 1024], PB[:]),
                      inc=(s_g, 1), waits=[(s_t, mm_done), (s_v, vaug_ones)])
                for r in range(2):
                    pbf = (tt * 2 + r) % 2
                    s_qs = bd.sem(f"atqs{pbf}")
                    for c in range(8):
                        cg = r * 8 + c
                        tw2 = []
                        if c == 0:
                            tw2 = [(s_v, qk_ready), (s_g, None)]
                        bd.do("tensor", lambda E_, qb=qb, cg=cg, c=c: E_.transpose(
                              PD[:, c * 128:(c + 1) * 128], QRS[:, qb + cg * 128: qb + cg * 128 + 128], IDB[:]),
                              inc=(s_t, 1), waits=tw2)
                    bd.do("scalar", lambda E_, pbf=pbf: E_.copy(PDS[:, pbf * 1024:(pbf + 1) * 1024], PD[:]),
                          inc=(s_g, 1), waits=[(s_t, None), (s_qs, None)])
                    bd.do("sync", lambda E_, tt=tt, r=r, pbf=pbf, s_qs=s_qs: E_.dma_start(
                          out=qkt_dram.rearrange("c p t -> p c t")[:, r * 8:(r + 1) * 8, tt * 128:(tt + 1) * 128],
                          in_=PDS[:, pbf * 1024:(pbf + 1) * 1024].rearrange("p (c m) -> p c m", c=8)),
                          inc=(s_qs, 16), waits=[(s_g, None)])
            bd.wait("sync", bd.sem("atqs0"))
            bd.wait("sync", bd.sem("atqs1"))
        bd.barrier(f"prj{layer}{b}")
        emit_scores_l01(bd, layer, b, ffn_cfg, VAUG, WPRJ, (BPRJ if band else None),
                        (ETsb if band else None), (pprime[b % 2] if band else None),
                        qkt_dram, zdram, s_dram, sems, s_bd, s_qp)


def emit_attn_l01(bd: Build, layer, ffn_cfg):
    nc = bd.nc
    rope = (layer == 0)
    band = (layer == 1)
    NH = 16
    E = ffn_cfg["E"]
    xn_dram = bd.dram["xn_dram"]
    xnlo_dram = bd.dram["xnlo_dram"]
    s_dram = bd.dscratch("s_dram", (T, D), F32)
    qkt_dram = bd.dscratch("qkt_dram", (16, 128, S), BF16)
    zdram = bd.dscratch("z_bounce", (16, S), F32)
    wqkv = bd.din(f"l{layer}_wqkv", (D, 3 * D))
    wproj = bd.din(f"l{layer}_wproj", (D, D))
    rw = bd.din(f"l{layer}_rw", (D, 16))
    rbias = bd.din(f"l{layer}_rb", (1, 16))
    if band:
        et = bd.din("l1_etdup", (128, 2047))
        bproj_d = bd.din("l1_bproj", (1, D))
        pprime = [bd.dscratch(f"pprime{i}", (16, S, 2688), BF16) for i in range(2)]

    PA, PB, PC, PD = bd.sb["PA"], bd.sb["PB"], bd.sb["PC"], bd.sb["PD"]
    IDB, IDF = bd.sb["IDB"], bd.sb["IDF"]
    ONESRF, ONESRB = bd.sb["ONESRF"], bd.sb["ONESRB"]
    SHIFTC = bd.sb["SHIFTC"]
    RWsb, RBB = bd.sb["RWsb"], bd.sb["RBB"]

    s_w = bd.sem("atw")
    s_x = bd.sem("atx")
    s_t = bd.sem("att")
    s_v = bd.sem("atv")
    s_g = bd.sem("atg")
    s_st = bd.sem("atst")
    s_bd = bd.sem("atbd")
    s_qp = bd.sem("atqp")
    sems = {"t": s_t, "v": s_v, "g": s_g, "x": s_x, "st": s_st, "w": s_w}

    with ExitStack() as lctx:
        WPRJ = lctx.enter_context(nc.sbuf_tensor(f"at_wprj{layer}", [128, 8 * D], BF16))
        if rope:
            ROPEC = lctx.enter_context(nc.sbuf_tensor("at_ropec", [128, 16 * 64], F32))
            ROPES = lctx.enter_context(nc.sbuf_tensor("at_ropes", [128, 16 * 64], F32))
        if band:
            ETsb = lctx.enter_context(nc.sbuf_tensor("at_et", [128, 2047], BF16))
            BPRJ = lctx.enter_context(nc.sbuf_tensor("at_bprj", [1, D], BF16))

        bd.do("gpsimd", lambda E_: E_.dma_start(out=WPRJ[:].rearrange("p (c n) -> p c n", c=8),
              in_=wproj.rearrange("(c p) n -> p c n", p=128)), inc=(s_w, 16))
        s_c = bd.sem("atc")
        RWF = bd.sb["RWF"]
        RWLO = bd.sb["RWLO"]
        bd.do("sync", lambda E_: E_.dma_start(out=bd.sb["RBROW"][:], in_=rbias), inc=(s_c, 16))
        if rope:
            bd.do("sync", lambda E_: E_.dma_start(out=ROPEC[:], in_=bd.din("ropec_r", (128, 1024))), inc=(s_c, 16))
            bd.do("sync", lambda E_: E_.dma_start(out=ROPES[:], in_=bd.din("ropes_r", (128, 1024))), inc=(s_c, 16))
        bd.do("sync", lambda E_: E_.dma_start(out=RWF[:].rearrange("p (c n) -> p c n", c=8),
              in_=rw.rearrange("(c p) n -> p c n", p=128)), inc=(s_c, 16))
        bd.do("vector", lambda E_: E_.tensor_copy(RWsb[:], RWF[:]), inc=(s_v, 1), waits=[(s_c, None), (s_t, None), (s_g, None)])
        bd.do("vector", lambda E_: E_.tensor_tensor(RWLO[:], RWF[:], RWsb[:], op=ALU.subtract),
              inc=(s_v, 1), waits=[(s_v, None)])
        bd.do("tensor", lambda E_: E_.matmul(PC[:, 256:272], ONESRF[:], bd.sb["RBROW"][:], start=True, stop=True),
              inc=(s_t, 1), waits=[(s_c, None), (s_v, None)])
        bd.do("vector", lambda E_: E_.tensor_copy(RBB[:], PC[:, 256:272]), inc=(s_v, 1), waits=[(s_t, None)])
        if band:
            bd.do("gpsimd", lambda E_: E_.dma_start(out=ETsb[:], in_=et), inc=(s_w, 16))
            bd.do("gpsimd", lambda E_: E_.dma_start(out=BPRJ[:], in_=bproj_d), inc=(s_w, 16))
        if layer == 0:
            bd.do("vector", lambda E_: E_.memset(bd.sb["SSUM"][:], 0.0), inc=(s_v, 1))
        w_ready, x_ready = s_w.n, s_c.n

        for b in range(BPC):
            shared = (PA, PB, PC, PD, IDB, IDF, ONESRF, ONESRB, SHIFTC, RWsb, RBB,
                      wqkv, WPRJ, (ROPEC if rope else None), (ROPES if rope else None),
                      (ETsb if band else None), (BPRJ if band else None), xn_dram, xnlo_dram, s_dram,
                      qkt_dram, zdram, (pprime if band else None), rope, band,
                      s_w, s_x, s_t, s_v, s_g, s_st, s_bd, s_qp, s_c, w_ready, x_ready)
            _attn_sample(bd, layer, b, ffn_cfg, nc, sems, shared)
            bd.barrier(f"atb{layer}{b}")
        if layer == 0:
            emit_l0_scatter_fix(bd, E, sems)
    bd.barrier(f"attn{layer}")


def emit_scores_l01(bd, layer, b, ffn_cfg, VAUG, WPRJ, BPRJ, ETsb, pp_dram, qkt_dram, zdram, s_dram, sems, s_bd, s_qp):
    nc = bd.nc
    band = (layer == 1)
    NH = 16
    PA, PB, PC, PD = bd.sb["PA"], bd.sb["PB"], bd.sb["PC"], bd.sb["PD"]
    IDB = bd.sb["IDB"]
    ONESRF, ONESRB, ONESCB = bd.sb["ONESRF"], bd.sb["ONESRB"], bd.sb["ONESCB"]
    SHIFTC = bd.sb["SHIFTC"]
    ZSEL = bd.sb["ZSEL"]
    s_t, s_v, s_g, s_x, s_st = sems["t"], sems["v"], sems["g"], sems["x"], sems["st"]
    scale = 1.0 / 8.0

    with ExitStack() as sctx:
        QP = sctx.enter_context(nc.sbuf_tensor(f"sc_qp{layer}{b}", [128, 2 * 2 * S], BF16))
        CTX = sctx.enter_context(nc.sbuf_tensor(f"sc_ctx{layer}{b}", [128, 8 * 512], BF16))
        EXPS = sctx.enter_context(nc.sbuf_tensor(f"sc_exps{layer}{b}", [128, 2 * 512], BF16))
        ZST = sctx.enter_context(nc.sbuf_tensor(f"sc_zst{layer}{b}", [128, 1024], F32))
        ZTQ = sctx.enter_context(nc.sbuf_tensor(f"sc_ztq{layer}{b}", [16, 2 * 512], F32))
        SOUT = sctx.enter_context(nc.sbuf_tensor(f"sc_sout{layer}{b}", [128, 2 * D], F32))
        if band:
            BANDT = sctx.enter_context(nc.sbuf_tensor(f"sc_bandt{b}", [128, 4 * 512], BF16))
            PPS = sctx.enter_context(nc.sbuf_tensor(f"sc_pps{b}", [128, 2 * 512], BF16))
            BCOL = sctx.enter_context(nc.sbuf_tensor(f"sc_bcol{b}", [128, 2], F32))
            ZPAD = sctx.enter_context(nc.sbuf_tensor(f"sc_zpad{b}", [128, 642], BF16))

        qp_last_use = [0, 0]

        def load_pair(hp, buf):
            bd.do("sync", lambda E_, hp=hp, buf=buf: E_.dma_start(
                  out=QP[:, buf * 2 * S: buf * 2 * S + S], in_=qkt_dram[hp]),
                  inc=(s_qp, 16), waits=[(s_t, qp_last_use[buf])])
            bd.do("sync", lambda E_, hp=hp, buf=buf: E_.dma_start(
                  out=QP[:, buf * 2 * S + S: (buf + 1) * 2 * S], in_=qkt_dram[8 + hp]),
                  inc=(s_qp, 16))
            return s_qp.n

        band_ready = 0
        if band:
            bd.do("vector", lambda E_: E_.memset(ZPAD[:], 0.0), inc=(s_v, 1))
            zp = s_v.n
            for hz in range(16):
                for rg in range(16):
                    bd.do("sync", lambda E_, hz=hz, rg=rg: E_.dma_start(
                          out=pp_dram[hz, rg * 128:(rg + 1) * 128, 2046:2688], in_=ZPAD[:]),
                          inc=(s_bd, 16), waits=([(s_v, zp)] if (rg == 0 and hz == 0) else []))
            for hp in range(8):
                buf = hp % 2
                qpr = load_pair(hp, buf)
                for hh in range(2):
                    base = hh * 64
                    for qt in range(NTB):
                        mcs = [mc for mc in range(4) if (mc + 1) * 512 > 1920 - qt * 128]
                        first = True
                        for mc in mcs:
                            n = 512 if mc < 3 else 511
                            w_ = [(s_qp, qpr), (s_v, None)] if first else []
                            bd.do("tensor", lambda E_, base=base, buf=buf, qt=qt, mc=mc, n=n: E_.matmul(
                                  PA[:, mc * 512: mc * 512 + n],
                                  QP[base:base + 64, buf * 2 * S + qt * 128: buf * 2 * S + qt * 128 + 128],
                                  ETsb[base:base + 64, mc * 512: mc * 512 + n],
                                  start=True, stop=True), inc=(s_t, 1), waits=w_)
                            first = False
                        mmd = s_t.n
                        bbf = qt % 2
                        bd.do("vector", lambda E_, bbf=bbf: E_.tensor_copy(BCOL[:, bbf:bbf + 1], PA[:, 2046:2047]),
                              inc=(s_v, 1), waits=[(s_t, mmd)])
                        s_pb = bd.sem(f"atpp{bbf}")
                        for mc in mcs:
                            n = 512 if mc < 3 else 510
                            bd.do("vector", lambda E_, mc=mc, n=n, bbf=bbf: E_.tensor_scalar(
                                  PPS[:, bbf * 512: bbf * 512 + n], PA[:, mc * 512: mc * 512 + n],
                                  BCOL[:, bbf:bbf + 1], None, op0=ALU.subtract), inc=(s_v, 1),
                                  waits=[(s_v, None), (s_pb, None)])
                            bd.do("sync", lambda E_, h=hp * 2 + hh, qt=qt, mc=mc, n=n, bbf=bbf, s_pb=s_pb: E_.dma_start(
                                  out=pp_dram[h, qt * 128:(qt + 1) * 128, mc * 512: mc * 512 + n],
                                  in_=PPS[:, bbf * 512: bbf * 512 + n]), inc=(s_pb, 16), waits=[(s_v, None)])
                    qp_last_use[buf] = s_t.n
            bd.wait("sync", bd.sem("atpp0"))
            bd.wait("sync", bd.sem("atpp1"))
            bd.wait("sync", s_bd)
            band_ready = 0

        bandbuf_last = [0, 0, 0, 0]
        for qg in range(4):
            q0 = qg * 512
            for hp in range(8):
                buf = hp % 2
                qpr = load_pair(hp, buf)
                for hh in range(2):
                    h = hp * 2 + hh
                    base = hh * 64
                    pbreg = hh * 512
                    odd = (hh == 1)
                    kclim = (q0 + 510) // 128 if band else -1
                    exp_after = {}
                    smm_after = {}

                    def emit_smm(kc, h=h, base=base, buf=buf, q0=q0, kclim=kclim, qpr=qpr):
                        ps = 1024 + (kc % 2) * 512
                        w_ = []
                        if kc < 2:
                            w_.append((s_qp, qpr))
                        if kc >= 2:
                            w_.append((s_g, exp_after[(kc - 2) // 2]))
                        hasband = band and kc <= kclim
                        bd.do("tensor", lambda E_, kc=kc, ps=ps: E_.matmul(
                              PA[:, ps:ps + 512],
                              QP[base:base + 64, buf * 2 * S + S + kc * 128: buf * 2 * S + S + kc * 128 + 128],
                              QP[base:base + 64, buf * 2 * S + q0: buf * 2 * S + q0 + 512],
                              start=True, stop=(not hasband)), inc=(s_t, 1), waits=w_)
                        if hasband:
                            bbuf = kc % 4
                            s_bb = bd.sem(f"atbb{bbuf}")
                            bd.do("sync", lambda E_, kc=kc, bbuf=bbuf, s_bb=s_bb: E_.dma_start(
                                  out=BANDT[:, bbuf * 512:(bbuf + 1) * 512],
                                  in_=bass.AP(pp_dram.tensor, h * S * 2688 + q0 * 2687 + kc * 128 + 2047, [[2687, 512], [1, 128]]),
                                  transpose=True), inc=(s_bb, 16),
                                  waits=[(s_t, bandbuf_last[bbuf])])
                            bdr = s_bb.n
                            bd.do("tensor", lambda E_, kc=kc, ps=ps, bbuf=bbuf, s_bb=s_bb: E_.matmul(
                                  PA[:, ps:ps + 512], IDB[:], BANDT[:, bbuf * 512:(bbuf + 1) * 512],
                                  start=False, stop=True), inc=(s_t, 1), waits=[(s_bb, bdr)])
                            bandbuf_last[bbuf] = s_t.n
                        smm_after[kc] = s_t.n

                    def emit_exp_pair(i):
                        bd.do("scalar", lambda E_: E_.activation(
                              EXPS[:, 0:1024], PA[:, 1024:2048], AF.Exp,
                              bias=SHIFTC[:, 0:1], scale=scale), inc=(s_g, 1),
                              waits=[(s_t, None)])
                        exp_after[i] = s_g.n

                    def emit_pv(kc, h=h, odd=odd, pbreg=pbreg):
                        eb = (kc % 2) * 512
                        w_ = [(s_g, exp_after[kc // 2])] if kc % 2 == 0 else []
                        if kc == 0:
                            w_ += [(s_g, None), (s_v, None)]
                        prow = slice(0, 64) if not odd else slice(64, 128)
                        bd.do("tensor", lambda E_, kc=kc, eb=eb, prow=prow: E_.matmul(
                              PB[prow, pbreg:pbreg + 512], VAUG[:, kc * 1024 + h * 64: kc * 1024 + h * 64 + 64],
                              EXPS[:, eb:eb + 512], start=(kc == 0), stop=(kc == 15)), inc=(s_t, 1), waits=w_)
                        bd.do("tensor", lambda E_, kc=kc, eb=eb: E_.matmul(
                              PC[0:1, 0:512], ONESCB[:], EXPS[:, eb:eb + 512],
                              start=(kc == 0), stop=(kc == 15)), inc=(s_t, 1))

                    emit_smm(0)
                    emit_smm(1)
                    for i in range(8):
                        emit_exp_pair(i)
                        if i < 7:
                            emit_smm(2 * i + 2)
                            emit_smm(2 * i + 3)
                        emit_pv(2 * i)
                        emit_pv(2 * i + 1)
                    pv_done = s_t.n
                    s_zw = bd.sem(f"atzw{h % 2}")
                    zslot = (h % 2) * 512
                    prow = slice(0, 64) if not odd else slice(64, 128)
                    bd.do("scalar", lambda E_, zslot=zslot: E_.copy(ZST[0:1, zslot:zslot + 512], PC[0:1, 0:512]),
                          inc=(s_g, 1), waits=[(s_t, pv_done), (s_zw, None)])
                    bd.do("sync", lambda E_, h=h, q0=q0, s_zw=s_zw, zslot=zslot: E_.dma_start(
                          out=zdram[h:h + 1, q0:q0 + 512], in_=ZST[0:1, zslot:zslot + 512]),
                          inc=(s_zw, 16), waits=[(s_g, None)])
                    bd.do("scalar", lambda E_, h=h, pbreg=pbreg, prow=prow: E_.copy(
                          CTX[prow, (h // 2) * 512:(h // 2) * 512 + 512],
                          PB[prow, pbreg:pbreg + 512]), inc=(s_g, 1), waits=[(s_g, None)])
                qp_last_use[buf] = s_t.n
            allctx = s_g.n
            zq = (qg % 2) * 512
            bd.wait("sync", bd.sem("atzw0"))
            bd.wait("sync", bd.sem("atzw1"))
            s_zl = bd.sem("atzl")
            bd.do("sync", lambda E_, qg=qg, zq=zq: E_.dma_start(out=ZTQ[:, zq:zq + 512],
                  in_=zdram[:, qg * 512:(qg + 1) * 512]), inc=(s_zl, 16), waits=[(s_v, None)])
            zl = s_zl.n
            bd.do("vector", lambda E_, zq=zq: E_.reciprocal(ZTQ[:, zq:zq + 512], ZTQ[:, zq:zq + 512]),
                  inc=(s_v, 1), waits=[(s_zl, zl)])
            zi = s_v.n
            for hp in range(8):
                bd.do("tensor", lambda E_, hp=hp, zq=zq: E_.matmul(
                      PC[:, 0:512], ZSEL[0:16, hp * 128:(hp + 1) * 128], ZTQ[0:16, zq:zq + 512],
                      start=True, stop=True), inc=(s_t, 1),
                      waits=[(s_v, zi), (s_g, allctx), (s_v, None)])
                zb = s_t.n
                bd.do("vector", lambda E_, hp=hp: E_.scalar_tensor_tensor(
                      CTX[:, hp * 512:(hp + 1) * 512], CTX[:, hp * 512:(hp + 1) * 512], 1.0,
                      PC[:, 0:512], op0=ALU.mult, op1=ALU.mult), inc=(s_v, 1), waits=[(s_t, zb)])
            ctxn = s_v.n
            for sl in range(4):
                PRJ = PB if sl % 2 == 0 else PA
                first = True
                for c in range(8):
                    for nb in range(2):
                        w_ = []
                        if first:
                            w_ = [(s_v, ctxn)]
                            if sl >= 2:
                                w_.append((s_v, None))
                        bd.do("tensor", lambda E_, sl=sl, c=c, nb=nb, PRJ=PRJ: E_.matmul(
                              PRJ[:, nb * 512:(nb + 1) * 512], CTX[:, c * 512 + sl * 128: c * 512 + sl * 128 + 128],
                              WPRJ[:, c * D + nb * 512: c * D + nb * 512 + 512],
                              start=(c == 0), stop=(c == 7 and not band)), inc=(s_t, 1), waits=w_)
                        first = False
                if band:
                    for nb in range(2):
                        bd.do("tensor", lambda E_, nb=nb, PRJ=PRJ: E_.matmul(
                              PRJ[:, nb * 512:(nb + 1) * 512], ONESRB[:], BPRJ[:, nb * 512:(nb + 1) * 512],
                              start=False, stop=True), inc=(s_t, 1))
                pj = s_t.n
                sbf = sl % 2
                s_so = bd.sem(f"atso{sbf}")
                bd.do("vector", lambda E_, sbf=sbf, PRJ=PRJ: E_.tensor_copy(SOUT[:, sbf * D:(sbf + 1) * D], PRJ[:, 0:D]),
                      inc=(s_v, 1), waits=[(s_t, pj), (s_so, None)])
                bd.do("sync", lambda E_, b=b, qg=qg, sl=sl, sbf=sbf, s_so=s_so: E_.dma_start(
                      out=s_dram[b * S + qg * 512 + sl * 128: b * S + qg * 512 + sl * 128 + 128, :],
                      in_=SOUT[:, sbf * D:(sbf + 1) * D]), inc=(s_so, 16), waits=[(s_v, None)])
        bd.wait("sync", bd.sem("atso0"))
        bd.wait("sync", bd.sem("atso1"))
        bd.barrier(f"sc{layer}{b}")




FFN_CFGS = {
    0: dict(E=8, H=1024, topk=2, w1="l0_w1", b1="l0_b1", w2="l0_w2", b2="l0_b2", G=512),
    1: dict(E=16, H=2048, topk=4, w1="l1_w1", b1="l1_b1", w2="l1_w2", b2="l1_b2", G=512),
    2: dict(E=8, H=2048, topk=1, w1="l2_w1", b1=None, w2="l2_w2", b2=None,
            l2_mode=True, gamma="l2_gamma", beta="l2_beta", G=512),
}


def build_program(layers):
    nc = bass.Bass("TRN2", target_bir_lowering=False)
    ctx = ExitStack()
    with ctx:
        bd = Build(nc, ctx)
        bd.psum("PA", (128, 2048), F32)
        bd.psum("PB", (128, 1024), F32)
        bd.psum("PC", (128, 512), F32)
        bd.psum("PD", (128, 1024), BF16)
        bd.sbuf("IDB", (128, 128), BF16)
        bd.sbuf("IDF", (128, 128), F32)
        bd.sbuf("ONESRF", (1, 128), F32)
        bd.sbuf("ONESRB", (1, 128), BF16)
        bd.sbuf("ONESCB", (128, 1), BF16)
        bd.sbuf("SHIFTC", (128, 1), F32)
        bd.sbuf("RWsb", (128, 8 * 16), BF16)
        bd.sbuf("RWLO", (128, 8 * 16), BF16)
        bd.sbuf("RWF", (128, 8 * 16), F32)
        bd.sbuf("RBB", (128, 16), F32)
        bd.sbuf("RBROW", (1, 16), F32)
        bd.sbuf("ZSEL", (16, 8 * 128), F32)
        bd.sbuf("C_sb", (128, NT * 16), F32)
        bd.sbuf("CT_sb", (16, T), BF16)
        bd.sbuf("TV_sb", (128, NT), F32)
        bd.sbuf("SSUM", (8, 4), F32)
        bd.sbuf("C0G", (128, 8), F32)
        bd.sbuf("ADDT", (128, 8), F32)
        bd.sbuf("RL", (128, 2 * 16), F32)
        bd.sbuf("RP", (128, 2 * 16), F32)
        bd.sbuf("RT8", (128, 2 * 8), F32)
        bd.sbuf("RC", (128, 2 * 16), F32)
        bd.sbuf("RS", (128, 8), F32)
        bd.sbuf("ROH", (128, 2 * 16), F32)
        bd.sbuf("RTP", (128, 4), F32)
        bd.sbuf("BEAT", (1, 384), BF16)
        bd.sbuf("ONESCF", (128, 1), F32)
        bd.sbuf("UTRF", (128, 128), F32)

        bd.dscratch("beat_dram", (1, 128), BF16)
        s_k = bd.sem("konst")
        for nm, src in (("IDB", "identb"), ("IDF", "identf"), ("ONESRF", "onesrf"),
                        ("ONESRB", "onesrb"), ("ONESCB", "onescb"), ("ZSEL", "zsel"),
                        ("C0G", "c0g"), ("ONESCF", "onescf"), ("UTRF", "utrif")):
            shp = tuple(bd.sb[nm].shape)
            dt = bd.sb[nm].dtype
            d = bd.din("c_" + src, shp, dt)
            bd.do("sync", lambda E_, nm=nm, d=d: E_.dma_start(out=bd.sb[nm][:], in_=d), inc=(s_k, 16))
        bd.do("vector", lambda E_: E_.memset(bd.sb["SHIFTC"][:], -12.0), inc=(bd.sem("ffv"), 1))
        for e in ENGINES:
            bd.wait(e, s_k)
            bd.wait(e, bd.sem("ffv"))

        x_in = bd.din("x_in", (T, D), F32)
        x_out = bd.dout("x_out", (T, D), F32)
        xb = [bd.dscratch("xbuf0", (T, D), F32), bd.dscratch("xbuf1", (T, D), F32)]
        bd.dscratch("xnlo_dram", (T, D), BF16)

        import os as _os
        kphase = _os.environ.get("KPHASE", "")
        s_wp = bd.sem("wprep")
        wprep = {}
        for l in layers:
            if l in FFN_CFGS:
                cfg0 = FFN_CFGS[l]
                E0, H0 = cfg0["E"], cfg0["H"]
                w1s = bd.din(cfg0["w1"], (E0, D, H0))
                w2s = bd.din(cfg0["w2"], (E0, H0, D))
                w1b = bd.dscratch(cfg0["w1"] + "_b", (E0, D, H0), BF16)
                w2b = bd.dscratch(cfg0["w2"] + "_b", (E0, H0, D), BF16)
                for e0 in range(E0):
                    bd.do("gpsimd", lambda E_, w1s=w1s, w1b=w1b, e0=e0: E_.dma_start(out=w1b[e0], in_=w1s[e0]),
                          inc=(s_wp, 16))
                    bd.do("gpsimd", lambda E_, w2s=w2s, w2b=w2b, e0=e0: E_.dma_start(out=w2b[e0], in_=w2s[e0]),
                          inc=(s_wp, 16))
                wprep[l] = (w1b, w2b)
        bd.wprep = wprep
        bd.s_wp = s_wp

        cur = x_in
        for i, l in enumerate(layers):
            dst = x_out if i == len(layers) - 1 else xb[i % 2]
            emit_ln(bd, l, cur)
            if kphase == "ln":
                s_fin = bd.sem("fin")
                bd.do("gpsimd", lambda E_: E_.dma_start(out=x_out, in_=bd.dram["xn_dram"]), inc=(s_fin, 16))
                bd.wait("gpsimd", s_fin)
                break
            if l in (0, 1):
                emit_attn_l01(bd, l, FFN_CFGS[l])
                if kphase == "attn":
                    s_fin = bd.sem("fin")
                    bd.do("sync", lambda E_: E_.dma_start(out=x_out, in_=bd.dram["s_dram"]), inc=(s_fin, 16))
                    bd.wait("sync", s_fin)
                    break
                emit_ffn(bd, l, FFN_CFGS[l], cur, dst)
            elif l == 2:
                emit_attn_l2(bd, FFN_CFGS[2])
                emit_ffn(bd, 2, FFN_CFGS[2], cur, dst)
            else:
                emit_attn_l3(bd)
                emit_ffn_l3(bd, cur, dst)
            cur = dst
        bd.wait("sync", bd.sem("ffst"))
        with nc.Block() as block:
            bd.emit(block)
    return nc, bd


def _permute_l0_wqkv(w):
    w = np.array(w)
    perm = np.concatenate([np.arange(0, 64, 2), np.arange(1, 64, 2)])
    out = w.copy()
    for h in range(16):
        out[:, h * 64:(h + 1) * 64] = w[:, h * 64 + perm]
        out[:, 1024 + h * 64:1024 + (h + 1) * 64] = w[:, 1024 + h * 64 + perm]
    return out


def _b1c(b1, H):
    E = b1.shape[0]
    return np.ascontiguousarray(b1.reshape(E, H // 128, 128).transpose(0, 2, 1)).astype(np.float32)


def make_host_inputs(ins):
    sh = {}
    c = CONSTS
    sh["c_identb"] = c["identb"]
    sh["c_identf"] = c["identf"]
    sh["c_onesrf"] = np.ones((1, 128), np.float32)
    sh["c_onesrb"] = np.ones((1, 128), BF)
    sh["c_onescb"] = np.ones((128, 1), BF)
    zsel = np.zeros((16, 8 * 128), np.float32)
    for hp in range(8):
        zsel[2 * hp, hp * 128: hp * 128 + 64] = 1.0
        zsel[2 * hp + 1, hp * 128 + 64: hp * 128 + 128] = 1.0
    sh["c_zsel"] = zsel
    sh["ropec_r"] = np.ascontiguousarray(
        c["ropeC"].reshape(16, 128, 64).transpose(1, 0, 2).reshape(128, 1024))
    sh["ropes_r"] = np.ascontiguousarray(
        c["ropeS"].reshape(16, 128, 64).transpose(1, 0, 2).reshape(128, 1024))
    sh["l0_wqkv"] = _permute_l0_wqkv(ins["L0_attn_w_qkv"])
    sh["l0_wproj"] = ins["L0_attn_w_proj"]
    rw = np.zeros((D, 16), np.float32); rw[:, :8] = ins["L0_ffn_router_w"]
    sh["l0_rw"] = rw
    rb = np.full((1, 16), -1e30, np.float32); rb[0, :8] = ins["L0_ffn_router_b"]
    rb[0, 8:] = 0.0
    sh["l0_rb"] = rb
    sh["l0_w1"] = ins["L0_ffn_w1"]
    sh["l0_b1_c"] = _b1c(ins["L0_ffn_b1"], 1024)
    sh["l0_w2"] = ins["L0_ffn_w2"]
    sh["l0_b2"] = ins["L0_ffn_b2"]
    if "L1_attn_W_qkv" in ins:
        sh["l1_wqkv"] = ins["L1_attn_W_qkv"]
        sh["l1_wproj"] = ins["L1_attn_W_proj"]
        sh["l1_bproj"] = ins["L1_attn_b_proj"].reshape(1, D)
        etd = np.zeros((128, 2047), np.float32)
        etd[0:64] = ins["L1_attn_rel_pos_emb"].T
        etd[64:128] = ins["L1_attn_rel_pos_emb"].T
        sh["l1_etdup"] = etd
        sh["l1_rw"] = ins["L1_ffn_router_w"]
        sh["l1_rb"] = ins["L1_ffn_router_b"].reshape(1, 16)
        sh["l1_w1"] = ins["L1_ffn_w1"]
        sh["l1_b1_c"] = _b1c(ins["L1_ffn_b1"], 2048)
        sh["l1_w2"] = ins["L1_ffn_w2"]
        sh["l1_b2"] = ins["L1_ffn_b2"]
    sh["c_onescf"] = np.ones((128, 1), np.float32)
    for off in (0, 128, 256, 384):
        sh[f"cm{off}"] = CONSTS[f"cm{off}"]
    if "L3_attn_w_qkv" in ins:
        sh["l3_wqkv"] = ins["L3_attn_w_qkv"]
        sh["l3_wo"] = ins["L3_attn_w_o"]
        sh["l3_fw"] = ins["L3_ffn_weight"]
        sh["l3_fb"] = ins["L3_ffn_bias"].reshape(1, D)
    sh["c_utrif"] = CONSTS["utrif"]
    if "L2_attn_Wq" in ins:
        sh["l2_wq"] = ins["L2_attn_Wq"]
        sh["l2_wk"] = ins["L2_attn_Wk"]
        sh["l2_wv"] = ins["L2_attn_Wv"]
        sh["l2_wg"] = ins["L2_attn_Wg"]
        sh["l2_gkw1"] = ins["L2_attn_gk_w1"]
        sh["l2_gkw2"] = ins["L2_attn_gk_w2"]
        sh["l2_gkb2"] = ins["L2_attn_gk_b2"].reshape(1, 512)
        sh["l2_wo"] = ins["L2_attn_Wo"]
        sh["l2_gnw"] = ins["L2_attn_g_norm_w"].reshape(1, 256)
        rw2 = np.zeros((D, 16), np.float32); rw2[:, :8] = np.asarray(ins["L2_ffn_gate_w"]).T
        sh["l2_rw"] = rw2
        sh["l2_rb"] = np.zeros((1, 16), np.float32)
        sh["l2_w1"] = ins["L2_ffn_expert_w1"]
        sh["l2_w2"] = ins["L2_ffn_expert_w2"]
        sh["l2_gamma"] = ins["L2_ffn_ln_gamma"].reshape(1, D)
        sh["l2_beta"] = ins["L2_ffn_ln_beta"].reshape(1, D)
    per_core = []
    x = np.asarray(ins["L0_attn_x"])
    for core in range(NCORES):
        m = {"x_in": np.ascontiguousarray(x[core * BPC:(core + 1) * BPC].reshape(T, D))}
        cg = np.zeros((128, 8), np.float32)
        if core == 0:
            cg[0:8, 0:2] = 1.0
        m["c_c0g"] = cg
        per_core.append(m)
    return sh, per_core


_LAYERS = (0, 1, 2, 3)


def kernel(**inputs):
    import os
    layers = tuple(int(x) for x in os.environ.get("KLAYERS", "0123"))
    nc, bd = build_program(layers)
    sh, per_core = make_host_inputs(inputs)
    needed = set(bd.inputs.keys())
    in_maps = []
    for core in range(NCORES):
        m = {}
        for k in needed:
            if k == "x_out":
                continue
            if k in per_core[core]:
                m[k] = per_core[core][k]
            elif k in sh:
                m[k] = sh[k]
            else:
                raise KeyError(k)
        in_maps.append(m)
    res = run_bass_kernel_spmd(nc, in_maps, core_ids=list(range(NCORES)))
    out = np.concatenate([r["x_out"].reshape(BPC, S, D) for r in res.results], axis=0)
    return out.astype(np.float32)



def emit_attn_l2(bd: Build, ffn_cfg):
    nc = bd.nc
    xn_dram = bd.dram["xn_dram"]
    xnlo_dram = bd.dram["xnlo_dram"]
    s_dram = bd.dscratch("s_dram", (T, D), F32)
    wq = bd.din("l2_wq", (D, 512))
    wk = bd.din("l2_wk", (D, 512))
    wv = bd.din("l2_wv", (D, D))
    wg = bd.din("l2_wg", (D, D))
    gkw1 = bd.din("l2_gkw1", (D, 16))
    gkw2 = bd.din("l2_gkw2", (16, 512))
    gkb2 = bd.din("l2_gkb2", (1, 512))
    wo = bd.din("l2_wo", (D, D))
    gnw = bd.din("l2_gnw", (1, 256))
    rw = bd.din("l2_rw", (D, 16))
    rbias = bd.din("l2_rb", (1, 16))
    qdr = bd.dscratch("l2_q", (S, 512), BF16)
    kdr = bd.dscratch("l2_k", (S, 512), BF16)
    vdr = bd.dscratch("l2_v", (S, D), BF16)
    gdr = bd.dscratch("l2_g", (S, D), BF16)
    adr = bd.dscratch("l2_a", (S, 512), F32)
    otdr = bd.dscratch("l2_ot", (D, S), BF16)

    PA, PB, PC, PD = bd.sb["PA"], bd.sb["PB"], bd.sb["PC"], bd.sb["PD"]
    IDB, IDF = bd.sb["IDB"], bd.sb["IDF"]
    ONESRF, ONESRB, ONESCB = bd.sb["ONESRF"], bd.sb["ONESRB"], bd.sb["ONESCB"]
    ONESCF = bd.sb["ONESCF"]
    UTRF = bd.sb["UTRF"]
    RWsb, RBB = bd.sb["RWsb"], bd.sb["RBB"]
    s_w = bd.sem("atw")
    s_x = bd.sem("atx")
    s_t = bd.sem("att")
    s_v = bd.sem("atv")
    s_g = bd.sem("atg")
    s_st = bd.sem("atst")
    s_c = bd.sem("atc")
    sems = {"t": s_t, "v": s_v, "g": s_g, "x": s_x, "st": s_st, "w": s_w}

    with ExitStack() as lctx:
        RWF, RWLO = bd.sb["RWF"], bd.sb["RWLO"]
        GKW2 = lctx.enter_context(nc.sbuf_tensor("l2_gkw2f", [16, 512], F32))
        GKB2 = lctx.enter_context(nc.sbuf_tensor("l2_gkb2f", [1, 512], F32))
        bd.sb["GKW2F"] = GKW2
        bd.sb["GKB2F"] = GKB2
        GNWB = lctx.enter_context(nc.sbuf_tensor("l2_gnwb", [128, 256], F32))
        GNWR = lctx.enter_context(nc.sbuf_tensor("l2_gnwr", [1, 256], F32))
        WO = lctx.enter_context(nc.sbuf_tensor("l2_wo_s", [128, 8 * D], BF16))
        bd.do("sync", lambda E_: E_.dma_start(out=bd.sb["RBROW"][:], in_=rbias), inc=(s_c, 16))
        bd.do("sync", lambda E_: E_.dma_start(out=RWF[:].rearrange("p (c n) -> p c n", c=8),
              in_=rw.rearrange("(c p) n -> p c n", p=128)), inc=(s_c, 16))
        bd.do("sync", lambda E_: E_.dma_start(out=GNWR[:], in_=gnw), inc=(s_c, 16))
        bd.do("sync", lambda E_: E_.dma_start(out=GKW2[:], in_=gkw2), inc=(s_c, 16))
        bd.do("sync", lambda E_: E_.dma_start(out=GKB2[:], in_=gkb2), inc=(s_c, 16))
        bd.do("gpsimd", lambda E_: E_.dma_start(out=WO[:].rearrange("p (c n) -> p c n", c=8),
              in_=wo.rearrange("(c p) n -> p c n", p=128)), inc=(s_w, 16))
        bd.do("vector", lambda E_: E_.tensor_copy(RWsb[:], RWF[:]), inc=(s_v, 1),
              waits=[(s_c, None), (s_t, None), (s_g, None)])
        bd.do("vector", lambda E_: E_.tensor_tensor(RWLO[:], RWF[:], RWsb[:], op=ALU.subtract),
              inc=(s_v, 1), waits=[(s_v, None)])
        bd.do("tensor", lambda E_: E_.matmul(PC[:, 256:272], ONESRF[:], bd.sb["RBROW"][:], start=True, stop=True),
              inc=(s_t, 1), waits=[(s_c, None), (s_v, None)])
        bd.do("vector", lambda E_: E_.tensor_copy(RBB[:], PC[:, 256:272]), inc=(s_v, 1), waits=[(s_t, None)])
        bd.do("tensor", lambda E_: E_.matmul(PC[:, 272:272 + 256] if False else PB[:, 0:256], ONESRF[:], GNWR[:], start=True, stop=True),
              inc=(s_t, 1), waits=[(s_c, None), (s_v, None)])
        bd.do("vector", lambda E_: E_.tensor_copy(GNWB[:], PB[:, 0:256]), inc=(s_v, 1), waits=[(s_t, None)])

        for b in range(BPC):
            _l2_sample(bd, b, ffn_cfg, nc, sems, (PA, PB, PC, PD, IDB, IDF, ONESRF, ONESRB,
                       ONESCB, ONESCF, UTRF, RWsb, RBB, GKW2, GKB2, GNWB, WO,
                       xn_dram, xnlo_dram, s_dram, wq, wk, wv, wg, gkw1,
                       qdr, kdr, vdr, gdr, adr, otdr))
            bd.barrier(f"l2b{b}")
        del bd.sb["GKW2F"], bd.sb["GKB2F"]
    bd.barrier("attn2")


def _l2_sample(bd, b, ffn_cfg, nc, sems, shared):
    (PA, PB, PC, PD, IDB, IDF, ONESRF, ONESRB, ONESCB, ONESCF, UTRF, RWsb, RBB,
     GKW2, GKB2, GNWB, WO, xn_dram, xnlo_dram, s_dram, wq, wk, wv, wg, gkw1,
     qdr, kdr, vdr, gdr, adr, otdr) = shared
    s_w, s_x, s_t, s_v, s_g, s_st = (sems["w"], sems["x"], sems["t"], sems["v"], sems["g"], sems["st"])
    QSC = 128.0 ** -0.5

    with ExitStack() as bctx:
        XNT = bctx.enter_context(nc.sbuf_tensor(f"l2_xnt{b}", [128, 8 * S], BF16))
        for c in range(8):
            bd.do("sync", lambda E_, b=b, c=c: E_.dma_start(out=XNT[:, c * S:(c + 1) * S],
                  in_=xn_dram[b * S:(b + 1) * S, c * 128:(c + 1) * 128], transpose=True),
                  inc=(s_x, 16), waits=([(s_t, None), (s_v, None), (s_g, None)] if c == 0 else []))
        with ExitStack() as rctx:
            XNTL = rctx.enter_context(nc.sbuf_tensor(f"l2_xntl{b}", [128, 8 * S], BF16))
            for c in range(8):
                bd.do("sync", lambda E_, b=b, c=c: E_.dma_start(out=XNTL[:, c * S:(c + 1) * S],
                      in_=xnlo_dram[b * S:(b + 1) * S, c * 128:(c + 1) * 128], transpose=True),
                      inc=(s_x, 16))
            emit_router(bd, 2, b, ffn_cfg, XNT, XNTL, sems)
            bd.barrier(f"l2rt{b}")
        with ExitStack() as pctx:
            WQK = pctx.enter_context(nc.sbuf_tensor(f"l2_wqk{b}", [128, 8 * 1024], BF16))
            WVG = pctx.enter_context(nc.sbuf_tensor(f"l2_wvg{b}", [128, 8 * 2048], BF16))
            GKW1 = pctx.enter_context(nc.sbuf_tensor(f"l2_gkw1{b}", [128, 8 * 16], BF16))
            EVQ = pctx.enter_context(nc.sbuf_tensor(f"l2_evq{b}", [128, 2 * 1024], BF16))
            EVV = pctx.enter_context(nc.sbuf_tensor(f"l2_evv{b}", [128, 2 * 2048], BF16))
            EVA = pctx.enter_context(nc.sbuf_tensor(f"l2_eva{b}", [128, 2 * 512], F32))
            EVZ = pctx.enter_context(nc.sbuf_tensor(f"l2_evz{b}", [16, 2 * 128], F32))
            ASC = pctx.enter_context(nc.sbuf_tensor(f"l2_asc{b}", [128, 2 * 512], F32))
            s_pw = bd.sem("l2pw")
            bd.do("gpsimd", lambda E_: E_.dma_start(out=WQK[:, 0:8 * 512].rearrange("p (c n) -> p c n", c=8),
                  in_=wq.rearrange("(c p) n -> p c n", p=128)), inc=(s_pw, 16),
                  waits=[(s_t, None), (s_v, None), (s_g, None)])
            bd.do("gpsimd", lambda E_: E_.dma_start(out=WQK[:, 8 * 512:8 * 1024].rearrange("p (c n) -> p c n", c=8),
                  in_=wk.rearrange("(c p) n -> p c n", p=128)), inc=(s_pw, 16))
            bd.do("gpsimd", lambda E_: E_.dma_start(out=WVG[:, 0:8 * 1024].rearrange("p (c n) -> p c n", c=8),
                  in_=wv.rearrange("(c p) n -> p c n", p=128)), inc=(s_pw, 16))
            bd.do("gpsimd", lambda E_: E_.dma_start(out=WVG[:, 8 * 1024:8 * 2048].rearrange("p (c n) -> p c n", c=8),
                  in_=wg.rearrange("(c p) n -> p c n", p=128)), inc=(s_pw, 16))
            bd.do("gpsimd", lambda E_: E_.dma_start(out=GKW1[:].rearrange("p (c n) -> p c n", c=8),
                  in_=gkw1.rearrange("(c p) n -> p c n", p=128)), inc=(s_pw, 16))
            pw = s_pw.n
            s_ps = [bd.sem("l2ps0"), bd.sem("l2ps1")]
            for tt in range(NTB):
                pf = tt % 2
                tw = [(s_pw, pw), (s_x, None)]
                if tt > 0:
                    tw += [(s_v, None), (s_g, None)]
                first = True
                for nb in range(2):
                    for dc in range(8):
                        bd.do("tensor", lambda E_, tt=tt, nb=nb, dc=dc: E_.matmul(
                              PA[:, nb * 512:(nb + 1) * 512], XNT[:, dc * S + tt * 128: dc * S + tt * 128 + 128],
                              WQK[:, dc * 512 + (nb * 8 * 512 if False else 0) + nb * 8 * 512 // 1: 0] if False else
                              WQK[:, nb * 8 * 512 + dc * 512: nb * 8 * 512 + (dc + 1) * 512],
                              start=(dc == 0), stop=(dc == 7)), inc=(s_t, 1), waits=(tw if first else []))
                        first = False
                for nb in range(2):
                    for dc in range(8):
                        bd.do("tensor", lambda E_, tt=tt, nb=nb, dc=dc: E_.matmul(
                              PB[:, nb * 512:(nb + 1) * 512], XNT[:, dc * S + tt * 128: dc * S + tt * 128 + 128],
                              WVG[:, dc * 1024 + nb * 512: dc * 1024 + (nb + 1) * 512],
                              start=(dc == 0), stop=(dc == 7)), inc=(s_t, 1))
                for nb in range(2):
                    for dc in range(8):
                        bd.do("tensor", lambda E_, tt=tt, nb=nb, dc=dc: E_.matmul(
                              PA[:, 1024 + nb * 512:1024 + (nb + 1) * 512],
                              XNT[:, dc * S + tt * 128: dc * S + tt * 128 + 128],
                              WVG[:, 8 * 1024 + dc * 1024 + nb * 512: 8 * 1024 + dc * 1024 + (nb + 1) * 512],
                              start=(dc == 0), stop=(dc == 7)), inc=(s_t, 1))
                for dc in range(8):
                    bd.do("tensor", lambda E_, tt=tt, dc=dc: E_.matmul(
                          PC[:, 0:16], XNT[:, dc * S + tt * 128: dc * S + tt * 128 + 128],
                          GKW1[:, dc * 16:(dc + 1) * 16],
                          start=(dc == 0), stop=(dc == 7)), inc=(s_t, 1))
                mmd = s_t.n
                bd.do("scalar", lambda E_, pf=pf: E_.activation(EVQ[:, pf * 1024: pf * 1024 + 512], PA[:, 0:512],
                      AF.Copy, scale=QSC), inc=(s_g, 1), waits=[(s_t, mmd), (s_ps[pf], None)])
                bd.do("scalar", lambda E_, pf=pf: E_.copy(EVQ[:, pf * 1024 + 512: (pf + 1) * 1024], PA[:, 512:1024]),
                      inc=(s_g, 1))
                bd.do("scalar", lambda E_, pf=pf: E_.copy(EVV[:, pf * 2048: pf * 2048 + 1024], PB[:, 0:1024]),
                      inc=(s_g, 1))
                bd.do("scalar", lambda E_, pf=pf: E_.copy(EVV[:, pf * 2048 + 1024: (pf + 1) * 2048], PA[:, 1024:2048]),
                      inc=(s_g, 1))
                bd.do("tensor", lambda E_: E_.transpose(PC[0:16, 128:256], PC[:, 0:16], IDF[:]) if False else None, inc=None) if False else None
                bd.do("vector", lambda E_, pf=pf: E_.tensor_copy(ASC[:, pf * 512: pf * 512 + 16], PC[:, 0:16]),
                      inc=(s_v, 1), waits=[(s_t, mmd)])
                bd.do("tensor", lambda E_, pf=pf: E_.transpose(PC[0:16, 128:256], ASC[:, pf * 512: pf * 512 + 16], IDF[:]),
                      inc=(s_t, 1), waits=[(s_v, None)])
                bd.do("vector", lambda E_, pf=pf: E_.tensor_copy(EVZ[:, pf * 128:(pf + 1) * 128], PC[0:16, 128:256]),
                      inc=(s_v, 1), waits=[(s_t, None)])
                zt = s_v.n
                bd.do("vector", lambda E_, pf=pf: E_.tensor_copy(EVZ[:, pf * 128:(pf + 1) * 128], EVZ[:, pf * 128:(pf + 1) * 128]),
                      inc=(s_v, 1), waits=[(s_v, zt)]) if False else None
                GKW2F = bd.sb["GKW2F"]
                GKB2F = bd.sb["GKB2F"]
                ONESRFl = bd.sb["ONESRF"]
                bd.do("tensor", lambda E_, pf=pf, GKW2F=GKW2F: E_.matmul(PC[:, 0:512], EVZ[:, pf * 128:(pf + 1) * 128],
                      GKW2F[0:16, :], start=True, stop=False), inc=(s_t, 1), waits=[(s_v, None)])
                bd.do("tensor", lambda E_, ONESRFl=ONESRFl, GKB2F=GKB2F: E_.matmul(PC[:, 0:512], ONESRFl[:], GKB2F[:],
                      start=False, stop=True), inc=(s_t, 1))
                z2d = s_t.n
                bd.do("scalar", lambda E_, pf=pf: E_.activation(ASC[:, pf * 512:(pf + 1) * 512], PC[:, 0:512], AF.Abs),
                      inc=(s_g, 1), waits=[(s_t, z2d)])
                bd.do("scalar", lambda E_, pf=pf: E_.activation(ASC[:, pf * 512:(pf + 1) * 512],
                      ASC[:, pf * 512:(pf + 1) * 512], AF.Exp, scale=-1.0), inc=(s_g, 1), waits=[(s_g, None)])
                bd.do("scalar", lambda E_, pf=pf: E_.activation(ASC[:, pf * 512:(pf + 1) * 512],
                      ASC[:, pf * 512:(pf + 1) * 512], AF.Ln, bias=1.0), inc=(s_g, 1), waits=[(s_g, None)])
                bd.do("vector", lambda E_, pf=pf: E_.tensor_scalar(EVA[:, pf * 512:(pf + 1) * 512], PC[:, 0:512],
                      0.0, None, op0=ALU.min), inc=(s_v, 1), waits=[(s_g, None), (s_ps[pf], None)])
                bd.do("vector", lambda E_, pf=pf: E_.scalar_tensor_tensor(EVA[:, pf * 512:(pf + 1) * 512],
                      EVA[:, pf * 512:(pf + 1) * 512], 1.0, ASC[:, pf * 512:(pf + 1) * 512],
                      op0=ALU.mult, op1=ALU.subtract), inc=(s_v, 1), waits=[(s_v, None)])
                bd.do("vector", lambda E_, pf=pf: E_.tensor_scalar(EVA[:, pf * 512:(pf + 1) * 512],
                      EVA[:, pf * 512:(pf + 1) * 512], 1.0 / 16.0, None, op0=ALU.mult),
                      inc=(s_v, 1), waits=[(s_v, None)])
                evd_g, evd_v = s_g.n, s_v.n
                row = b * S + tt * 128
                for (dst, srcT, wid, ctr) in ((qdr, EVQ, 1024, s_ps[pf]),):
                    pass
                bd.do("sync", lambda E_, row=row, pf=pf: E_.dma_start(out=qdr[row - b * S:row - b * S + 128, :] if False else qdr[(row - b * S) + 0:0, :], in_=EVQ[:, 0:0]), inc=None) if False else None
                r0 = tt * 128
                bd.do("sync", lambda E_, r0=r0, pf=pf, s_ps=s_ps: E_.dma_start(out=qdr[r0:r0 + 128, :],
                      in_=EVQ[:, pf * 1024: pf * 1024 + 512]), inc=(s_ps[pf], 16),
                      waits=[(s_g, evd_g), (s_v, evd_v)])
                bd.do("sync", lambda E_, r0=r0, pf=pf, s_ps=s_ps: E_.dma_start(out=kdr[r0:r0 + 128, :],
                      in_=EVQ[:, pf * 1024 + 512: (pf + 1) * 1024]), inc=(s_ps[pf], 16))
                bd.do("sync", lambda E_, r0=r0, pf=pf, s_ps=s_ps: E_.dma_start(out=vdr[r0:r0 + 128, :],
                      in_=EVV[:, pf * 2048: pf * 2048 + 1024]), inc=(s_ps[pf], 16))
                bd.do("sync", lambda E_, r0=r0, pf=pf, s_ps=s_ps: E_.dma_start(out=gdr[r0:r0 + 128, :],
                      in_=EVV[:, pf * 2048 + 1024: (pf + 1) * 2048]), inc=(s_ps[pf], 16))
                bd.do("sync", lambda E_, r0=r0, pf=pf, s_ps=s_ps: E_.dma_start(out=adr[r0:r0 + 128, :],
                      in_=EVA[:, pf * 512:(pf + 1) * 512]), inc=(s_ps[pf], 16))
            bd.wait("sync", s_ps[0])
            bd.wait("sync", s_ps[1])
        bd.barrier(f"l2p{b}")
        for h in range(4):
            _l2_head(bd, b, h, nc, sems, (PA, PB, PC, PD, IDB, IDF, ONESCF, UTRF, GNWB,
                     qdr, kdr, vdr, gdr, adr, otdr))
        bd.barrier(f"l2s{b}")
        with ExitStack() as wctx:
            OT = wctx.enter_context(nc.sbuf_tensor(f"l2_otl{b}", [128, 8 * S], BF16))
            SOUT = wctx.enter_context(nc.sbuf_tensor(f"l2_sout{b}", [128, 2 * D], F32))
            s_ol = bd.sem("l2ol")
            bd.do("sync", lambda E_, b=b: E_.dma_start(out=OT[:].rearrange("p (c t) -> p c t", c=8),
                  in_=otdr.rearrange("(c p) t -> p c t", p=128)), inc=(s_ol, 16),
                  waits=[(s_t, None), (s_v, None)])
            olr = s_ol.n
            for tt in range(NTB):
                first = True
                for nb in range(2):
                    for dc in range(8):
                        w_ = [(s_ol, olr)] if first else []
                        if first and tt >= 2:
                            w_.append((s_v, None))
                        bd.do("tensor", lambda E_, tt=tt, nb=nb, dc=dc: E_.matmul(
                              PB[:, nb * 512:(nb + 1) * 512] if tt % 2 == 0 else PA[:, nb * 512:(nb + 1) * 512],
                              OT[:, dc * S + tt * 128: dc * S + tt * 128 + 128],
                              WO[:, dc * D + nb * 512: dc * D + nb * 512 + 512],
                              start=(dc == 0), stop=(dc == 7)), inc=(s_t, 1), waits=w_)
                        first = False
                pj = s_t.n
                sbf = tt % 2
                s_so = bd.sem(f"atso{sbf}")
                bd.do("vector", lambda E_, sbf=sbf, tt=tt: E_.tensor_copy(SOUT[:, sbf * D:(sbf + 1) * D],
                      PB[:, 0:D] if tt % 2 == 0 else PA[:, 0:D]), inc=(s_v, 1), waits=[(s_t, pj), (s_so, None)])
                bd.do("sync", lambda E_, b=b, tt=tt, sbf=sbf, s_so=s_so: E_.dma_start(
                      out=s_dram[b * S + tt * 128: b * S + tt * 128 + 128, :],
                      in_=SOUT[:, sbf * D:(sbf + 1) * D]), inc=(s_so, 16), waits=[(s_v, None)])
            bd.wait("sync", bd.sem("atso0"))
            bd.wait("sync", bd.sem("atso1"))


def _l2_head(bd, b, h, nc, sems, shared):
    (PA, PB, PC, PD, IDB, IDF, ONESCF, UTRF, GNWB, qdr, kdr, vdr, gdr, adr, otdr) = shared
    s_t, s_v, s_g, s_st = sems["t"], sems["v"], sems["g"], sems["st"]
    NC_ = 16
    with ExitStack() as hctx:
        QH = hctx.enter_context(nc.sbuf_tensor(f"l2h_q{b}{h}", [128, NC_ * 128], BF16))
        KH = hctx.enter_context(nc.sbuf_tensor(f"l2h_k{b}{h}", [128, NC_ * 128], BF16))
        VH = hctx.enter_context(nc.sbuf_tensor(f"l2h_v{b}{h}", [128, NC_ * 256], BF16))
        AH = hctx.enter_context(nc.sbuf_tensor(f"l2h_a{b}{h}", [128, NC_ * 128], F32))
        GH = hctx.enter_context(nc.sbuf_tensor(f"l2h_g{b}{h}", [128, 2 * 256], BF16))
        SST = hctx.enter_context(nc.sbuf_tensor(f"l2h_s{b}{h}", [128, 256], F32))
        SSB = hctx.enter_context(nc.sbuf_tensor(f"l2h_sb{b}{h}", [128, 256], BF16))
        LAM = hctx.enter_context(nc.sbuf_tensor(f"l2h_lam{b}{h}", [128, 2 * 128], F32))
        LAI = hctx.enter_context(nc.sbuf_tensor(f"l2h_lai{b}{h}", [128, 2 * 128], F32))
        QT = hctx.enter_context(nc.sbuf_tensor(f"l2h_qt{b}{h}", [128, 2 * 128], BF16))
        KT = hctx.enter_context(nc.sbuf_tensor(f"l2h_kt{b}{h}", [128, 2 * 128], BF16))
        KTM = hctx.enter_context(nc.sbuf_tensor(f"l2h_ktm{b}{h}", [128, 2 * 128], BF16))
        AMS = hctx.enter_context(nc.sbuf_tensor(f"l2h_am{b}{h}", [128, 2 * 128], BF16))
        GAM = hctx.enter_context(nc.sbuf_tensor(f"l2h_gam{b}{h}", [128, 4], F32))
        OEV = hctx.enter_context(nc.sbuf_tensor(f"l2h_oev{b}{h}", [128, 2 * 256], F32))
        OST = hctx.enter_context(nc.sbuf_tensor(f"l2h_ost{b}{h}", [128, 2 * 256], BF16))
        s_hl = bd.sem("l2hl")
        s_gl = [bd.sem("l2gl0"), bd.sem("l2gl1")]
        s_os = [bd.sem("l2os0"), bd.sem("l2os1")]
        b0 = b * S
        bd.do("sync", lambda E_: E_.dma_start(out=QH[:].rearrange("p (c n) -> p c n", n=128),
              in_=qdr.rearrange("(c p) n -> p c n", p=128)[:, :, h * 128:(h + 1) * 128]),
              inc=(s_hl, 16), waits=[(s_t, None), (s_v, None), (s_g, None)])
        bd.do("sync", lambda E_: E_.dma_start(out=KH[:].rearrange("p (c n) -> p c n", n=128),
              in_=kdr.rearrange("(c p) n -> p c n", p=128)[:, :, h * 128:(h + 1) * 128]),
              inc=(s_hl, 16))
        bd.do("sync", lambda E_: E_.dma_start(out=VH[:].rearrange("p (c n) -> p c n", n=256),
              in_=vdr.rearrange("(c p) n -> p c n", p=128)[:, :, h * 256:(h + 1) * 256]),
              inc=(s_hl, 16))
        bd.do("sync", lambda E_: E_.dma_start(out=AH[:].rearrange("p (c n) -> p c n", n=128),
              in_=adr.rearrange("(c p) n -> p c n", p=128)[:, :, h * 128:(h + 1) * 128]),
              inc=(s_hl, 16))
        hl = s_hl.n
        bd.do("vector", lambda E_: E_.memset(SST[:], 0.0), inc=(s_v, 1))
        bd.do("vector", lambda E_: E_.memset(SSB[:], 0.0), inc=(s_v, 1))
        sini = s_v.n
        for c in range(NC_):
            pf = c % 2
            qs = slice(c * 128, (c + 1) * 128)
            vs = slice(c * 256, (c + 1) * 256)
            lf = slice(pf * 128, (pf + 1) * 128)
            vf = slice(pf * 256, (pf + 1) * 256)
            bd.do("tensor", lambda E_, qs=qs: E_.matmul(PC[:, 0:128], UTRF[:], AH[:, qs], start=True, stop=True),
                  inc=(s_t, 1), waits=[(s_hl, hl), (s_g, None), (s_v, None)])
            bd.do("tensor", lambda E_, qs=qs: E_.matmul(PC[:, 192:193], AH[:, qs], ONESCF[:], start=True, stop=True),
                  inc=(s_t, 1))
            cumd = s_t.n
            bd.do("scalar", lambda E_, lf=lf: E_.activation(LAM[:, lf], PC[:, 0:128], AF.Exp), inc=(s_g, 1),
                  waits=[(s_t, cumd)])
            bd.do("scalar", lambda E_, lf=lf: E_.activation(LAI[:, lf], PC[:, 0:128], AF.Exp, scale=-1.0), inc=(s_g, 1))
            bd.do("scalar", lambda E_, pf=pf: E_.activation(GAM[:, pf:pf + 1], PC[:, 192:193], AF.Exp), inc=(s_g, 1))
            lamd = s_g.n
            bd.do("vector", lambda E_, qs=qs, lf=lf: E_.tensor_tensor(QT[:, lf], QH[:, qs], LAM[:, lf], op=ALU.mult),
                  inc=(s_v, 1), waits=[(s_g, lamd), (s_t, None)])
            bd.do("vector", lambda E_, qs=qs, lf=lf: E_.tensor_tensor(KT[:, lf], KH[:, qs], LAI[:, lf], op=ALU.mult),
                  inc=(s_v, 1), waits=[(s_v, None)])
            qkd = s_v.n
            bd.do("tensor", lambda E_, lf=lf: E_.transpose(PD[:, 0:128], QT[:, lf], IDB[:]), inc=(s_t, 1),
                  waits=[(s_v, qkd), (s_g, None)])
            bd.do("tensor", lambda E_, lf=lf: E_.transpose(PD[:, 128:256], KT[:, lf], IDB[:]), inc=(s_t, 1))
            tpd = s_t.n
            bd.do("scalar", lambda E_, lf=lf: E_.copy(QT[:, lf], PD[:, 0:128]), inc=(s_g, 1), waits=[(s_t, tpd)])
            bd.do("scalar", lambda E_, lf=lf: E_.copy(KTM[:, lf], PD[:, 128:256]), inc=(s_g, 1))
            tpe = s_g.n
            bd.do("tensor", lambda E_, lf=lf: E_.matmul(PA[:, 0:128], KTM[:, lf], QT[:, lf], start=True, stop=True),
                  inc=(s_t, 1), waits=[(s_g, tpe)])
            atd = s_t.n
            bd.do("vector", lambda E_, lf=lf: E_.tensor_tensor(AMS[:, lf], PA[:, 0:128], UTRF[:], op=ALU.mult),
                  inc=(s_v, 1), waits=[(s_t, atd)])
            amd = s_v.n
            bd.do("tensor", lambda E_, lf=lf, vs=vs: E_.matmul(PB[:, 0:256], AMS[:, lf], VH[:, vs],
                  start=True, stop=False), inc=(s_t, 1), waits=[(s_v, amd), (s_v, sini), (s_g, None)])
            bd.do("tensor", lambda E_, lf=lf: E_.matmul(PB[:, 0:256], QT[:, lf], SSB[:],
                  start=False, stop=True), inc=(s_t, 1))
            od = s_t.n
            bd.do("tensor", lambda E_, lf=lf, vs=vs: E_.matmul(PB[:, 512:768], KT[:, lf], VH[:, vs],
                  start=True, stop=True), inc=(s_t, 1))
            kvd = s_t.n
            bd.do("vector", lambda E_: E_.tensor_tensor(SST[:], SST[:], PB[:, 512:768], op=ALU.add),
                  inc=(s_v, 1), waits=[(s_t, kvd)])
            bd.do("vector", lambda E_, pf=pf: E_.tensor_scalar(SST[:], SST[:], GAM[:, pf:pf + 1], None, op0=ALU.mult),
                  inc=(s_v, 1), waits=[(s_v, None)])
            bd.do("vector", lambda E_: E_.tensor_copy(SSB[:], SST[:]), inc=(s_v, 1), waits=[(s_v, None)])
            bd.do("scalar", lambda E_, vf=vf, pf=pf: E_.activation(OEV[:, vf], PB[:, 0:256], AF.Square,
                  accum_out=GAM[:, 2 + pf:3 + pf]), inc=(s_g, 1), waits=[(s_t, od)])
            msd = s_g.n
            bd.do("vector", lambda E_, pf=pf: E_.tensor_scalar(GAM[:, 2 + pf:3 + pf], GAM[:, 2 + pf:3 + pf],
                  1.0 / 256.0, 1e-5, op0=ALU.mult, op1=ALU.add), inc=(s_v, 1), waits=[(s_g, msd)])
            bd.do("vector", lambda E_, pf=pf: E_.reciprocal(GAM[:, 2 + pf:3 + pf], GAM[:, 2 + pf:3 + pf]),
                  inc=(s_v, 1), waits=[(s_v, None)])
            rsd = s_v.n
            bd.do("scalar", lambda E_, pf=pf: E_.activation(GAM[:, 2 + pf:3 + pf], GAM[:, 2 + pf:3 + pf], AF.Sqrt),
                  inc=(s_g, 1), waits=[(s_v, rsd)])
            rsq = s_g.n
            bd.do("sync", lambda E_, c=c, pf=pf, b0=b0: E_.dma_start(out=GH[:, pf * 256:(pf + 1) * 256],
                  in_=gdr[c * 128:(c + 1) * 128, h * 256:(h + 1) * 256]), inc=(s_gl[pf], 16),
                  waits=[(s_v, None)])
            gld = s_gl[pf].n
            bd.do("scalar", lambda E_, pf=pf, vf=vf: E_.activation(GH[:, pf * 256:(pf + 1) * 256].bitcast(BF16) if False else GH[:, pf * 256:(pf + 1) * 256],
                  GH[:, pf * 256:(pf + 1) * 256], AF.Silu), inc=(s_g, 1), waits=[(s_gl[pf], gld)]) if False else None
            bd.do("scalar", lambda E_, pf=pf: E_.activation(OEV[:, pf * 256:(pf + 1) * 256] if False else OEV[:, vf], OEV[:, vf], AF.Identity), inc=None) if False else None
            bd.do("vector", lambda E_, vf=vf, pf=pf: E_.tensor_scalar(OEV[:, vf], PB[:, 0:256],
                  GAM[:, 2 + pf:3 + pf], None, op0=ALU.mult), inc=(s_v, 1), waits=[(s_g, rsq)])
            bd.do("vector", lambda E_, vf=vf: E_.tensor_tensor(OEV[:, vf], OEV[:, vf], GNWB[:], op=ALU.mult),
                  inc=(s_v, 1), waits=[(s_v, None)])
            sgd = s_v.n
            bd.do("scalar", lambda E_, pf=pf, vf=vf: E_.activation(GH[:, pf * 256:(pf + 1) * 256],
                  GH[:, pf * 256:(pf + 1) * 256], AF.Silu), inc=(s_g, 1), waits=[(s_gl[pf], gld), (s_v, sgd)])
            bd.do("vector", lambda E_, vf=vf, pf=pf: E_.tensor_tensor(OST[:, vf], OEV[:, vf],
                  GH[:, pf * 256:(pf + 1) * 256], op=ALU.mult), inc=(s_v, 1),
                  waits=[(s_g, None), (s_os[pf], None)])
            ofd = s_v.n
            bd.do("tensor", lambda E_, vf=vf: E_.transpose(PD[:, 256:384], OST[:, vf.start:vf.start + 128], IDB[:]),
                  inc=(s_t, 1), waits=[(s_v, ofd), (s_g, None)])
            bd.do("tensor", lambda E_, vf=vf: E_.transpose(PD[:, 384:512], OST[:, vf.start + 128:vf.stop], IDB[:]),
                  inc=(s_t, 1))
            otd = s_t.n
            bd.do("scalar", lambda E_, vf=vf: E_.copy(OST[:, vf], PD[:, 256:512]), inc=(s_g, 1), waits=[(s_t, otd)])
            ote = s_g.n
            bd.do("sync", lambda E_, c=c, vf=vf, pf=pf, s_os=s_os: E_.dma_start(
                  out=otdr[h * 256:(h + 1) * 256, c * 128:(c + 1) * 128].rearrange("(u p) t -> p u t", p=128),
                  in_=OST[:, vf].rearrange("p (u t) -> p u t", u=2)), inc=(s_os[pf], 16),
                  waits=[(s_g, ote), (s_os[pf], None)])
        bd.wait("sync", s_os[0])
        bd.wait("sync", s_os[1])



def emit_attn_l3(bd: Build):
    nc = bd.nc
    xn_dram = bd.dram["xn_dram"]
    s_dram = bd.dscratch("s_dram", (T, D), F32)
    qkt_dram = bd.dscratch("qkt_dram", (16, 128, S), BF16)
    zdram = bd.dscratch("z_bounce", (16, S), F32)
    wqkv = bd.din("l3_wqkv", (D, 1536))
    wo = bd.din("l3_wo", (D, D))
    PA, PB, PC, PD = bd.sb["PA"], bd.sb["PB"], bd.sb["PC"], bd.sb["PD"]
    IDB, IDF = bd.sb["IDB"], bd.sb["IDF"]
    ONESRF, ONESCB = bd.sb["ONESRF"], bd.sb["ONESCB"]
    ZSEL = bd.sb["ZSEL"]
    s_w = bd.sem("atw")
    s_x = bd.sem("atx")
    s_t = bd.sem("att")
    s_v = bd.sem("atv")
    s_g = bd.sem("atg")
    s_st = bd.sem("atst")
    s_c = bd.sem("atc")
    s_qp = bd.sem("atqp")
    sems = {"t": s_t, "v": s_v, "g": s_g, "x": s_x, "st": s_st, "w": s_w}

    with ExitStack() as lctx:
        WPRJ = lctx.enter_context(nc.sbuf_tensor("l3_wo_s", [128, 8 * D], BF16))
        ROPEC = lctx.enter_context(nc.sbuf_tensor("l3_ropec", [128, 16 * 64], F32))
        ROPES = lctx.enter_context(nc.sbuf_tensor("l3_ropes", [128, 16 * 64], F32))
        CMS = lctx.enter_context(nc.sbuf_tensor("l3_cms", [128, 4 * 512], BF16))
        bd.do("gpsimd", lambda E_: E_.dma_start(out=WPRJ[:].rearrange("p (c n) -> p c n", c=8),
              in_=wo.rearrange("(c p) n -> p c n", p=128)), inc=(s_w, 16))
        bd.do("sync", lambda E_: E_.dma_start(out=ROPEC[:], in_=bd.din("ropec_r", (128, 1024))), inc=(s_c, 16))
        bd.do("sync", lambda E_: E_.dma_start(out=ROPES[:], in_=bd.din("ropes_r", (128, 1024))), inc=(s_c, 16))
        for off in range(4):
            bd.do("sync", lambda E_, off=off: E_.dma_start(out=CMS[:, off * 512:(off + 1) * 512],
                  in_=bd.din(f"cm{off * 128}", (128, 512), BF16)), inc=(s_c, 16))
        w_ready, x_ready = s_w.n, s_c.n
        for b in range(BPC):
            _l3_sample(bd, b, nc, sems, (PA, PB, PC, PD, IDB, IDF, ONESRF, ONESCB, ZSEL,
                       WPRJ, ROPEC, ROPES, CMS, xn_dram, s_dram, qkt_dram, zdram, wqkv,
                       s_c, s_qp, w_ready, x_ready))
            bd.barrier(f"l3b{b}")
    bd.barrier("attn3")


def _l3_sample(bd, b, nc, sems, shared):
    (PA, PB, PC, PD, IDB, IDF, ONESRF, ONESCB, ZSEL, WPRJ, ROPEC, ROPES, CMS,
     xn_dram, s_dram, qkt_dram, zdram, wqkv, s_c, s_qp, w_ready, x_ready) = shared
    s_w, s_x, s_t, s_v, s_g, s_st = (sems["w"], sems["x"], sems["t"], sems["v"], sems["g"], sems["st"])
    NH = 16
    with ExitStack() as bctx:
        XNT = bctx.enter_context(nc.sbuf_tensor(f"l3_xnt{b}", [128, 8 * S], BF16))
        for c in range(8):
            bd.do("sync", lambda E_, b=b, c=c: E_.dma_start(out=XNT[:, c * S:(c + 1) * S],
                  in_=xn_dram[b * S:(b + 1) * S, c * 128:(c + 1) * 128], transpose=True),
                  inc=(s_x, 16), waits=([(s_t, None), (s_v, None), (s_g, None)] if c == 0 else []))
        xnt_ready = s_x.n
        VAUG = bctx.enter_context(nc.sbuf_tensor(f"l3_vaug{b}", [128, NTB * 4 * 64], BF16))
        with ExitStack() as pctx:
            WQKV = pctx.enter_context(nc.sbuf_tensor(f"l3_wqkv{b}", [128, 8 * 1536], BF16))
            NRM = pctx.enter_context(nc.sbuf_tensor(f"l3_nrm{b}", [128, 1536], F32))
            NST = pctx.enter_context(nc.sbuf_tensor(f"l3_nst{b}", [128, 64], F32))
            QRS = pctx.enter_context(nc.sbuf_tensor(f"l3_qrs{b}", [128, 2 * 1280], BF16))
            RTMP = pctx.enter_context(nc.sbuf_tensor(f"l3_rtmp{b}", [128, 2 * 1280], F32))
            PDS = pctx.enter_context(nc.sbuf_tensor(f"l3_pds{b}", [128, 2 * 1024], BF16))
            bd.do("gpsimd", lambda E_: E_.dma_start(out=WQKV[:].rearrange("p (c n) -> p c n", c=8),
                  in_=wqkv.rearrange("(c p) n -> p c n", p=128)), inc=(s_w, 16),
                  waits=[(s_t, None), (s_v, None), (s_g, None)])
            w2 = s_w.n
            bd.do("vector", lambda E_: E_.memset(VAUG[:], 1.0), inc=(s_v, 1))
            vaug_ones = s_v.n
            for tt in range(NTB):
                tw = [(s_w, w2), (s_c, x_ready), (s_x, xnt_ready)]
                if tt > 0:
                    tw += [(s_v, None), (s_g, None)]
                first = True
                for nb in range(3):
                    for dc in range(8):
                        bd.do("tensor", lambda E_, tt=tt, nb=nb, dc=dc: E_.matmul(
                              PA[:, nb * 512:(nb + 1) * 512], XNT[:, dc * S + tt * 128: dc * S + tt * 128 + 128],
                              WQKV[:, dc * 1536 + nb * 512: dc * 1536 + (nb + 1) * 512],
                              start=(dc == 0), stop=(dc == 7)), inc=(s_t, 1), waits=(tw if first else []))
                        first = False
                mm_done = s_t.n
                pf = tt % 2
                bd.do("scalar", lambda E_: E_.activation(NRM[:, 0:1024], PA[:, 0:1024], AF.Square),
                      inc=(s_g, 1), waits=[(s_t, mm_done), (s_v, None)])
                bd.do("scalar", lambda E_: E_.activation(NRM[:, 1024:1280], PA[:, 1024:1280], AF.Square),
                      inc=(s_g, 1))
                sqd = s_g.n
                bd.do("vector", lambda E_: E_.tensor_reduce(NST[:, 0:20], NRM[:, 0:1280].rearrange("p (h x) -> p h x", x=64),
                      axis=AX.X, op=ALU.add), inc=(s_v, 1), waits=[(s_g, sqd)])
                bd.do("scalar", lambda E_: E_.activation(NST[:, 32:52], NST[:, 0:20], AF.Sqrt),
                      inc=(s_g, 1), waits=[(s_v, None)])
                bd.do("vector", lambda E_: E_.tensor_scalar(NST[:, 32:52], NST[:, 32:52], 1e-10, None, op0=ALU.add),
                      inc=(s_v, 1), waits=[(s_g, None)])
                bd.do("vector", lambda E_: E_.reciprocal(NST[:, 0:20], NST[:, 32:52]),
                      inc=(s_v, 1), waits=[(s_v, None)])
                rcd = s_v.n
                bd.do("vector", lambda E_: E_.tensor_tensor(RTMP[:, 0:1280].rearrange("p (h x) -> p h x", x=64),
                      PA[:, 0:1280].rearrange("p (h x) -> p h x", x=64),
                      NST[:, 0:20].unsqueeze(2).to_broadcast([128, 20, 64]), op=ALU.mult),
                      inc=(s_v, 1), waits=[(s_v, rcd)])
                qnd = s_v.n
                rc = tt * 64
                bd.do("vector", lambda E_, rc=rc: E_.tensor_tensor(
                      RTMP[:, 1280:2560].rearrange("p (h x) -> p h x", x=64).__getitem__((slice(None), slice(0, 20), slice(0, 32))) if False else
                      RTMP[:, 1280:1280 + 1280].rearrange("p (h x) -> p h x", x=64)[:, :, 0:32],
                      RTMP[:, 0:1280].rearrange("p (h x) -> p h x", x=64)[:, :, 32:64],
                      ROPES[:, rc:rc + 64].unsqueeze(1).to_broadcast([128, 20, 64])[:, :, 0:32], op=ALU.mult),
                      inc=(s_v, 1), waits=[(s_v, qnd)])
                bd.do("vector", lambda E_, rc=rc: E_.tensor_tensor(
                      RTMP[:, 1280:2560].rearrange("p (h x) -> p h x", x=64)[:, :, 32:64],
                      RTMP[:, 0:1280].rearrange("p (h x) -> p h x", x=64)[:, :, 0:32],
                      ROPES[:, rc:rc + 64].unsqueeze(1).to_broadcast([128, 20, 64])[:, :, 32:64], op=ALU.mult),
                      inc=(s_v, 1), waits=[(s_v, None)])
                bd.do("vector", lambda E_, rc=rc: E_.tensor_tensor(
                      RTMP[:, 0:1280].rearrange("p (h x) -> p h x", x=64),
                      RTMP[:, 0:1280].rearrange("p (h x) -> p h x", x=64),
                      ROPEC[:, rc:rc + 64].unsqueeze(1).to_broadcast([128, 20, 64]), op=ALU.mult),
                      inc=(s_v, 1), waits=[(s_v, None)])
                pf1 = (tt % 2) * 1280
                bd.do("vector", lambda E_, pf1=pf1: E_.tensor_tensor(QRS[:, pf1:pf1 + 1280],
                      RTMP[:, 0:1280], RTMP[:, 1280:2560], op=ALU.add),
                      inc=(s_v, 1), waits=[(s_v, None)])
                qk_ready = s_v.n
                bd.do("scalar", lambda E_, tt=tt: E_.copy(
                      VAUG[:, tt * 256: tt * 256 + 256], PA[:, 1280:1536]),
                      inc=(s_g, 1), waits=[(s_t, mm_done), (s_v, vaug_ones)])
                for r in range(2):
                    pbf = (tt * 2 + r) % 2
                    s_qs = bd.sem(f"atqs{pbf}")
                    nch = 8 if r == 0 else 2
                    for cdx in range(nch):
                        cg = r * 8 + cdx
                        tw2 = []
                        if cdx == 0:
                            tw2 = [(s_v, qk_ready), (s_g, None)]
                        bd.do("tensor", lambda E_, pf1=pf1, cg=cg, cdx=cdx: E_.transpose(
                              PD[:, cdx * 128:(cdx + 1) * 128], QRS[:, pf1 + cg * 128: pf1 + cg * 128 + 128], IDB[:]),
                              inc=(s_t, 1), waits=tw2)
                    bd.do("scalar", lambda E_, pbf=pbf, nch=nch: E_.copy(PDS[:, pbf * 1024: pbf * 1024 + nch * 128], PD[:, 0:nch * 128]),
                          inc=(s_g, 1), waits=[(s_t, None), (s_qs, None)])
                    bd.do("sync", lambda E_, tt=tt, r=r, pbf=pbf, nch=nch, s_qs=s_qs: E_.dma_start(
                          out=qkt_dram.rearrange("c p t -> p c t")[:, r * 8:r * 8 + nch, tt * 128:(tt + 1) * 128],
                          in_=PDS[:, pbf * 1024: pbf * 1024 + nch * 128].rearrange("p (c m) -> p c m", m=128)),
                          inc=(s_qs, 16), waits=[(s_g, None)])
                    if r == 1:
                        for g2 in range(4):
                            srcc = 8 + g2 // 2
                            srow = (g2 % 2) * 64
                            for half in range(2):
                                bd.do("sync", lambda E_, tt=tt, g2=g2, srcc=srcc, srow=srow, half=half, pbf=pbf, s_qs=s_qs: E_.dma_start(
                                      out=qkt_dram[10 + g2, half * 64:(half + 1) * 64, tt * 128:(tt + 1) * 128],
                                      in_=PDS[:, pbf * 1024 + (srcc - 8) * 128: pbf * 1024 + (srcc - 8) * 128 + 128][srow:srow + 64, :]),
                                      inc=(s_qs, 16), waits=[])
            bd.wait("sync", bd.sem("atqs0"))
            bd.wait("sync", bd.sem("atqs1"))
        bd.barrier(f"l3p{b}")
        _l3_scores(bd, b, nc, sems, (PA, PB, PC, PD, IDB, IDF, ONESRF, ONESCB, ZSEL,
                   WPRJ, CMS, VAUG, s_dram, qkt_dram, zdram, s_qp))


def _l3_scores(bd, b, nc, sems, shared):
    (PA, PB, PC, PD, IDB, IDF, ONESRF, ONESCB, ZSEL, WPRJ, CMS, VAUG,
     s_dram, qkt_dram, zdram, s_qp) = shared
    s_w, s_x, s_t, s_v, s_g, s_st = (sems["w"], sems["x"], sems["t"], sems["v"], sems["g"], sems["st"])
    scale = 1.0 / 8.0
    with ExitStack() as sctx:
        QP = sctx.enter_context(nc.sbuf_tensor(f"l3s_qp{b}", [128, 2 * 2 * S], BF16))
        KD = sctx.enter_context(nc.sbuf_tensor(f"l3s_kd{b}", [128, 4 * S], BF16))
        CTX = sctx.enter_context(nc.sbuf_tensor(f"l3s_ctx{b}", [128, 8 * 512], BF16))
        EXPS = sctx.enter_context(nc.sbuf_tensor(f"l3s_exps{b}", [128, 2 * 512], BF16))
        ZST = sctx.enter_context(nc.sbuf_tensor(f"l3s_zst{b}", [128, 1024], F32))
        ZTQ = sctx.enter_context(nc.sbuf_tensor(f"l3s_ztq{b}", [16, 2 * 512], F32))
        SOUT = sctx.enter_context(nc.sbuf_tensor(f"l3s_sout{b}", [128, 2 * D], F32))
        s_kl = bd.sem("l3kl")
        for g2 in range(4):
            bd.do("sync", lambda E_, g2=g2: E_.dma_start(out=KD[:, g2 * S:(g2 + 1) * S], in_=qkt_dram[10 + g2]),
                  inc=(s_kl, 16), waits=([(s_t, None), (s_v, None), (s_g, None)] if g2 == 0 else []))
        klr = s_kl.n
        qp_last_use = [0, 0]

        def load_qpair(hp, buf):
            s_q2 = bd.sem(f"l3qp{buf}")
            bd.do("sync", lambda E_, hp=hp, buf=buf, s_q2=s_q2: E_.dma_start(
                  out=QP[:, buf * 2 * S: buf * 2 * S + S], in_=qkt_dram[hp]),
                  inc=(s_q2, 16), waits=[(s_t, qp_last_use[buf])])
            return s_q2, s_q2.n

        for qg in range(4):
            q0 = qg * 512
            kcmax = (q0 + 511) // 128
            for hp in range(8):
                buf = hp % 2
                s_q2, qpr = load_qpair(hp, buf)
                for hh in range(2):
                    h = hp * 2 + hh
                    g2 = h // 4
                    base = hh * 64
                    pbreg = hh * 512
                    odd = (hh == 1)
                    exp_after = {}
                    smm_after = {}

                    def emit_smm(kc, h=h, base=base, buf=buf, q0=q0, g2=g2, s_q2=s_q2, qpr=qpr):
                        ps = 1024 + (kc % 2) * 512
                        w_ = [(s_q2, qpr), (s_kl, klr)]
                        if kc >= 2:
                            w_.append((s_g, exp_after[kc - 2][0]))
                            w_.append((s_v, exp_after[kc - 2][1]))
                        bd.do("tensor", lambda E_, kc=kc, ps=ps: E_.matmul(
                              PA[:, ps:ps + 512],
                              KD[base:base + 64, g2 * S + kc * 128: g2 * S + kc * 128 + 128],
                              QP[base:base + 64, buf * 2 * S + q0: buf * 2 * S + q0 + 512],
                              start=True, stop=True), inc=(s_t, 1), waits=w_)
                        smm_after[kc] = s_t.n

                    def emit_exp(kc, q0=q0):
                        ps = 1024 + (kc % 2) * 512
                        eb = (kc % 2) * 512
                        off = kc * 128 - q0
                        bd.do("scalar", lambda E_, ps=ps, eb=eb: E_.activation(
                              EXPS[:, eb:eb + 512], PA[:, ps:ps + 512], AF.Exp, scale=scale),
                              inc=(s_g, 1), waits=[(s_t, smm_after[kc])])
                        if off >= 0:
                            moff = off // 128
                            bd.do("vector", lambda E_, eb=eb, moff=moff: E_.tensor_tensor(
                                  EXPS[:, eb:eb + 512], EXPS[:, eb:eb + 512],
                                  CMS[:, moff * 512:(moff + 1) * 512], op=ALU.mult),
                                  inc=(s_v, 1), waits=[(s_g, None)])
                            exp_after[kc] = (s_g.n, s_v.n)
                        else:
                            exp_after[kc] = (s_g.n, s_v.n)

                    def emit_pv(kc, h=h, odd=odd, pbreg=pbreg, q0=q0):
                        eb = (kc % 2) * 512
                        w_ = [(s_g, exp_after[kc][0]), (s_v, exp_after[kc][1])]
                        if kc == 0:
                            w_ += [(s_g, None), (s_v, None)]
                        lst = (kc == kcmax)
                        g2 = h // 4
                        prow = slice(0, 64) if not odd else slice(64, 128)
                        bd.do("tensor", lambda E_, kc=kc, eb=eb, lst=lst, prow=prow: E_.matmul(
                              PB[prow, pbreg:pbreg + 512],
                              VAUG[:, kc * 256 + g2 * 64: kc * 256 + g2 * 64 + 64],
                              EXPS[:, eb:eb + 512], start=(kc == 0), stop=lst), inc=(s_t, 1), waits=w_)
                        bd.do("tensor", lambda E_, kc=kc, eb=eb, lst=lst: E_.matmul(
                              PC[0:1, 0:512], ONESCB[:], EXPS[:, eb:eb + 512],
                              start=(kc == 0), stop=lst), inc=(s_t, 1))

                    def exp_guard(kc):
                        return exp_after[kc][0]

                    emit_smm(0)
                    for kc in range(kcmax + 1):
                        if kc < kcmax:
                            pass
                        if kc < kcmax:
                            emit_smm(kc + 1)
                        emit_exp(kc)
                        emit_pv(kc)
                    pv_done = s_t.n
                    s_zw = bd.sem(f"atzw{h % 2}")
                    zslot = (h % 2) * 512
                    prow = slice(0, 64) if not odd else slice(64, 128)
                    bd.do("scalar", lambda E_, zslot=zslot: E_.copy(ZST[0:1, zslot:zslot + 512], PC[0:1, 0:512]),
                          inc=(s_g, 1), waits=[(s_t, pv_done), (s_zw, None)])
                    bd.do("sync", lambda E_, h=h, q0=q0, s_zw=s_zw, zslot=zslot: E_.dma_start(
                          out=zdram[h:h + 1, q0:q0 + 512], in_=ZST[0:1, zslot:zslot + 512]),
                          inc=(s_zw, 16), waits=[(s_g, None)])
                    bd.do("scalar", lambda E_, h=h, pbreg=pbreg, prow=prow: E_.copy(
                          CTX[prow, (h // 2) * 512:(h // 2) * 512 + 512],
                          PB[prow, pbreg:pbreg + 512]), inc=(s_g, 1), waits=[(s_g, None)])
                qp_last_use[buf] = s_t.n
            allctx = s_g.n
            zq = (qg % 2) * 512
            bd.wait("sync", bd.sem("atzw0"))
            bd.wait("sync", bd.sem("atzw1"))
            s_zl = bd.sem("atzl")
            bd.do("sync", lambda E_, qg=qg, zq=zq: E_.dma_start(out=ZTQ[:, zq:zq + 512],
                  in_=zdram[:, qg * 512:(qg + 1) * 512]), inc=(s_zl, 16), waits=[(s_v, None)])
            zl = s_zl.n
            bd.do("vector", lambda E_, zq=zq: E_.reciprocal(ZTQ[:, zq:zq + 512], ZTQ[:, zq:zq + 512]),
                  inc=(s_v, 1), waits=[(s_zl, zl)])
            zi = s_v.n
            for hp in range(8):
                bd.do("tensor", lambda E_, hp=hp, zq=zq: E_.matmul(
                      PC[:, 0:512], ZSEL[0:16, hp * 128:(hp + 1) * 128], ZTQ[0:16, zq:zq + 512],
                      start=True, stop=True), inc=(s_t, 1),
                      waits=[(s_v, zi), (s_g, allctx), (s_v, None)])
                zb = s_t.n
                bd.do("vector", lambda E_, hp=hp: E_.scalar_tensor_tensor(
                      CTX[:, hp * 512:(hp + 1) * 512], CTX[:, hp * 512:(hp + 1) * 512], 1.0,
                      PC[:, 0:512], op0=ALU.mult, op1=ALU.mult), inc=(s_v, 1), waits=[(s_t, zb)])
            ctxn = s_v.n
            for sl in range(4):
                PRJ = PB if sl % 2 == 0 else PA
                first = True
                for c in range(8):
                    for nb in range(2):
                        w_ = []
                        if first:
                            w_ = [(s_v, ctxn)]
                            if sl >= 2:
                                w_.append((s_v, None))
                        bd.do("tensor", lambda E_, sl=sl, c=c, nb=nb, PRJ=PRJ: E_.matmul(
                              PRJ[:, nb * 512:(nb + 1) * 512], CTX[:, c * 512 + sl * 128: c * 512 + sl * 128 + 128],
                              WPRJ[:, c * D + nb * 512: c * D + nb * 512 + 512],
                              start=(c == 0), stop=(c == 7)), inc=(s_t, 1), waits=w_)
                        first = False
                pj = s_t.n
                sbf = sl % 2
                s_so = bd.sem(f"atso{sbf}")
                bd.do("vector", lambda E_, sbf=sbf, PRJ=PRJ: E_.tensor_copy(SOUT[:, sbf * D:(sbf + 1) * D], PRJ[:, 0:D]),
                      inc=(s_v, 1), waits=[(s_t, pj), (s_so, None)])
                bd.do("sync", lambda E_, b=b, qg=qg, sl=sl, sbf=sbf, s_so=s_so: E_.dma_start(
                      out=s_dram[b * S + qg * 512 + sl * 128: b * S + qg * 512 + sl * 128 + 128, :],
                      in_=SOUT[:, sbf * D:(sbf + 1) * D]), inc=(s_so, 16), waits=[(s_v, None)])
        bd.wait("sync", bd.sem("atso0"))
        bd.wait("sync", bd.sem("atso1"))
        bd.barrier(f"l3sc{b}")



def emit_ffn_l3(bd: Build, x_src, x_dst):
    nc = bd.nc
    xdt_dram = bd.dram["xdt_dram"]
    s_dram = bd.dram["s_dram"]
    w_d = bd.din("l3_fw", (D, D))
    bias_d = bd.din("l3_fb", (1, D))
    PA, PB, PC, PD = bd.sb["PA"], bd.sb["PB"], bd.sb["PC"], bd.sb["PD"]
    IDF = bd.sb["IDF"]
    ONESRF = bd.sb["ONESRF"]
    s_w = bd.sem("ffw")
    s_t = bd.sem("fft")
    s_v = bd.sem("ffv")
    s_g = bd.sem("ffg")
    s_cl2 = [bd.sem("ffcla"), bd.sem("ffclb")]
    s_ot = bd.sem("ffot")
    s_xl = bd.sem("ffxl")
    with ExitStack() as fctx:
        WDT = fctx.enter_context(nc.sbuf_tensor("l3f_wdt", [128, 8 * D], F32))
        BROW = fctx.enter_context(nc.sbuf_tensor("l3f_brow", [1, D], F32))
        CMBX = fctx.enter_context(nc.sbuf_tensor("l3f_cx", [128, D], F32))
        CMBS = fctx.enter_context(nc.sbuf_tensor("l3f_cs", [128, D], F32))
        CMBG = fctx.enter_context(nc.sbuf_tensor("l3f_cg", [128, D], F32))
        XDT = fctx.enter_context(nc.sbuf_tensor("l3f_xdt", [128, 2 * D], F32))
        with ExitStack() as qctx:
            WF = qctx.enter_context(nc.sbuf_tensor("l3f_wf", [128, 8 * D], F32))
            QS = qctx.enter_context(nc.sbuf_tensor("l3f_qs", [128, 2 * D], F32))
            QI = qctx.enter_context(nc.sbuf_tensor("l3f_qi", [128, D], I32))
            QM = qctx.enter_context(nc.sbuf_tensor("l3f_qm", [128, 3 * 32], F32))
            QSG = qctx.enter_context(nc.sbuf_tensor("l3f_qsg", [128, D], F32))
            bd.do("sync", lambda E_: E_.dma_start(out=WF[:].rearrange("p (c n) -> p c n", c=8),
                  in_=w_d.rearrange("(c p) n -> p c n", p=128)), inc=(s_ot, 16))
            bd.do("sync", lambda E_: E_.dma_start(out=BROW[:], in_=bias_d), inc=(s_ot, 16))
            wfr = s_ot.n
            for ch in range(8):
                ws = slice(ch * D, (ch + 1) * D)
                _quant_chain(bd, WF, ws, QS, QI, QM, QSG, s_v, s_g,
                             first_waits=[(s_ot, wfr)])
            s_qt = bd.sem("l3qt")
            for cc in range(8):
                for rr in range(8):
                    gslot = rr % 4
                    tw = []
                    if gslot == 0:
                        tw = [(s_v, None), (s_g, None)]
                    bd.do("tensor", lambda E_, cc=cc, rr=rr, gslot=gslot: E_.transpose(
                          PC[:, gslot * 128:(gslot + 1) * 128],
                          WF[:, rr * D + cc * 128: rr * D + cc * 128 + 128], IDF[:]),
                          inc=(s_t, 1), waits=tw)
                    if gslot == 3 or rr == 7:
                        pass
                    if gslot == 3:
                        bd.do("vector", lambda E_, cc=cc, rr=rr: E_.tensor_copy(
                              WDT[:, cc * D + (rr - 3) * 128: cc * D + (rr + 1) * 128], PC[:, 0:512]),
                              inc=(s_v, 1), waits=[(s_t, None)])
            bd.wait("tensor", s_v)
        for ti in range(NT):
            bf = ti % 2
            xls = slice(bf * D, (bf + 1) * D)
            lw = [(s_t, None)] if ti >= 2 else []
            bd.do("sync", lambda E_, ti=ti, xls=xls: E_.dma_start(
                  out=XDT[:, xls].rearrange("p (c m) -> p c m", c=8),
                  in_=xdt_dram.rearrange("(c p) t -> p c t", p=128)[:, :, ti * 128:(ti + 1) * 128]),
                  inc=(s_xl, 16), waits=lw)
            xlr = s_xl.n
            first = True
            for nb in range(2):
                for dc in range(8):
                    w_ = []
                    if first:
                        w_ = [(s_xl, xlr), (s_v, None)]
                    bd.do("tensor", lambda E_, nb=nb, dc=dc, xls=xls: E_.matmul(
                          PB[:, nb * 512:(nb + 1) * 512], XDT[:, xls.start + dc * 128: xls.start + (dc + 1) * 128],
                          WDT[:, dc * D + nb * 512: dc * D + nb * 512 + 512],
                          start=(dc == 0), stop=False), inc=(s_t, 1), waits=w_)
                    first = False
                bd.do("tensor", lambda E_, nb=nb: E_.matmul(PB[:, nb * 512:(nb + 1) * 512], ONESRF[:],
                      BROW[:, nb * 512:(nb + 1) * 512], start=False, stop=True), inc=(s_t, 1))
            mmd = s_t.n
            clw = [(s_v, None), (bd.sem("ffxo0"), None)] if ti >= 1 else []
            bd.do("sync", lambda E_, ti=ti: E_.dma_start(out=CMBX[:], in_=x_src[ti * 128:(ti + 1) * 128, :]),
                  inc=(s_cl2[0], 16), waits=clw)
            bd.do("sync", lambda E_, ti=ti: E_.dma_start(out=CMBS[:], in_=s_dram[ti * 128:(ti + 1) * 128, :]),
                  inc=(s_cl2[0], 16))
            clr = s_cl2[0].n
            bd.do("scalar", lambda E_: E_.activation(CMBG[:], PB[:, 0:D], AF.Silu),
                  inc=(s_g, 1), waits=[(s_t, mmd), (s_v, None)])
            bd.do("vector", lambda E_: E_.tensor_tensor(CMBS[:], CMBS[:], CMBG[:], op=ALU.mult),
                  inc=(s_v, 1), waits=[(s_g, None), (s_cl2[0], clr)])
            bd.do("vector", lambda E_: E_.tensor_tensor(CMBX[:], CMBX[:], CMBS[:], op=ALU.add),
                  inc=(s_v, 1), waits=[(s_v, None)])
            s_xo = bd.sem("ffxo0")
            bd.do("sync", lambda E_, ti=ti, s_xo=s_xo: E_.dma_start(out=x_dst[ti * 128:(ti + 1) * 128, :], in_=CMBX[:]),
                  inc=(s_xo, 16), waits=[(s_v, None)])
        bd.wait("sync", bd.sem("ffxo0"))
    bd.barrier("ffn3")


def _quant_chain(bd, BUF, ws, QS, QI, QM, QSG, s_v, s_g, first_waits=()):
    D_ = ws.stop - ws.start
    NB = D_ // 32
    bd.do("vector", lambda E_: E_.tensor_reduce(QM[:, 0:NB],
          BUF[:, ws].rearrange("p (b c) -> p b c", c=32), axis=AX.X, op=ALU.max,
          apply_absolute_value=True), inc=(s_v, 1), waits=list(first_waits) + [(s_v, None), (s_g, None)])
    bd.do("vector", lambda E_: E_.tensor_scalar(QM[:, 32:32 + NB], QM[:, 0:NB], 1.0 / 7.5, 1e-30,
          op0=ALU.mult, op1=ALU.max), inc=(s_v, 1), waits=[(s_v, None)])
    bd.do("vector", lambda E_: E_.reciprocal(QM[:, 64:64 + NB], QM[:, 32:32 + NB]), inc=(s_v, 1), waits=[(s_v, None)])
    bd.do("vector", lambda E_: E_.tensor_tensor(QS[:, 0:D_].rearrange("p (b c) -> p b c", c=32),
          BUF[:, ws].rearrange("p (b c) -> p b c", c=32),
          QM[:, 64:64 + NB].unsqueeze(2).to_broadcast([128, NB, 32]), op=ALU.mult),
          inc=(s_v, 1), waits=[(s_v, None)])
    bd.do("scalar", lambda E_: E_.activation(QSG[:, 0:D_], QS[:, 0:D_], AF.Sign), inc=(s_g, 1), waits=[(s_v, None)])
    bd.do("scalar", lambda E_: E_.activation(QS[:, D_:2 * D_], QS[:, 0:D_], AF.Abs), inc=(s_g, 1), waits=[(s_g, None)])
    bd.do("vector", lambda E_: E_.tensor_scalar(BUF[:, ws], QS[:, D_:2 * D_], 2.0, None, op0=ALU.is_ge),
          inc=(s_v, 1), waits=[(s_g, None)])
    bd.do("vector", lambda E_: E_.tensor_scalar(BUF[:, ws], BUF[:, ws], -4.0, 8.0, op0=ALU.mult, op1=ALU.add),
          inc=(s_v, 1), waits=[(s_v, None)])
    bd.do("vector", lambda E_: E_.tensor_scalar(QI[:, 0:D_], QS[:, D_:2 * D_], 4.0, None, op0=ALU.is_ge),
          inc=(s_v, 1), waits=[(s_v, None)])
    bd.do("vector", lambda E_: E_.scalar_tensor_tensor(BUF[:, ws], QI[:, 0:D_], -2.0, BUF[:, ws],
          op0=ALU.mult, op1=ALU.add), inc=(s_v, 1), waits=[(s_v, None)])
    bd.do("vector", lambda E_: E_.tensor_tensor(QS[:, D_:2 * D_], QS[:, D_:2 * D_], BUF[:, ws], op=ALU.mult),
          inc=(s_v, 1), waits=[(s_v, None)])
    bd.do("vector", lambda E_: E_.tensor_copy(QI[:, 0:D_], QS[:, D_:2 * D_]), inc=(s_v, 1), waits=[(s_v, None)])
    bd.do("vector", lambda E_: E_.tensor_copy(QS[:, D_:2 * D_], QI[:, 0:D_]), inc=(s_v, 1), waits=[(s_v, None)])
    bd.do("vector", lambda E_: E_.reciprocal(BUF[:, ws], BUF[:, ws]), inc=(s_v, 1), waits=[(s_v, None)])
    bd.do("vector", lambda E_: E_.tensor_tensor(QS[:, D_:2 * D_], QS[:, D_:2 * D_], BUF[:, ws], op=ALU.mult),
          inc=(s_v, 1), waits=[(s_v, None)])
    bd.do("vector", lambda E_: E_.tensor_tensor(QS[:, D_:2 * D_], QS[:, D_:2 * D_], QSG[:, 0:D_], op=ALU.mult),
          inc=(s_v, 1), waits=[(s_v, None)])
    bd.do("vector", lambda E_: E_.tensor_tensor(BUF[:, ws].rearrange("p (b c) -> p b c", c=32),
          QS[:, D_:2 * D_].rearrange("p (b c) -> p b c", c=32),
          QM[:, 32:32 + NB].unsqueeze(2).to_broadcast([128, NB, 32]), op=ALU.mult),
          inc=(s_v, 1), waits=[(s_v, None)])


# version 04 (final)
# speedup vs baseline: 1.1265x; 1.1265x over previous
import numpy as np
import ml_dtypes
from contextlib import ExitStack

import concourse.bass as bass
import concourse.mybir as mybir
from concourse.bass_utils import run_bass_kernel_spmd

F32 = mybir.dt.float32
BF16 = mybir.dt.bfloat16
I32 = mybir.dt.int32
AF = mybir.ActivationFunctionType
ALU = mybir.AluOpType
AX = mybir.AxisListType

NCORES = 8
BPC = 2
S = 2048
D = 1024
T = BPC * S
NT = T // 128
NTB = S // 128

ENGINES = ("tensor", "vector", "scalar", "gpsimd", "sync")
BF = ml_dtypes.bfloat16


class Ctr:
    def __init__(self, handle, name):
        self.h = handle
        self.name = name
        self.n = 0


class Build:
    def __init__(self, nc, ctx):
        self.nc = nc
        self.ctx = ctx
        self.q = {e: [] for e in ENGINES}
        self.sems = {}
        self.dram = {}
        self.inputs = {}
        self.sb = {}
        self.uid = 0
        self.last_inc = {e: None for e in ENGINES}
        self.dma_ctrs = {"sync": {}, "gpsimd": {}}
        self.waited = {e: {} for e in ENGINES}

    def do(self, eng, fn, inc=None, waits=()):
        waits = [(c, (c.n if v is None else v)) for (c, v) in waits]
        w8 = self.waited[eng]
        pruned = []
        for c, v in waits:
            if v > 0 and w8.get(c.name, 0) < v:
                w8[c.name] = v
                pruned.append((c, v))
        waits = pruned
        if inc is not None:
            ctr, k = inc
            ctr.n += k
            tgt = (ctr, k)
            self.last_inc[eng] = (ctr, ctr.n)
            if eng in self.dma_ctrs:
                self.dma_ctrs[eng][ctr.name] = ctr
        else:
            tgt = None

        def closure(E, fn=fn, waits=waits, tgt=tgt):
            for c, v in waits:
                if v > 0:
                    E.wait_ge(c.h, v)
            r = fn(E)
            if tgt is not None:
                c, k = tgt
                r.then_inc(c.h, k)
        self.q[eng].append(closure)

    def wait(self, eng, ctr, val=None):
        v = ctr.n if val is None else val
        if v > 0 and self.waited[eng].get(ctr.name, 0) < v:
            self.waited[eng][ctr.name] = v
            self.q[eng].append(lambda E, h=ctr.h, v=v: E.wait_ge(h, v))

    def emit(self, block):
        for eng in ENGINES:
            dec = getattr(block, eng)

            def make(eng_):
                def _(E):
                    for fn in self.q[eng_]:
                        fn(E)
                return _
            dec(make(eng_=eng))

    def sem(self, name):
        if name not in self.sems:
            h = self.ctx.enter_context(self.nc.semaphore(name))
            self.sems[name] = Ctr(h, name)
        return self.sems[name]

    def din(self, name, shape, dtype=F32):
        if name not in self.inputs:
            self.inputs[name] = self.nc.dram_tensor(name, list(shape), dtype, kind="ExternalInput").ap()
        return self.inputs[name]

    def dout(self, name, shape, dtype=F32):
        self.inputs[name] = self.nc.dram_tensor(name, list(shape), dtype, kind="ExternalOutput").ap()
        return self.inputs[name]

    def dscratch(self, name, shape, dtype=F32):
        if name not in self.dram:
            self.dram[name] = self.nc.dram_tensor(name, list(shape), dtype, kind="Internal").ap()
        return self.dram[name]

    def sbuf(self, name, shape, dtype=F32):
        if name not in self.sb:
            self.sb[name] = self.ctx.enter_context(self.nc.sbuf_tensor(name, list(shape), dtype))
        return self.sb[name]

    def psum(self, name, shape, dtype=F32):
        if name not in self.sb:
            self.sb[name] = self.ctx.enter_context(self.nc.psum_tensor(name, list(shape), dtype))
        return self.sb[name]

    def barrier(self, tag):
        beats = {e: self.sem(f"beat_{e}") for e in ENGINES}
        PC, IDB = self.sb["PC"], self.sb["IDB"]
        BT = self.sb["BEAT"]
        btd = self.dram["beat_dram"]
        self.do("vector", lambda E: E.memset(BT[0:1, 0:64], 0.0), inc=(beats["vector"], 1),
                waits=[(beats["vector"], None)])
        self.do("scalar", lambda E: E.memzero(BT[0:1, 64:128]), inc=(beats["scalar"], 1),
                waits=[(beats["scalar"], None)])
        for eng, dst in (("sync", slice(128, 256)), ("gpsimd", slice(256, 384))):
            for c in list(self.dma_ctrs[eng].values()):
                self.wait(eng, c)
            self.do(eng, lambda E, dst=dst: E.dma_start(out=BT[0:1, dst], in_=btd),
                    inc=(beats[eng], 16), waits=[(beats[eng], None)])
        self.do("tensor", lambda E: E.matmul(PC[0:1, 508:512], IDB[0:1, 0:1], IDB[0:1, 0:4],
                start=True, stop=True), inc=(beats["tensor"], 1),
                waits=[(beats["tensor"], None), (beats["vector"], None), (beats["scalar"], None),
                       (beats["sync"], None), (beats["gpsimd"], None)])
        for e in ENGINES:
            for e2 in ENGINES:
                if e2 != e:
                    self.wait(e, beats[e2])
        for e in ENGINES:
            w8 = self.waited[e]
            for name, c in self.sems.items():
                if w8.get(name, 0) < c.n:
                    w8[name] = c.n


def ceil_div(a, b):
    return (a + b - 1) // b



def make_constants():
    c = {}
    c["identb"] = np.eye(128, dtype=BF)
    c["identf"] = np.eye(128, dtype=np.float32)
    inv_freq = 10000.0 ** (-(np.arange(32, dtype=np.float64) / 32.0))
    ang = np.arange(S, dtype=np.float64)[:, None] * inv_freq[None, :]
    c["ropeC"] = np.concatenate([np.cos(ang), np.cos(ang)], 1).astype(np.float32)
    c["ropeS"] = np.concatenate([-np.sin(ang), np.sin(ang)], 1).astype(np.float32)
    idx = np.arange(128)
    c["utrib"] = (idx[:, None] <= idx[None, :]).astype(BF)
    c["utrif"] = (idx[:, None] <= idx[None, :]).astype(np.float32)
    c["onescolb"] = np.ones((128, 1), dtype=BF)
    for off in (-384, -256, -128, 0, 128, 256, 384):
        m = ((off + idx[:, None]) <= np.arange(512)[None, :]).astype(BF)
        c[f"cm{off}"] = m
    return c


CONSTS = make_constants()



def emit_ln(bd: Build, layer, x_src):
    xn_dram = bd.dscratch("xn_dram", (T, D), BF16)
    quant = (layer == 3)
    if quant:
        xdt_dram = bd.dscratch("xdt_dram", (D, T), F32)
        PC = bd.sb["PC"]
        identf = bd.sb["IDF"]

    nc = bd.nc
    lctx = ExitStack()
    LNX = lctx.enter_context(nc.sbuf_tensor(f"ln_x{layer}", [128, 2 * D], F32))
    SCR = lctx.enter_context(nc.sbuf_tensor(f"ln_scr{layer}", [128, 2 * D], F32))
    XNB = lctx.enter_context(nc.sbuf_tensor(f"ln_xnb{layer}", [128, 4 * D], BF16))
    ST = lctx.enter_context(nc.sbuf_tensor(f"ln_st{layer}", [128, 16], F32))
    s_ld = [bd.sem("lnld0"), bd.sem("lnld1")]
    s_sq = bd.sem("lnsq")
    s_v = bd.sem("lnv")
    s_st = [bd.sem("lnst0"), bd.sem("lnst1")]
    s_t = bd.sem("lnt")
    if quant:
        QS = lctx.enter_context(nc.sbuf_tensor("q_scr", [128, 4 * D], F32))
        QI = lctx.enter_context(nc.sbuf_tensor("q_int", [128, 2 * D], I32))
        QM = lctx.enter_context(nc.sbuf_tensor("q_small", [128, 2 * 3 * 32], F32))
        QXD = lctx.enter_context(nc.sbuf_tensor("q_xd", [128, 2 * D], F32))
        QSG = lctx.enter_context(nc.sbuf_tensor("q_sgn", [128, 2 * D], F32))
        QDT = lctx.enter_context(nc.sbuf_tensor("q_xdt", [128, 2 * D], F32))

    vdone, sqdone, stdone = [], [], []
    for i in range(NT):
        bf = i % 2
        xs = slice(bf * D, (bf + 1) * D)
        c0 = bf * 8
        guards = [(s_ld, None)] if False else []
        lw = []
        if i >= 2:
            lw = [(s_v, vdone[i - 2]), (s_sq, sqdone[i - 2])]
        bd.do("sync", lambda E, i=i, xs=xs: E.dma_start(out=LNX[:, xs], in_=x_src[i * 128:(i + 1) * 128, :]),
              inc=(s_ld[bf], 16), waits=lw)
        ldn = s_ld[bf].n
        bd.do("scalar", lambda E, xs=xs, c0=c0: E.activation(SCR[:, xs], LNX[:, xs], AF.Square,
              accum_out=ST[:, c0 + 1:c0 + 2]), inc=(s_sq, 1), waits=[(s_ld[bf], ldn)])
        bd.do("vector", lambda E, xs=xs, c0=c0: E.tensor_reduce(ST[:, c0:c0 + 1], LNX[:, xs], axis=AX.X, op=ALU.add),
              inc=(s_v, 1), waits=[(s_ld[bf], ldn)])
        bd.do("vector", lambda E, c0=c0: E.tensor_scalar(ST[:, c0 + 2:c0 + 3], ST[:, c0:c0 + 1], 1.0 / D, None, op0=ALU.mult),
              inc=(s_v, 1), waits=[(s_v, None)])
        bd.do("vector", lambda E, c0=c0: E.tensor_tensor(ST[:, c0 + 3:c0 + 4], ST[:, c0 + 2:c0 + 3], ST[:, c0 + 2:c0 + 3], op=ALU.mult),
              inc=(s_v, 1), waits=[(s_v, None)])
        bd.do("vector", lambda E, c0=c0: E.scalar_tensor_tensor(ST[:, c0 + 4:c0 + 5], ST[:, c0 + 1:c0 + 2], 1.0 / D,
              ST[:, c0 + 3:c0 + 4], op0=ALU.mult, op1=ALU.subtract), inc=(s_v, 1), waits=[(s_sq, None), (s_v, None)])
        bd.do("vector", lambda E, c0=c0: E.tensor_scalar(ST[:, c0 + 5:c0 + 6], ST[:, c0 + 4:c0 + 5], 1e-5, None, op0=ALU.add),
              inc=(s_v, 1), waits=[(s_v, None)])
        bd.do("vector", lambda E, c0=c0: E.reciprocal(ST[:, c0 + 6:c0 + 7], ST[:, c0 + 5:c0 + 6]),
              inc=(s_v, 1), waits=[(s_v, None)])
        bd.do("scalar", lambda E, c0=c0: E.activation(ST[:, c0 + 7:c0 + 8], ST[:, c0 + 6:c0 + 7], AF.Sqrt),
              inc=(s_sq, 1), waits=[(s_v, None)])
        xw = [(s_sq, None)] + ([(s_st[bf], None)] if i >= 2 else [])
        bd.do("vector", lambda E, xs=xs, c0=c0: E.tensor_scalar(SCR[:, xs], LNX[:, xs], ST[:, c0 + 2:c0 + 3],
              ST[:, c0 + 7:c0 + 8], op0=ALU.subtract, op1=ALU.mult), inc=(s_v, 1), waits=xw)
        bd.do("vector", lambda E, xs=xs: E.tensor_copy(XNB[:, xs], SCR[:, xs]),
              inc=(s_v, 1), waits=[(s_v, None)])
        los = slice(2 * D + xs.start, 2 * D + xs.stop)
        bd.do("vector", lambda E, xs=xs, los=los: E.tensor_tensor(XNB[:, los], SCR[:, xs], XNB[:, xs], op=ALU.subtract),
              inc=(s_v, 1), waits=[(s_v, None)])
        bd.do("sync", lambda E, i=i, xs=xs: E.dma_start(out=xn_dram[i * 128:(i + 1) * 128, :], in_=XNB[:, xs]),
              inc=(s_st[bf], 16), waits=[(s_v, None)])
        bd.do("sync", lambda E, i=i, los=los: E.dma_start(out=bd.dram["xnlo_dram"][i * 128:(i + 1) * 128, :], in_=XNB[:, los]),
              inc=(s_st[bf], 16), waits=[(s_v, None)])

        if quant:
            ys = slice(bf * 2 * D, bf * 2 * D + D)
            ds = slice(bf * 2 * D + D, (bf + 1) * 2 * D)
            m0 = bf * 96
            bd.do("vector", lambda E, xs=xs, ys=ys, c0=c0: E.tensor_scalar(QS[:, ys], LNX[:, xs], ST[:, c0 + 2:c0 + 3],
                  ST[:, c0 + 7:c0 + 8], op0=ALU.subtract, op1=ALU.mult), inc=(s_v, 1), waits=[(s_sq, None)])
            bd.do("vector", lambda E, ys=ys, m0=m0: E.tensor_reduce(QM[:, m0:m0 + 32],
                  QS[:, ys].rearrange("p (b c) -> p b c", c=32), axis=AX.X, op=ALU.max, apply_absolute_value=True),
                  inc=(s_v, 1), waits=[(s_v, None)])
            bd.do("vector", lambda E, m0=m0: E.tensor_scalar(QM[:, m0 + 32:m0 + 64], QM[:, m0:m0 + 32], 1.0 / 7.5, 1e-30,
                  op0=ALU.mult, op1=ALU.max), inc=(s_v, 1), waits=[(s_v, None)])
            bd.do("vector", lambda E, m0=m0: E.reciprocal(QM[:, m0 + 64:m0 + 96], QM[:, m0 + 32:m0 + 64]),
                  inc=(s_v, 1), waits=[(s_v, None)])
            bd.do("vector", lambda E, ys=ys, m0=m0: E.tensor_tensor(QS[:, ys].rearrange("p (b c) -> p b c", c=32),
                  QS[:, ys].rearrange("p (b c) -> p b c", c=32),
                  QM[:, m0 + 64:m0 + 96].unsqueeze(2).to_broadcast([128, 32, 32]), op=ALU.mult),
                  inc=(s_v, 1), waits=[(s_v, None)])
            bd.do("scalar", lambda E, ys=ys, xs=xs: E.activation(QSG[:, xs], QS[:, ys], AF.Sign),
                  inc=(s_sq, 1), waits=[(s_v, None)])
            bd.do("scalar", lambda E, ys=ys, xs=xs: E.activation(QXD[:, xs], QS[:, ys], AF.Abs),
                  inc=(s_sq, 1), waits=[(s_v, None)])
            bd.do("vector", lambda E, xs=xs, ds=ds: E.tensor_scalar(QS[:, ds], QXD[:, xs], 2.0, None, op0=ALU.is_ge),
                  inc=(s_v, 1), waits=[(s_sq, None)])
            bd.do("vector", lambda E, ds=ds: E.tensor_scalar(QS[:, ds], QS[:, ds], -4.0, 8.0, op0=ALU.mult, op1=ALU.add),
                  inc=(s_v, 1), waits=[(s_v, None)])
            bd.do("vector", lambda E, xs=xs: E.tensor_scalar(QDT[:, xs], QXD[:, xs], 4.0, None, op0=ALU.is_ge),
                  inc=(s_v, 1), waits=[(s_sq, None)])
            bd.do("vector", lambda E, xs=xs, ds=ds: E.scalar_tensor_tensor(QS[:, ds], QDT[:, xs], -2.0, QS[:, ds],
                  op0=ALU.mult, op1=ALU.add), inc=(s_v, 1), waits=[(s_v, None)])
            bd.do("vector", lambda E, xs=xs, ds=ds: E.tensor_tensor(QXD[:, xs], QXD[:, xs], QS[:, ds], op=ALU.mult),
                  inc=(s_v, 1), waits=[(s_v, None)])
            bd.do("vector", lambda E, xs=xs: E.tensor_copy(QI[:, xs], QXD[:, xs]),
                  inc=(s_v, 1), waits=[(s_v, None)])
            bd.do("vector", lambda E, xs=xs: E.tensor_copy(QXD[:, xs], QI[:, xs]),
                  inc=(s_v, 1), waits=[(s_v, None)])
            bd.do("vector", lambda E, ds=ds: E.reciprocal(QS[:, ds], QS[:, ds]),
                  inc=(s_v, 1), waits=[(s_v, None)])
            bd.do("vector", lambda E, xs=xs, ds=ds: E.tensor_tensor(QXD[:, xs], QXD[:, xs], QS[:, ds], op=ALU.mult),
                  inc=(s_v, 1), waits=[(s_v, None)])
            bd.do("vector", lambda E, xs=xs: E.tensor_tensor(QXD[:, xs], QXD[:, xs], QSG[:, xs], op=ALU.mult),
                  inc=(s_v, 1), waits=[(s_v, None)])
            bd.do("vector", lambda E, xs=xs, m0=m0: E.tensor_tensor(QXD[:, xs].rearrange("p (b c) -> p b c", c=32),
                  QXD[:, xs].rearrange("p (b c) -> p b c", c=32),
                  QM[:, m0 + 32:m0 + 64].unsqueeze(2).to_broadcast([128, 32, 32]), op=ALU.mult),
                  inc=(s_v, 1), waits=[(s_v, None)])
            for g2 in range(2):
                for cc in range(4):
                    c = g2 * 4 + cc
                    tw = [(s_v, None)] if cc == 0 else []
                    bd.do("tensor", lambda E, xs=xs, c=c, cc=cc: E.transpose(PC[:, cc * 128:(cc + 1) * 128],
                          QXD[:, xs.start + c * 128: xs.start + (c + 1) * 128], identf[:]),
                          inc=(s_t, 1), waits=tw)
                bd.do("vector", lambda E, g2=g2, xs=xs: E.tensor_copy(QDT[:, xs.start + g2 * 512: xs.start + (g2 + 1) * 512], PC[:]),
                      inc=(s_v, 1), waits=[(s_t, None), (s_st[bf], None)])
                if g2 == 0:
                    pass
            bd.do("sync", lambda E, i=i, xs=xs: E.dma_start(
                  out=xdt_dram.rearrange("(c p) t -> p c t", p=128)[:, :, i * 128:(i + 1) * 128],
                  in_=QDT[:, xs].rearrange("p (c m) -> p c m", c=8)),
                  inc=(s_st[bf], 16), waits=[(s_v, None)])
        vdone.append(s_v.n)
        sqdone.append(s_sq.n)
        stdone.append(0)
    bd.wait("sync", s_st[0])
    bd.wait("sync", s_st[1])
    lctx.close()
    bd.barrier(f"ln{layer}")



def emit_ffn(bd: Build, layer, cfg, x_src, x_dst):
    E, H = cfg["E"], cfg["H"]
    NHC = H // 128
    G = cfg.get("G", 1024)
    NG = T // G
    NTG = G // 128
    xn_dram = bd.dram["xn_dram"]
    s_dram = bd.dram["s_dram"]
    w1 = bd.din(cfg["w1"], (E, D, H))
    w2 = bd.din(cfg["w2"], (E, H, D))
    has_b = cfg.get("b1") is not None
    if has_b:
        b1c = bd.din(cfg["b1"] + "_c", (E, 128, NHC))
        b2 = bd.din(cfg["b2"], (E, D))
    l2 = cfg.get("l2_mode", False)

    PA, PB, PC = bd.sb["PA"], bd.sb["PB"], bd.sb["PC"]
    IDB = bd.sb["IDB"]
    C_sb = bd.sb["C_sb"]
    CT_sb = bd.sb["CT_sb"]

    nc = bd.nc
    fctx = ExitStack()
    XG = fctx.enter_context(nc.sbuf_tensor(f"ff_xg{layer}", [128, NTG * D], BF16))
    ACC = fctx.enter_context(nc.sbuf_tensor(f"ff_acc{layer}", [128, NTG * D], F32))
    XINT = fctx.enter_context(nc.sbuf_tensor(f"ff_xint{layer}", [128, 8 * G], BF16))
    HT = fctx.enter_context(nc.sbuf_tensor(f"ff_ht{layer}", [128, NHC * G], BF16))
    W1C = fctx.enter_context(nc.sbuf_tensor(f"ff_w1{layer}", [128, 2 * 1024], BF16))
    W2SB = fctx.enter_context(nc.sbuf_tensor(f"ff_w2{layer}", [128, 2 * NHC * D], BF16))
    B1SB = fctx.enter_context(nc.sbuf_tensor(f"ff_b1{layer}", [128, 2 * 16], F32))
    B2SB = fctx.enter_context(nc.sbuf_tensor(f"ff_b2{layer}", [16, D], F32))
    DG = fctx.enter_context(nc.sbuf_tensor(f"ff_dg{layer}", [128, 2 * 128], BF16))
    CMBX = fctx.enter_context(nc.sbuf_tensor(f"cmb_x{layer}", [128, D], F32))
    CMBS = fctx.enter_context(nc.sbuf_tensor(f"cmb_s{layer}", [128, D], F32))
    CMBG = fctx.enter_context(nc.sbuf_tensor(f"cmb_g{layer}", [128, D], F32))
    SQS = CMBG
    LST = fctx.enter_context(nc.sbuf_tensor(f"ff_lst{layer}", [128, 16], F32))

    s_w = bd.sem("ffw")
    s_w1 = [bd.sem("ffw1a"), bd.sem("ffw1b")]
    s_w2 = [bd.sem("ffw2a"), bd.sem("ffw2b")]
    s_b1 = [bd.sem("ffb1a"), bd.sem("ffb1b")]
    s_cl2 = [bd.sem("ffcla"), bd.sem("ffclb")]
    s_ot = bd.sem("ffot")
    s_x = bd.sem("ffx")
    s_t = bd.sem("fft")
    s_g = bd.sem("ffg")
    s_v = bd.sem("ffv")
    s_st = bd.sem("ffst")
    s_cl = bd.sem("ffcl")

    w1_last_mm = [0, 0]
    w2_last_mm = [0, 0]
    b1_last_g = [0, 0]

    if has_b:
        bd.do("sync", lambda E_: E_.dma_start(out=B2SB[:E, :], in_=b2), inc=(s_ot, 16))
    if l2:
        gam = bd.din(cfg["gamma"], (1, D))
        bet = bd.din(cfg["beta"], (1, D))
        GB = fctx.enter_context(nc.sbuf_tensor("l2_gb_rows", [1, 2 * D], F32))
        GMB = fctx.enter_context(nc.sbuf_tensor("l2_gamma_b", [128, 2 * D], BF16))
        ONESRF = bd.sb["ONESRF"]
        bd.do("sync", lambda E_: E_.dma_start(out=GB[:, 0:D], in_=gam), inc=(s_ot, 16))
        bd.do("sync", lambda E_: E_.dma_start(out=GB[:, D:2 * D], in_=bet), inc=(s_ot, 16))
        xw = s_ot.n
        for hb in range(2):
            bd.do("tensor", lambda E_, hb=hb: E_.matmul(PB[:, hb * 512:(hb + 1) * 512], ONESRF[:],
                  GB[:, hb * 512:(hb + 1) * 512], start=True, stop=True),
                  inc=(s_t, 1), waits=([(s_ot, xw)] if hb == 0 else []))
            bd.do("tensor", lambda E_, hb=hb: E_.matmul(PA[:, hb * 512:(hb + 1) * 512], ONESRF[:],
                  GB[:, D + hb * 512: D + (hb + 1) * 512], start=True, stop=True), inc=(s_t, 1))
        bd.do("vector", lambda E_: E_.tensor_copy(GMB[:, 0:D], PB[:]), inc=(s_v, 1), waits=[(s_t, None)])
        bd.do("vector", lambda E_: E_.tensor_copy(GMB[:, D:2 * D], PA[:, 0:D]), inc=(s_v, 1), waits=[(s_t, None)])

    for g in range(NG):
        t0 = g * NTG
        ldw = [(s_t, None)] if g > 0 else []
        bd.do("sync", lambda E_, g=g: E_.dma_start(out=XG[:].rearrange("p (i d) -> p i d", d=D),
              in_=xn_dram.rearrange("(i p) d -> p i d", p=128)[:, g * NTG:(g + 1) * NTG, :]),
              inc=(s_x, 16), waits=ldw)
        xg_ready = s_x.n
        for tt in range(NTG):
            ti = t0 + tt
            if has_b:
                for db in range(2):
                    bd.do("tensor", lambda E_, ti=ti, db=db: E_.matmul(PB[:, db * 512:(db + 1) * 512],
                          CT_sb[0:E, ti * 128:(ti + 1) * 128],
                          B2SB[0:E, db * 512:(db + 1) * 512], start=True, stop=True), inc=(s_t, 1),
                          waits=([(s_x, xg_ready), (s_ot, None), (s_v, None)] if db == 0 else []))
                bd.do("vector", lambda E_, tt=tt: E_.tensor_copy(ACC[:, tt * D:(tt + 1) * D], PB[:]),
                      inc=(s_v, 1), waits=[(s_t, None)])
            else:
                bd.do("vector", lambda E_, tt=tt: E_.memset(ACC[:, tt * D:(tt + 1) * D], 0.0), inc=(s_v, 1))

        for e in range(E):
            ebuf = e % 2
            bd.do("gpsimd", lambda E_, e=e, ebuf=ebuf: E_.dma_start(
                  out=W2SB[:, ebuf * NHC * D: (ebuf + 1) * NHC * D].rearrange("p (c d) -> p c d", d=D),
                  in_=w2[e].rearrange("(c p) d -> p c d", p=128)),
                  inc=(s_w2[ebuf], 16), waits=[(s_t, w2_last_mm[ebuf])])
            if has_b:
                bd.do("gpsimd", lambda E_, e=e, ebuf=ebuf: E_.dma_start(
                      out=B1SB[:, ebuf * 16: ebuf * 16 + NHC], in_=b1c[e]),
                      inc=(s_b1[ebuf], 16), waits=[(s_g, b1_last_g[ebuf])])
            w2_ready = s_w2[ebuf].n
            b1_ready = s_b1[ebuf].n
            gelu_guard = s_g.n
            for tt in range(NTG):
                ti = t0 + tt
                dbf = tt % 2
                bd.do("vector", lambda E_, ti=ti, e=e, dbf=dbf: E_.tensor_scalar(
                      DG[:, dbf * 128:(dbf + 1) * 128], IDB[:], C_sb[:, ti * E + e: ti * E + e + 1], None, op0=ALU.mult),
                      inc=(s_v, 1), waits=[(s_x, xg_ready), (s_t, None)])
                pa0 = dbf * D
                for c in range(8):
                    tw = [(s_v, None), (s_g, gelu_guard)] if c == 0 else []
                    if c == 0 and tt >= 2:
                        tw.append((s_v, None))
                        tw.append((s_g, None))
                    bd.do("tensor", lambda E_, tt=tt, c=c, pa0=pa0, dbf=dbf: E_.matmul(
                          PA[:, pa0 + c * 128: pa0 + (c + 1) * 128], XG[:, tt * D + c * 128: tt * D + (c + 1) * 128],
                          DG[:, dbf * 128:(dbf + 1) * 128], start=True, stop=True),
                          inc=(s_t, 1), waits=tw)
                eng = "vector" if tt % 2 == 0 else "scalar"
                ctrx = s_v if eng == "vector" else s_g

                def evac_fn(E_, tt=tt, pa0=pa0, eng=eng):
                    src = PA[:, pa0:pa0 + D].rearrange("p (c m) -> p c m", c=8)
                    dv = XINT[:].rearrange("p (c t) -> p c t", c=8)[:, :, tt * 128:(tt + 1) * 128]
                    if eng == "vector":
                        return E_.tensor_copy(dv, src)
                    return E_.copy(dv, src)
                bd.do(eng, evac_fn, inc=(ctrx, 1), waits=[(s_t, None)])
            xin_v, xin_g = s_v.n, s_g.n
            GB512 = G // 512
            for hc in range(NHC):
                pp = hc % 2
                wbuf = hc % 2
                bd.do("gpsimd", lambda E_, e=e, hc=hc, wbuf=wbuf: E_.dma_start(
                      out=W1C[:, wbuf * 1024: (wbuf + 1) * 1024].rearrange("p (c h) -> p c h", c=8),
                      in_=w1[e].rearrange("(c p) h -> p c h", p=128)[:, :, hc * 128:(hc + 1) * 128]),
                      inc=(s_w1[wbuf], 16), waits=[(s_t, w1_last_mm[wbuf])])
                w1rdy = s_w1[wbuf].n
                mmw = [(s_w1[wbuf], w1rdy), (s_v, xin_v), (s_g, xin_g)]
                if hc >= 2:
                    mmw.append((s_g, None))
                for tb in range(GB512):
                    for dc in range(8):
                        w_ = mmw if (tb == 0 and dc == 0) else []
                        bd.do("tensor", lambda E_, pp=pp, tb=tb, dc=dc, wbuf=wbuf: E_.matmul(
                              PA[:, pp * G + tb * 512: pp * G + tb * 512 + 512],
                              W1C[:, wbuf * 1024 + dc * 128: wbuf * 1024 + (dc + 1) * 128],
                              XINT[:, dc * G + tb * 512: dc * G + tb * 512 + 512],
                              start=(dc == 0), stop=(dc == 7)), inc=(s_t, 1), waits=w_)
                w1_last_mm[wbuf] = s_t.n

                def gelu_fn(E_, hc=hc, pp=pp, ebuf=ebuf):
                    bias = B1SB[:, ebuf * 16 + hc: ebuf * 16 + hc + 1] if has_b else 0.0
                    return E_.activation(HT[:, hc * G:(hc + 1) * G], PA[:, pp * G:(pp + 1) * G],
                                         AF.Gelu, bias=bias)
                bd.do("scalar", gelu_fn, inc=(s_g, 1), waits=[(s_t, None), (s_b1[ebuf], b1_ready) if has_b else (s_t, 0)])
                if has_b:
                    b1_last_g[ebuf] = s_g.n
            ht_ready = s_g.n
            for tt in range(NTG):
                ti = t0 + tt
                mmw = [(s_g, ht_ready), (s_w2[ebuf], w2_ready)]
                if tt >= 1:
                    mmw.append((s_v, None))
                for db in range(2):
                    for hc in range(NHC):
                        w_ = mmw if (db == 0 and hc == 0) else []
                        bd.do("tensor", lambda E_, tt=tt, db=db, hc=hc, ebuf=ebuf: E_.matmul(
                              PB[:, db * 512:(db + 1) * 512],
                              HT[:, hc * G + tt * 128: hc * G + (tt + 1) * 128],
                              W2SB[:, ebuf * NHC * D + hc * D + db * 512: ebuf * NHC * D + hc * D + db * 512 + 512],
                              start=(hc == 0), stop=(hc == NHC - 1)), inc=(s_t, 1), waits=w_)
                bd.do("vector", lambda E_, tt=tt, ti=ti, e=e: E_.scalar_tensor_tensor(
                      ACC[:, tt * D:(tt + 1) * D], PB[:], C_sb[:, ti * E + e: ti * E + e + 1],
                      ACC[:, tt * D:(tt + 1) * D], op0=ALU.mult, op1=ALU.add),
                      inc=(s_v, 1), waits=[(s_t, None)])
            w2_last_mm[ebuf] = s_t.n

        for tt in range(NTG):
            ti = t0 + tt
            bf = 0
            xs = slice(0, D)
            clw = ([(s_v, None), (bd.sem("ffxo0"), None)] if (g > 0 or tt >= 1) else [])
            bd.do("sync", lambda E_, ti=ti, xs=xs, bf=bf: E_.dma_start(out=CMBX[:, xs], in_=x_src[ti * 128:(ti + 1) * 128, :]),
                  inc=(s_cl2[bf], 16), waits=clw)
            bd.do("sync", lambda E_, ti=ti, xs=xs, bf=bf: E_.dma_start(out=CMBS[:, xs], in_=s_dram[ti * 128:(ti + 1) * 128, :]),
                  inc=(s_cl2[bf], 16))
            clr = s_cl2[bf].n
            accs = slice(tt * D, (tt + 1) * D)
            if l2:
                TVc = bd.sb["TV_sb"]
                c0 = bf * 8
                bd.do("vector", lambda E_, accs=accs, ti=ti: E_.tensor_scalar(ACC[:, accs], ACC[:, accs],
                      TVc[:, ti:ti + 1], None, op0=ALU.mult), inc=(s_v, 1), waits=[(s_v, None), (s_g, None)])
                bd.do("vector", lambda E_, accs=accs, c0=c0: E_.tensor_reduce(LST[:, c0:c0 + 1], ACC[:, accs], axis=AX.X, op=ALU.add),
                      inc=(s_v, 1), waits=[(s_v, None)])
                bd.do("scalar", lambda E_, accs=accs, c0=c0: E_.activation(SQS[:, 0:D], ACC[:, accs], AF.Square,
                      accum_out=LST[:, c0 + 1:c0 + 2]), inc=(s_g, 1), waits=[(s_v, None)])
                bd.do("vector", lambda E_, c0=c0: E_.tensor_scalar(LST[:, c0 + 2:c0 + 3], LST[:, c0:c0 + 1], 1.0 / D, None, op0=ALU.mult),
                      inc=(s_v, 1), waits=[(s_v, None)])
                bd.do("vector", lambda E_, c0=c0: E_.tensor_tensor(LST[:, c0 + 3:c0 + 4], LST[:, c0 + 2:c0 + 3], LST[:, c0 + 2:c0 + 3], op=ALU.mult),
                      inc=(s_v, 1), waits=[(s_v, None)])
                bd.do("vector", lambda E_, c0=c0: E_.scalar_tensor_tensor(LST[:, c0 + 4:c0 + 5], LST[:, c0 + 1:c0 + 2], 1.0 / D,
                      LST[:, c0 + 3:c0 + 4], op0=ALU.mult, op1=ALU.subtract), inc=(s_v, 1), waits=[(s_g, None), (s_v, None)])
                bd.do("vector", lambda E_, c0=c0: E_.tensor_scalar(LST[:, c0 + 5:c0 + 6], LST[:, c0 + 4:c0 + 5], 1e-5, None, op0=ALU.add),
                      inc=(s_v, 1), waits=[(s_v, None)])
                bd.do("vector", lambda E_, c0=c0: E_.reciprocal(LST[:, c0 + 6:c0 + 7], LST[:, c0 + 5:c0 + 6]),
                      inc=(s_v, 1), waits=[(s_v, None)])
                bd.do("scalar", lambda E_, c0=c0: E_.activation(LST[:, c0 + 7:c0 + 8], LST[:, c0 + 6:c0 + 7], AF.Sqrt),
                      inc=(s_g, 1), waits=[(s_v, None)])
                bd.do("vector", lambda E_, accs=accs, c0=c0: E_.tensor_scalar(ACC[:, accs], ACC[:, accs], LST[:, c0 + 2:c0 + 3],
                      LST[:, c0 + 7:c0 + 8], op0=ALU.subtract, op1=ALU.mult), inc=(s_v, 1), waits=[(s_g, None)])
                bd.do("vector", lambda E_, accs=accs: E_.tensor_tensor(ACC[:, accs], ACC[:, accs], GMB[:, 0:D], op=ALU.mult),
                      inc=(s_v, 1), waits=[(s_v, None)])
                bd.do("vector", lambda E_, accs=accs: E_.tensor_tensor(ACC[:, accs], ACC[:, accs], GMB[:, D:2 * D], op=ALU.add),
                      inc=(s_v, 1), waits=[(s_v, None)])
            bd.do("scalar", lambda E_, accs=accs: E_.activation(CMBG[:, 0:D], ACC[:, accs], AF.Silu),
                  inc=(s_g, 1), waits=[(s_v, None)])
            bd.do("vector", lambda E_, xs=xs, bf=bf: E_.tensor_tensor(CMBS[:, xs], CMBS[:, xs], CMBG[:, 0:D], op=ALU.mult),
                  inc=(s_v, 1), waits=[(s_g, None), (s_cl2[bf], clr)])
            bd.do("vector", lambda E_, xs=xs: E_.tensor_tensor(CMBX[:, xs], CMBX[:, xs], CMBS[:, xs], op=ALU.add),
                  inc=(s_v, 1), waits=[(s_v, None)])
            s_xo = bd.sem(f"ffxo{bf}")
            bd.do("sync", lambda E_, ti=ti, xs=xs, s_xo=s_xo: E_.dma_start(out=x_dst[ti * 128:(ti + 1) * 128, :], in_=CMBX[:, xs]),
                  inc=(s_xo, 16), waits=[(s_v, None)])
    bd.wait("sync", bd.sem("ffxo0"))
    bd.wait("sync", bd.sem("ffxo1"))
    fctx.close()
    bd.barrier(f"ffn{layer}")



def emit_router(bd, layer, b, cfg, XNT, XNTL, sems):
    E = cfg["E"]
    K = cfg["topk"]
    l2 = cfg.get("l2_mode", False)
    l0 = (layer == 0)
    PC = bd.sb["PC"]
    IDF = bd.sb["IDF"]
    C_sb, CT_sb = bd.sb["C_sb"], bd.sb["CT_sb"]
    RWsb, RBB = bd.sb["RWsb"], bd.sb["RBB"]
    s_t, s_v, s_g, s_x = sems["t"], sems["v"], sems["g"], sems["x"]
    s_w = sems["w"]
    RL = bd.sb["RL"]
    RP = bd.sb["RP"]
    RT8 = bd.sb["RT8"]
    RC = bd.sb["RC"]
    RS = bd.sb["RS"]
    ROH = bd.sb["ROH"]
    RTP = bd.sb["RTP"]
    SSUM = bd.sb["SSUM"]
    xnt_ready = s_x.n
    ct_evac_done = s_v.n
    for tt in range(NTB):
        ti = b * NTB + tt
        bf = tt % 2
        ls = slice(bf * 16, bf * 16 + E)
        c0 = bf * 4
        first = True
        RWLO = bd.sb["RWLO"]
        passes = ((XNT, RWsb), (XNT, RWLO), (XNTL, RWsb))
        for pi, (XX, WW) in enumerate(passes):
            for dc in range(8):
                w_ = [(s_x, xnt_ready), (s_v, None), (s_w, None)] if first else []
                st = (pi == 0 and dc == 0)
                sp = (pi == 2 and dc == 7)
                bd.do("tensor", lambda E_, tt=tt, dc=dc, XX=XX, WW=WW, st=st, sp=sp: E_.matmul(
                      PC[:, 0:E], XX[:, dc * S + tt * 128: dc * S + tt * 128 + 128],
                      WW[:, dc * 16: dc * 16 + E], start=st, stop=sp), inc=(s_t, 1), waits=w_)
                first = False
        lg = s_t.n
        bd.do("vector", lambda E_, ls=ls: E_.tensor_tensor(RL[:, ls], PC[:, 0:E], RBB[:, 0:E], op=ALU.add),
              inc=(s_v, 1), waits=[(s_t, lg)])
        bd.do("vector", lambda E_, ls=ls, c0=c0: E_.tensor_reduce(RS[:, c0:c0 + 1], RL[:, ls], axis=AX.X, op=ALU.max, negate=True),
              inc=(s_v, 1), waits=[(s_v, None)])
        bd.do("scalar", lambda E_, ls=ls, c0=c0: E_.activation(RP[:, ls], RL[:, ls], AF.Exp,
              bias=RS[:, c0:c0 + 1], accum_out=RS[:, c0 + 1:c0 + 2]), inc=(s_g, 1), waits=[(s_v, None)])
        bd.do("vector", lambda E_, c0=c0: E_.reciprocal(RS[:, c0 + 2:c0 + 3], RS[:, c0 + 1:c0 + 2]),
              inc=(s_v, 1), waits=[(s_g, None)])
        bd.do("vector", lambda E_, ls=ls, c0=c0: E_.tensor_scalar(RP[:, ls], RP[:, ls], RS[:, c0 + 2:c0 + 3], None, op0=ALU.mult),
              inc=(s_v, 1), waits=[(s_v, None)])
        t8 = slice(bf * 8, bf * 8 + 8)
        bd.do("vector", lambda E_, ls=ls, t8=t8: E_.max(out=RT8[:, t8], in_=RP[:, ls]),
              inc=(s_v, 1), waits=[(s_v, None)])
        if l2:
            TV = bd.sb["TV_sb"]
            bd.do("vector", lambda E_, t8=t8, c0=c0: E_.tensor_scalar(RS[:, c0 + 3:c0 + 4], RT8[:, t8.start:t8.start + 1], 1e-8, None, op0=ALU.add),
                  inc=(s_v, 1), waits=[(s_v, None)])
            bd.do("vector", lambda E_, c0=c0: E_.reciprocal(RS[:, c0:c0 + 1], RS[:, c0 + 3:c0 + 4]),
                  inc=(s_v, 1), waits=[(s_v, None)])
            bd.do("vector", lambda E_, ti=ti, t8=t8, c0=c0: E_.scalar_tensor_tensor(TV[:, ti:ti + 1],
                  RT8[:, t8.start:t8.start + 1], 1.0, RS[:, c0:c0 + 1], op0=ALU.mult, op1=ALU.mult),
                  inc=(s_v, 1), waits=[(s_v, None)])
            bd.do("vector", lambda E_, ls=ls, t8=t8: E_.tensor_scalar(RC[:, ls], RP[:, ls],
                  RT8[:, t8.start:t8.start + 1], None, op0=ALU.is_ge), inc=(s_v, 1), waits=[(s_v, None)])
        else:
            bd.do("vector", lambda E_, t8=t8, c0=c0: E_.tensor_reduce(RS[:, c0:c0 + 1], RT8[:, t8.start:t8.start + K], axis=AX.X, op=ALU.add),
                  inc=(s_v, 1), waits=[(s_v, None)])
            bd.do("vector", lambda E_, c0=c0: E_.reciprocal(RS[:, c0 + 3:c0 + 4], RS[:, c0:c0 + 1]),
                  inc=(s_v, 1), waits=[(s_v, None)])
            bd.do("vector", lambda E_, ls=ls, t8=t8, c0=c0: E_.tensor_scalar(RC[:, ls], RP[:, ls],
                  RT8[:, t8.start + K - 1:t8.start + K], None, op0=ALU.is_ge), inc=(s_v, 1), waits=[(s_v, None)])
            bd.do("vector", lambda E_, ls=ls, c0=c0: E_.scalar_tensor_tensor(RC[:, ls], RP[:, ls], RS[:, c0 + 3:c0 + 4],
                  RC[:, ls], op0=ALU.mult, op1=ALU.mult), inc=(s_v, 1), waits=[(s_v, None)])
        bd.do("vector", lambda E_, ti=ti, ls=ls: E_.tensor_copy(C_sb[:, ti * E:(ti + 1) * E], RC[:, ls]),
              inc=(s_v, 1), waits=[(s_v, None)])
        cwr = s_v.n
        if l0:
            for k in range(2):
                bd.do("vector", lambda E_, ls=ls, t8=t8, k=k: E_.tensor_scalar(ROH[:, ls], RP[:, ls],
                      RT8[:, t8.start + k:t8.start + k + 1], None, op0=ALU.is_equal),
                      inc=(s_v, 1), waits=[(s_v, None), (s_t, None)])
                bd.do("vector", lambda E_, t8=t8, k=k, c0=c0, bf=bf: E_.scalar_tensor_tensor(RTP[:, bf * 2 + k:bf * 2 + k + 1],
                      RT8[:, t8.start + k:t8.start + k + 1], 1.0, RS[:, c0 + 3:c0 + 4], op0=ALU.mult, op1=ALU.mult),
                      inc=(s_v, 1), waits=[(s_v, None)])
                bd.do("tensor", lambda E_, ls=ls, k=k, bf=bf: E_.matmul(PC[0:8, 24 + k:25 + k], ROH[:, ls],
                      RTP[:, bf * 2 + k:bf * 2 + k + 1], start=True, stop=True), inc=(s_t, 1), waits=[(s_v, None)])
            bd.do("vector", lambda E_: E_.scalar_tensor_tensor(SSUM[:, 0:2], PC[0:8, 24:26], 1.0, SSUM[:, 0:2],
                  op0=ALU.mult, op1=ALU.add), inc=(s_v, 1), waits=[(s_t, None)])
        bd.do("tensor", lambda E_, ls=ls: E_.transpose(PC[0:E, 128:256], RC[:, ls], IDF[:]),
              inc=(s_t, 1), waits=[(s_v, None)])
        bd.do("vector", lambda E_, ti=ti: E_.tensor_copy(CT_sb[0:E, ti * 128:(ti + 1) * 128], PC[0:E, 128:256]),
              inc=(s_v, 1), waits=[(s_t, None)])
        ct_evac_done = s_v.n



def emit_l0_scatter_fix(bd, E, sems):
    nc = bd.nc
    s_t, s_v, s_g, s_st = sems["t"], sems["v"], sems["g"], sems["st"]
    s_cc = bd.sem("l0cc")
    PC = bd.sb["PC"]
    IDF = bd.sb["IDF"]
    SSUM = bd.sb["SSUM"]
    C_sb, CT_sb = bd.sb["C_sb"], bd.sb["CT_sb"]
    C0G = bd.sb["C0G"]
    ADDT = bd.sb["ADDT"]
    ssum_d = bd.dscratch("ssum_d", (8, 2), F32)
    ssum_o = bd.dscratch("ssum_o", (8, 2), F32)
    s_fx = bd.sem("l0fx")
    bd.do("sync", lambda E_: E_.dma_start(out=ssum_d, in_=SSUM[:, 0:2]), inc=(s_fx, 16), waits=[(s_v, None)])
    bd.wait("gpsimd", s_fx)
    import os as _os
    if _os.environ.get("KNOCC"):
        bd.do("gpsimd", lambda E_: E_.dma_start(out=ssum_o, in_=ssum_d), inc=(s_cc, 16))
    else:
        bd.do("gpsimd", lambda E_: E_.collective_compute("AllReduce", ALU.add,
              replica_groups=[list(range(NCORES))], ins=[ssum_d.opt()], outs=[ssum_o.opt()]), inc=(s_cc, 1))
    bd.do("vector", lambda E_: E_.memset(ADDT[:], 0.0), inc=(s_v, 1))
    bd.do("sync", lambda E_: E_.dma_start(out=ADDT[0:8, 0:2], in_=ssum_o), inc=(s_fx, 16),
          waits=[(s_cc, None), (s_v, None)])
    ld = s_fx.n
    bd.do("vector", lambda E_: E_.tensor_tensor(ADDT[:], ADDT[:], C0G[:], op=ALU.mult),
          inc=(s_v, 1), waits=[(s_fx, ld)])
    bd.do("vector", lambda E_: E_.tensor_tensor(C_sb[:, 0:E], C_sb[:, 0:E], ADDT[:, 0:E], op=ALU.add),
          inc=(s_v, 1), waits=[(s_v, None)])
    bd.do("vector", lambda E_: E_.tensor_scalar(C_sb[:, 0:E], C_sb[:, 0:E], 5120.0, None, op0=ALU.min),
          inc=(s_v, 1), waits=[(s_v, None)])
    bd.do("tensor", lambda E_: E_.transpose(PC[0:E, 128:256], C_sb[:, 0:E], IDF[:]), inc=(s_t, 1),
          waits=[(s_v, None)])
    bd.do("vector", lambda E_: E_.tensor_copy(CT_sb[0:E, 0:128], PC[0:E, 128:256]), inc=(s_v, 1),
          waits=[(s_t, None)])



def _attn_sample(bd, layer, b, ffn_cfg, nc, sems, shared):
    NH = 16
    (PA, PB, PC, PD, IDB, IDF, ONESRF, ONESRB, SHIFTC, RWsb, RBB,
     wqkv, WPRJ, ROPEC, ROPES, ETsb, BPRJ, xn_dram, xnlo_dram, s_dram,
     qkt_dram, zdram, pprime, rope, band,
     s_w, s_x, s_t, s_v, s_g, s_st, s_bd, s_qp, s_c, w_ready, x_ready) = shared
    with ExitStack() as bctx:
        XNT = bctx.enter_context(nc.sbuf_tensor(f"at_xnt{layer}{b}", [128, 8 * S], BF16))
        for c in range(8):
            bd.do("sync", lambda E_, b=b, c=c: E_.dma_start(out=XNT[:, c * S:(c + 1) * S],
                  in_=xn_dram[b * S:(b + 1) * S, c * 128:(c + 1) * 128], transpose=True),
                  inc=(s_x, 16), waits=([(s_t, None), (s_v, None), (s_g, None)] if c == 0 else []))
        with ExitStack() as rctx:
            XNTL = rctx.enter_context(nc.sbuf_tensor(f"at_xntl{layer}{b}", [128, 8 * S], BF16))
            for c in range(8):
                bd.do("sync", lambda E_, b=b, c=c: E_.dma_start(out=XNTL[:, c * S:(c + 1) * S],
                      in_=xnlo_dram[b * S:(b + 1) * S, c * 128:(c + 1) * 128], transpose=True),
                      inc=(s_x, 16))
            emit_router(bd, layer, b, ffn_cfg, XNT, XNTL, sems)
            bd.barrier(f"rt{layer}{b}")
        VAUG = bctx.enter_context(nc.sbuf_tensor(f"at_vaug{layer}{b}", [128, NTB * NH * 65], BF16))
        with ExitStack() as pctx:
            WQKV = pctx.enter_context(nc.sbuf_tensor(f"at_wqkv{layer}{b}", [128, 8 * 3 * D], BF16))
            QRS = pctx.enter_context(nc.sbuf_tensor(f"at_qrs{layer}{b}", [128, 2 * 2048], BF16))
            RTMP = pctx.enter_context(nc.sbuf_tensor(f"at_rtmp{layer}{b}", [128, 2048], F32))
            PDS = pctx.enter_context(nc.sbuf_tensor(f"at_pds{layer}{b}", [128, 2 * 1024], BF16))
            bd.do("gpsimd", lambda E_: E_.dma_start(out=WQKV[:].rearrange("p (c n) -> p c n", c=8),
                  in_=wqkv.rearrange("(c p) n -> p c n", p=128)), inc=(s_w, 16),
                  waits=[(s_t, None), (s_v, None), (s_g, None)])
            w_ready2 = s_w.n
            bd.do("vector", lambda E_: E_.memset(VAUG[:], 1.0), inc=(s_v, 1))
            vaug_ones = s_v.n
            for tt in range(NTB):
                tw = [(s_w, w_ready2), (s_c, x_ready), (s_x, None)]
                if tt > 0:
                    tw += [(s_v, None), (s_g, None)]
                first = True
                for nb in range(4):
                    for dc in range(8):
                        bd.do("tensor", lambda E_, tt=tt, nb=nb, dc=dc: E_.matmul(
                              PA[:, nb * 512:(nb + 1) * 512],
                              XNT[:, dc * S + tt * 128: dc * S + tt * 128 + 128],
                              WQKV[:, dc * 3 * D + nb * 512: dc * 3 * D + nb * 512 + 512],
                              start=(dc == 0), stop=(dc == 7)), inc=(s_t, 1), waits=(tw if first else []))
                        first = False
                for nb in range(2):
                    for dc in range(8):
                        bd.do("tensor", lambda E_, tt=tt, nb=nb, dc=dc: E_.matmul(
                              PB[:, nb * 512:(nb + 1) * 512],
                              XNT[:, dc * S + tt * 128: dc * S + tt * 128 + 128],
                              WQKV[:, dc * 3 * D + 2048 + nb * 512: dc * 3 * D + 2048 + nb * 512 + 512],
                              start=(dc == 0), stop=(dc == 7)), inc=(s_t, 1))
                mm_done = s_t.n
                qb = (tt % 2) * 2048
                if rope:
                    rc = tt * 64
                    for half, srcoff in ((0, 0), (1, 1024)):
                        bd.do("vector", lambda E_, srcoff=srcoff, rc=rc: E_.tensor_tensor(
                              RTMP[:, 0:1024].rearrange("p (h x) -> p h x", x=64),
                              PA[:, srcoff:srcoff + 1024].rearrange("p (h x) -> p h x", x=64),
                              ROPEC[:, rc:rc + 64].unsqueeze(1).to_broadcast([128, 16, 64]), op=ALU.mult),
                              inc=(s_v, 1), waits=([(s_t, mm_done), (s_v, None)] if half == 0 else [(s_v, None)]))
                        bd.do("vector", lambda E_, srcoff=srcoff: E_.tensor_copy(
                              RTMP[:, 1024:2048].rearrange("p (h x) -> p h x", x=64)[:, :, 0:32],
                              PA[:, srcoff:srcoff + 1024].rearrange("p (h x) -> p h x", x=64)[:, :, 32:64]),
                              inc=(s_v, 1), waits=[(s_v, None)])
                        bd.do("vector", lambda E_, srcoff=srcoff: E_.tensor_copy(
                              RTMP[:, 1024:2048].rearrange("p (h x) -> p h x", x=64)[:, :, 32:64],
                              PA[:, srcoff:srcoff + 1024].rearrange("p (h x) -> p h x", x=64)[:, :, 0:32]),
                              inc=(s_v, 1), waits=[(s_v, None)])
                        bd.do("vector", lambda E_, rc=rc: E_.tensor_tensor(
                              RTMP[:, 1024:2048].rearrange("p (h x) -> p h x", x=64),
                              RTMP[:, 1024:2048].rearrange("p (h x) -> p h x", x=64),
                              ROPES[:, rc:rc + 64].unsqueeze(1).to_broadcast([128, 16, 64]), op=ALU.mult),
                              inc=(s_v, 1), waits=[(s_v, None)])
                        bd.do("vector", lambda E_, qb=qb, half=half: E_.tensor_tensor(
                              QRS[:, qb + half * 1024: qb + half * 1024 + 1024],
                              RTMP[:, 0:1024], RTMP[:, 1024:2048], op=ALU.add),
                              inc=(s_v, 1), waits=[(s_v, None)])
                else:
                    bd.do("vector", lambda E_, qb=qb: E_.tensor_copy(QRS[:, qb:qb + 2048], PA[:]),
                          inc=(s_v, 1), waits=[(s_t, mm_done), (s_v, None)])
                qk_ready = s_v.n
                bd.do("scalar", lambda E_, tt=tt: E_.copy(
                      VAUG[:, tt * 1040: tt * 1040 + 1040].rearrange("p (h c) -> p h c", c=65)[:, :, 0:64],
                      PB[:].rearrange("p (h c) -> p h c", c=64)),
                      inc=(s_g, 1), waits=[(s_t, mm_done), (s_v, vaug_ones)])
                for r in range(2):
                    pbf = (tt * 2 + r) % 2
                    s_qs = bd.sem(f"atqs{pbf}")
                    for c in range(8):
                        cg = r * 8 + c
                        tw2 = []
                        if c == 0:
                            tw2 = [(s_v, qk_ready), (s_g, None)]
                        bd.do("tensor", lambda E_, qb=qb, cg=cg, c=c: E_.transpose(
                              PD[:, c * 128:(c + 1) * 128], QRS[:, qb + cg * 128: qb + cg * 128 + 128], IDB[:]),
                              inc=(s_t, 1), waits=tw2)
                    bd.do("scalar", lambda E_, pbf=pbf: E_.copy(PDS[:, pbf * 1024:(pbf + 1) * 1024], PD[:]),
                          inc=(s_g, 1), waits=[(s_t, None), (s_qs, None)])
                    bd.do("sync", lambda E_, tt=tt, r=r, pbf=pbf, s_qs=s_qs: E_.dma_start(
                          out=qkt_dram.rearrange("c p t -> p c t")[:, r * 8:(r + 1) * 8, tt * 128:(tt + 1) * 128],
                          in_=PDS[:, pbf * 1024:(pbf + 1) * 1024].rearrange("p (c m) -> p c m", c=8)),
                          inc=(s_qs, 16), waits=[(s_g, None)])
            bd.wait("sync", bd.sem("atqs0"))
            bd.wait("sync", bd.sem("atqs1"))
        bd.barrier(f"prj{layer}{b}")
        emit_scores_l01(bd, layer, b, ffn_cfg, VAUG, WPRJ, (BPRJ if band else None),
                        (ETsb if band else None), (pprime[b % 2] if band else None),
                        qkt_dram, zdram, s_dram, sems, s_bd, s_qp)


def emit_attn_l01(bd: Build, layer, ffn_cfg):
    nc = bd.nc
    rope = (layer == 0)
    band = (layer == 1)
    NH = 16
    E = ffn_cfg["E"]
    xn_dram = bd.dram["xn_dram"]
    xnlo_dram = bd.dram["xnlo_dram"]
    s_dram = bd.dscratch("s_dram", (T, D), F32)
    qkt_dram = bd.dscratch("qkt_dram", (16, 128, S), BF16)
    zdram = bd.dscratch("z_bounce", (16, S), F32)
    wqkv = bd.din(f"l{layer}_wqkv", (D, 3 * D))
    wproj = bd.din(f"l{layer}_wproj", (D, D))
    rw = bd.din(f"l{layer}_rw", (D, 16))
    rbias = bd.din(f"l{layer}_rb", (1, 16))
    if band:
        et = bd.din("l1_etdup", (128, 2047))
        bproj_d = bd.din("l1_bproj", (1, D))
        pprime = [bd.dscratch(f"pprime{i}", (16, S, 2688), BF16) for i in range(2)]

    PA, PB, PC, PD = bd.sb["PA"], bd.sb["PB"], bd.sb["PC"], bd.sb["PD"]
    IDB, IDF = bd.sb["IDB"], bd.sb["IDF"]
    ONESRF, ONESRB = bd.sb["ONESRF"], bd.sb["ONESRB"]
    SHIFTC = bd.sb["SHIFTC"]
    RWsb, RBB = bd.sb["RWsb"], bd.sb["RBB"]

    s_w = bd.sem("atw")
    s_x = bd.sem("atx")
    s_t = bd.sem("att")
    s_v = bd.sem("atv")
    s_g = bd.sem("atg")
    s_st = bd.sem("atst")
    s_bd = bd.sem("atbd")
    s_qp = bd.sem("atqp")
    sems = {"t": s_t, "v": s_v, "g": s_g, "x": s_x, "st": s_st, "w": s_w}

    with ExitStack() as lctx:
        WPRJ = lctx.enter_context(nc.sbuf_tensor(f"at_wprj{layer}", [128, 8 * D], BF16))
        if rope:
            ROPEC = lctx.enter_context(nc.sbuf_tensor("at_ropec", [128, 16 * 64], F32))
            ROPES = lctx.enter_context(nc.sbuf_tensor("at_ropes", [128, 16 * 64], F32))
        if band:
            ETsb = lctx.enter_context(nc.sbuf_tensor("at_et", [128, 2047], BF16))
            BPRJ = lctx.enter_context(nc.sbuf_tensor("at_bprj", [1, D], BF16))

        bd.do("gpsimd", lambda E_: E_.dma_start(out=WPRJ[:].rearrange("p (c n) -> p c n", c=8),
              in_=wproj.rearrange("(c p) n -> p c n", p=128)), inc=(s_w, 16))
        s_c = bd.sem("atc")
        RWF = bd.sb["RWF"]
        RWLO = bd.sb["RWLO"]
        bd.do("sync", lambda E_: E_.dma_start(out=bd.sb["RBROW"][:], in_=rbias), inc=(s_c, 16))
        if rope:
            bd.do("sync", lambda E_: E_.dma_start(out=ROPEC[:], in_=bd.din("ropec_r", (128, 1024))), inc=(s_c, 16))
            bd.do("sync", lambda E_: E_.dma_start(out=ROPES[:], in_=bd.din("ropes_r", (128, 1024))), inc=(s_c, 16))
        bd.do("sync", lambda E_: E_.dma_start(out=RWF[:].rearrange("p (c n) -> p c n", c=8),
              in_=rw.rearrange("(c p) n -> p c n", p=128)), inc=(s_c, 16))
        bd.do("vector", lambda E_: E_.tensor_copy(RWsb[:], RWF[:]), inc=(s_v, 1), waits=[(s_c, None), (s_t, None), (s_g, None)])
        bd.do("vector", lambda E_: E_.tensor_tensor(RWLO[:], RWF[:], RWsb[:], op=ALU.subtract),
              inc=(s_v, 1), waits=[(s_v, None)])
        bd.do("tensor", lambda E_: E_.matmul(PC[:, 256:272], ONESRF[:], bd.sb["RBROW"][:], start=True, stop=True),
              inc=(s_t, 1), waits=[(s_c, None), (s_v, None)])
        bd.do("vector", lambda E_: E_.tensor_copy(RBB[:], PC[:, 256:272]), inc=(s_v, 1), waits=[(s_t, None)])
        if band:
            bd.do("gpsimd", lambda E_: E_.dma_start(out=ETsb[:], in_=et), inc=(s_w, 16))
            bd.do("gpsimd", lambda E_: E_.dma_start(out=BPRJ[:], in_=bproj_d), inc=(s_w, 16))
        if layer == 0:
            bd.do("vector", lambda E_: E_.memset(bd.sb["SSUM"][:], 0.0), inc=(s_v, 1))
        w_ready, x_ready = s_w.n, s_c.n

        for b in range(BPC):
            shared = (PA, PB, PC, PD, IDB, IDF, ONESRF, ONESRB, SHIFTC, RWsb, RBB,
                      wqkv, WPRJ, (ROPEC if rope else None), (ROPES if rope else None),
                      (ETsb if band else None), (BPRJ if band else None), xn_dram, xnlo_dram, s_dram,
                      qkt_dram, zdram, (pprime if band else None), rope, band,
                      s_w, s_x, s_t, s_v, s_g, s_st, s_bd, s_qp, s_c, w_ready, x_ready)
            _attn_sample(bd, layer, b, ffn_cfg, nc, sems, shared)
            bd.barrier(f"atb{layer}{b}")
        if layer == 0:
            emit_l0_scatter_fix(bd, E, sems)
    bd.barrier(f"attn{layer}")


def emit_scores_l01(bd, layer, b, ffn_cfg, VAUG, WPRJ, BPRJ, ETsb, pp_dram, qkt_dram, zdram, s_dram, sems, s_bd, s_qp):
    nc = bd.nc
    band = (layer == 1)
    NH = 16
    PA, PB, PC, PD = bd.sb["PA"], bd.sb["PB"], bd.sb["PC"], bd.sb["PD"]
    IDB = bd.sb["IDB"]
    ONESRF, ONESRB, ONESCB = bd.sb["ONESRF"], bd.sb["ONESRB"], bd.sb["ONESCB"]
    SHIFTC = bd.sb["SHIFTC"]
    ZSEL = bd.sb["ZSEL"]
    s_t, s_v, s_g, s_x, s_st = sems["t"], sems["v"], sems["g"], sems["x"], sems["st"]
    scale = 1.0 / 8.0

    with ExitStack() as sctx:
        QP = sctx.enter_context(nc.sbuf_tensor(f"sc_qp{layer}{b}", [128, 2 * 2 * S], BF16))
        CTX = sctx.enter_context(nc.sbuf_tensor(f"sc_ctx{layer}{b}", [128, 8 * 512], BF16))
        EXPS = sctx.enter_context(nc.sbuf_tensor(f"sc_exps{layer}{b}", [128, 2 * 512], BF16))
        ZST = sctx.enter_context(nc.sbuf_tensor(f"sc_zst{layer}{b}", [128, 512], F32))
        ZTQ = sctx.enter_context(nc.sbuf_tensor(f"sc_ztq{layer}{b}", [16, 2 * 512], F32))
        SOUT = sctx.enter_context(nc.sbuf_tensor(f"sc_sout{layer}{b}", [128, 2 * D], F32))
        if band:
            BANDT = sctx.enter_context(nc.sbuf_tensor(f"sc_bandt{b}", [128, 4 * 512], BF16))
            PPS = sctx.enter_context(nc.sbuf_tensor(f"sc_pps{b}", [128, 2 * 512], BF16))
            BCOL = sctx.enter_context(nc.sbuf_tensor(f"sc_bcol{b}", [128, 2], F32))
            ZPAD = sctx.enter_context(nc.sbuf_tensor(f"sc_zpad{b}", [128, 642], BF16))

        qp_last_use = [0, 0]

        def load_pair(hp, buf):
            bd.do("sync", lambda E_, hp=hp, buf=buf: E_.dma_start(
                  out=QP[:, buf * 2 * S: buf * 2 * S + S], in_=qkt_dram[hp]),
                  inc=(s_qp, 16), waits=[(s_t, qp_last_use[buf])])
            bd.do("sync", lambda E_, hp=hp, buf=buf: E_.dma_start(
                  out=QP[:, buf * 2 * S + S: (buf + 1) * 2 * S], in_=qkt_dram[8 + hp]),
                  inc=(s_qp, 16))
            return s_qp.n

        band_ready = 0
        if band:
            bd.do("vector", lambda E_: E_.memset(ZPAD[:], 0.0), inc=(s_v, 1))
            zp = s_v.n
            for hz in range(16):
                for rg in range(16):
                    bd.do("sync", lambda E_, hz=hz, rg=rg: E_.dma_start(
                          out=pp_dram[hz, rg * 128:(rg + 1) * 128, 2046:2688], in_=ZPAD[:]),
                          inc=(s_bd, 16), waits=([(s_v, zp)] if (rg == 0 and hz == 0) else []))
            for hp in range(8):
                buf = hp % 2
                qpr = load_pair(hp, buf)
                for hh in range(2):
                    base = hh * 64
                    for qt in range(NTB):
                        mcs = [mc for mc in range(4) if (mc + 1) * 512 > 1920 - qt * 128]
                        first = True
                        for mc in mcs:
                            n = 512 if mc < 3 else 511
                            w_ = [(s_qp, qpr), (s_v, None)] if first else []
                            bd.do("tensor", lambda E_, base=base, buf=buf, qt=qt, mc=mc, n=n: E_.matmul(
                                  PA[:, mc * 512: mc * 512 + n],
                                  QP[base:base + 64, buf * 2 * S + qt * 128: buf * 2 * S + qt * 128 + 128],
                                  ETsb[base:base + 64, mc * 512: mc * 512 + n],
                                  start=True, stop=True), inc=(s_t, 1), waits=w_)
                            first = False
                        mmd = s_t.n
                        bbf = qt % 2
                        bd.do("vector", lambda E_, bbf=bbf: E_.tensor_copy(BCOL[:, bbf:bbf + 1], PA[:, 2046:2047]),
                              inc=(s_v, 1), waits=[(s_t, mmd)])
                        s_pb = bd.sem(f"atpp{bbf}")
                        for mc in mcs:
                            n = 512 if mc < 3 else 510
                            bd.do("vector", lambda E_, mc=mc, n=n, bbf=bbf: E_.tensor_scalar(
                                  PPS[:, bbf * 512: bbf * 512 + n], PA[:, mc * 512: mc * 512 + n],
                                  BCOL[:, bbf:bbf + 1], None, op0=ALU.subtract), inc=(s_v, 1),
                                  waits=[(s_v, None), (s_pb, None)])
                            bd.do("sync", lambda E_, h=hp * 2 + hh, qt=qt, mc=mc, n=n, bbf=bbf, s_pb=s_pb: E_.dma_start(
                                  out=pp_dram[h, qt * 128:(qt + 1) * 128, mc * 512: mc * 512 + n],
                                  in_=PPS[:, bbf * 512: bbf * 512 + n]), inc=(s_pb, 16), waits=[(s_v, None)])
                    qp_last_use[buf] = s_t.n
            bd.wait("sync", bd.sem("atpp0"))
            bd.wait("sync", bd.sem("atpp1"))
            bd.wait("sync", s_bd)
            band_ready = 0

        bandbuf_last = [0, 0, 0, 0]
        for qg in range(4):
            q0 = qg * 512
            for hp in range(8):
                buf = hp % 2
                qpr = load_pair(hp, buf)
                for hh in range(2):
                    h = hp * 2 + hh
                    base = hh * 64
                    pbreg = hh * 512
                    odd = (hh == 1)
                    kclim = (q0 + 510) // 128 if band else -1
                    exp_after = {}
                    smm_after = {}

                    def emit_smm(kc, h=h, base=base, buf=buf, q0=q0, kclim=kclim):
                        ps = 1024 + (kc % 2) * 512
                        w_ = [(s_qp, qpr)]
                        if kc >= 2:
                            w_.append((s_g, exp_after[kc - 2]))
                        hasband = band and kc <= kclim
                        bd.do("tensor", lambda E_, kc=kc, ps=ps: E_.matmul(
                              PA[:, ps:ps + 512],
                              QP[base:base + 64, buf * 2 * S + S + kc * 128: buf * 2 * S + S + kc * 128 + 128],
                              QP[base:base + 64, buf * 2 * S + q0: buf * 2 * S + q0 + 512],
                              start=True, stop=(not hasband)), inc=(s_t, 1), waits=w_)
                        if hasband:
                            bbuf = kc % 4
                            s_bb = bd.sem(f"atbb{bbuf}")
                            bd.do("sync", lambda E_, kc=kc, bbuf=bbuf, s_bb=s_bb: E_.dma_start(
                                  out=BANDT[:, bbuf * 512:(bbuf + 1) * 512],
                                  in_=bass.AP(pp_dram.tensor, h * S * 2688 + q0 * 2687 + kc * 128 + 2047, [[2687, 512], [1, 128]]),
                                  transpose=True), inc=(s_bb, 16),
                                  waits=[(s_t, bandbuf_last[bbuf])])
                            bdr = s_bb.n
                            bd.do("tensor", lambda E_, kc=kc, ps=ps, bbuf=bbuf, s_bb=s_bb: E_.matmul(
                                  PA[:, ps:ps + 512], IDB[:], BANDT[:, bbuf * 512:(bbuf + 1) * 512],
                                  start=False, stop=True), inc=(s_t, 1), waits=[(s_bb, bdr)])
                            bandbuf_last[bbuf] = s_t.n
                        smm_after[kc] = s_t.n

                    def emit_exp(kc):
                        ps = 1024 + (kc % 2) * 512
                        eb = (kc % 2) * 512
                        bd.do("scalar", lambda E_, ps=ps, eb=eb: E_.activation(
                              EXPS[:, eb:eb + 512], PA[:, ps:ps + 512], AF.Exp,
                              bias=SHIFTC[:, 0:1], scale=scale), inc=(s_g, 1), waits=[(s_t, smm_after[kc])])
                        exp_after[kc] = s_g.n

                    def emit_pv(kc, h=h, odd=odd, pbreg=pbreg):
                        eb = (kc % 2) * 512
                        w_ = [(s_g, exp_after[kc])]
                        if kc == 0:
                            w_ += [(s_g, None), (s_v, None)]
                        if not odd:
                            bd.do("tensor", lambda E_, kc=kc, eb=eb: E_.matmul(
                                  PB[0:65, pbreg:pbreg + 512], VAUG[:, kc * 1040 + h * 65: kc * 1040 + h * 65 + 65],
                                  EXPS[:, eb:eb + 512], start=(kc == 0), stop=(kc == 15)), inc=(s_t, 1), waits=w_)
                        else:
                            bd.do("tensor", lambda E_, kc=kc, eb=eb: E_.matmul(
                                  PB[64:128, pbreg:pbreg + 512], VAUG[:, kc * 1040 + h * 65: kc * 1040 + h * 65 + 64],
                                  EXPS[:, eb:eb + 512], start=(kc == 0), stop=(kc == 15)), inc=(s_t, 1), waits=w_)
                            bd.do("tensor", lambda E_, kc=kc, eb=eb: E_.matmul(
                                  PC[0:1, 0:512], ONESCB[:], EXPS[:, eb:eb + 512],
                                  start=(kc == 0), stop=(kc == 15)), inc=(s_t, 1))

                    emit_smm(0)
                    for kc in range(16):
                        if kc < 15:
                            emit_smm(kc + 1)
                        emit_exp(kc)
                        emit_pv(kc)
                    pv_done = s_t.n
                    s_zw = bd.sem(f"atzw{h % 2}")
                    if not odd:
                        bd.do("scalar", lambda E_, pbreg=pbreg: E_.copy(ZST[64:65, 0:512],
                              PB[64:65, pbreg:pbreg + 512]), inc=(s_g, 1), waits=[(s_t, pv_done), (s_zw, None)])
                        bd.do("sync", lambda E_, h=h, q0=q0, s_zw=s_zw: E_.dma_start(
                              out=zdram[h:h + 1, q0:q0 + 512], in_=ZST[64:65, 0:512]),
                              inc=(s_zw, 16), waits=[(s_g, None)])
                        bd.do("scalar", lambda E_, h=h, pbreg=pbreg: E_.copy(
                              CTX[0:64, (h // 2) * 512:(h // 2) * 512 + 512],
                              PB[0:64, pbreg:pbreg + 512]), inc=(s_g, 1), waits=[(s_g, None)])
                    else:
                        bd.do("scalar", lambda E_: E_.copy(ZST[0:1, 0:512], PC[0:1, 0:512]),
                              inc=(s_g, 1), waits=[(s_t, pv_done), (s_zw, None)])
                        bd.do("sync", lambda E_, h=h, q0=q0, s_zw=s_zw: E_.dma_start(
                              out=zdram[h:h + 1, q0:q0 + 512], in_=ZST[0:1, 0:512]),
                              inc=(s_zw, 16), waits=[(s_g, None)])
                        bd.do("scalar", lambda E_, h=h, pbreg=pbreg: E_.copy(
                              CTX[64:128, (h // 2) * 512:(h // 2) * 512 + 512],
                              PB[64:128, pbreg:pbreg + 512]), inc=(s_g, 1), waits=[(s_g, None)])
                qp_last_use[buf] = s_t.n
            allctx = s_g.n
            zq = (qg % 2) * 512
            bd.wait("sync", bd.sem("atzw0"))
            bd.wait("sync", bd.sem("atzw1"))
            s_zl = bd.sem("atzl")
            bd.do("sync", lambda E_, qg=qg, zq=zq: E_.dma_start(out=ZTQ[:, zq:zq + 512],
                  in_=zdram[:, qg * 512:(qg + 1) * 512]), inc=(s_zl, 16), waits=[(s_v, None)])
            zl = s_zl.n
            bd.do("vector", lambda E_, zq=zq: E_.reciprocal(ZTQ[:, zq:zq + 512], ZTQ[:, zq:zq + 512]),
                  inc=(s_v, 1), waits=[(s_zl, zl)])
            zi = s_v.n
            for hp in range(8):
                bd.do("tensor", lambda E_, hp=hp, zq=zq: E_.matmul(
                      PC[:, 0:512], ZSEL[0:16, hp * 128:(hp + 1) * 128], ZTQ[0:16, zq:zq + 512],
                      start=True, stop=True), inc=(s_t, 1),
                      waits=[(s_v, zi), (s_g, allctx), (s_v, None)])
                zb = s_t.n
                bd.do("vector", lambda E_, hp=hp: E_.scalar_tensor_tensor(
                      CTX[:, hp * 512:(hp + 1) * 512], CTX[:, hp * 512:(hp + 1) * 512], 1.0,
                      PC[:, 0:512], op0=ALU.mult, op1=ALU.mult), inc=(s_v, 1), waits=[(s_t, zb)])
            ctxn = s_v.n
            for sl in range(4):
                PRJ = PB if sl % 2 == 0 else PA
                first = True
                for c in range(8):
                    for nb in range(2):
                        w_ = []
                        if first:
                            w_ = [(s_v, ctxn)]
                            if sl >= 2:
                                w_.append((s_v, None))
                        bd.do("tensor", lambda E_, sl=sl, c=c, nb=nb, PRJ=PRJ: E_.matmul(
                              PRJ[:, nb * 512:(nb + 1) * 512], CTX[:, c * 512 + sl * 128: c * 512 + sl * 128 + 128],
                              WPRJ[:, c * D + nb * 512: c * D + nb * 512 + 512],
                              start=(c == 0), stop=(c == 7 and not band)), inc=(s_t, 1), waits=w_)
                        first = False
                if band:
                    for nb in range(2):
                        bd.do("tensor", lambda E_, nb=nb, PRJ=PRJ: E_.matmul(
                              PRJ[:, nb * 512:(nb + 1) * 512], ONESRB[:], BPRJ[:, nb * 512:(nb + 1) * 512],
                              start=False, stop=True), inc=(s_t, 1))
                pj = s_t.n
                sbf = sl % 2
                s_so = bd.sem(f"atso{sbf}")
                bd.do("vector", lambda E_, sbf=sbf, PRJ=PRJ: E_.tensor_copy(SOUT[:, sbf * D:(sbf + 1) * D], PRJ[:, 0:D]),
                      inc=(s_v, 1), waits=[(s_t, pj), (s_so, None)])
                bd.do("sync", lambda E_, b=b, qg=qg, sl=sl, sbf=sbf, s_so=s_so: E_.dma_start(
                      out=s_dram[b * S + qg * 512 + sl * 128: b * S + qg * 512 + sl * 128 + 128, :],
                      in_=SOUT[:, sbf * D:(sbf + 1) * D]), inc=(s_so, 16), waits=[(s_v, None)])
        bd.wait("sync", bd.sem("atso0"))
        bd.wait("sync", bd.sem("atso1"))
        bd.barrier(f"sc{layer}{b}")




FFN_CFGS = {
    0: dict(E=8, H=1024, topk=2, w1="l0_w1", b1="l0_b1", w2="l0_w2", b2="l0_b2"),
    1: dict(E=16, H=2048, topk=4, w1="l1_w1", b1="l1_b1", w2="l1_w2", b2="l1_b2"),
    2: dict(E=8, H=2048, topk=1, w1="l2_w1", b1=None, w2="l2_w2", b2=None,
            l2_mode=True, gamma="l2_gamma", beta="l2_beta", G=512),
}


def build_program(layers):
    nc = bass.Bass("TRN2", target_bir_lowering=False)
    ctx = ExitStack()
    with ctx:
        bd = Build(nc, ctx)
        bd.psum("PA", (128, 2048), F32)
        bd.psum("PB", (128, 1024), F32)
        bd.psum("PC", (128, 512), F32)
        bd.psum("PD", (128, 1024), BF16)
        bd.sbuf("IDB", (128, 128), BF16)
        bd.sbuf("IDF", (128, 128), F32)
        bd.sbuf("ONESRF", (1, 128), F32)
        bd.sbuf("ONESRB", (1, 128), BF16)
        bd.sbuf("ONESCB", (128, 1), BF16)
        bd.sbuf("SHIFTC", (128, 1), F32)
        bd.sbuf("RWsb", (128, 8 * 16), BF16)
        bd.sbuf("RWLO", (128, 8 * 16), BF16)
        bd.sbuf("RWF", (128, 8 * 16), F32)
        bd.sbuf("RBB", (128, 16), F32)
        bd.sbuf("RBROW", (1, 16), F32)
        bd.sbuf("ZSEL", (16, 8 * 128), F32)
        bd.sbuf("C_sb", (128, NT * 16), F32)
        bd.sbuf("CT_sb", (16, T), F32)
        bd.sbuf("TV_sb", (128, NT), F32)
        bd.sbuf("SSUM", (8, 4), F32)
        bd.sbuf("C0G", (128, 8), F32)
        bd.sbuf("ADDT", (128, 8), F32)
        bd.sbuf("RL", (128, 2 * 16), F32)
        bd.sbuf("RP", (128, 2 * 16), F32)
        bd.sbuf("RT8", (128, 2 * 8), F32)
        bd.sbuf("RC", (128, 2 * 16), F32)
        bd.sbuf("RS", (128, 8), F32)
        bd.sbuf("ROH", (128, 2 * 16), F32)
        bd.sbuf("RTP", (128, 4), F32)
        bd.sbuf("BEAT", (1, 384), BF16)
        bd.sbuf("ONESCF", (128, 1), F32)
        bd.sbuf("UTRF", (128, 128), F32)

        bd.dscratch("beat_dram", (1, 128), BF16)
        s_k = bd.sem("konst")
        for nm, src in (("IDB", "identb"), ("IDF", "identf"), ("ONESRF", "onesrf"),
                        ("ONESRB", "onesrb"), ("ONESCB", "onescb"), ("ZSEL", "zsel"),
                        ("C0G", "c0g"), ("ONESCF", "onescf"), ("UTRF", "utrif")):
            shp = tuple(bd.sb[nm].shape)
            dt = bd.sb[nm].dtype
            d = bd.din("c_" + src, shp, dt)
            bd.do("sync", lambda E_, nm=nm, d=d: E_.dma_start(out=bd.sb[nm][:], in_=d), inc=(s_k, 16))
        bd.do("vector", lambda E_: E_.memset(bd.sb["SHIFTC"][:], -12.0), inc=(bd.sem("ffv"), 1))
        for e in ENGINES:
            bd.wait(e, s_k)
            bd.wait(e, bd.sem("ffv"))

        x_in = bd.din("x_in", (T, D), F32)
        x_out = bd.dout("x_out", (T, D), F32)
        xb = [bd.dscratch("xbuf0", (T, D), F32), bd.dscratch("xbuf1", (T, D), F32)]
        bd.dscratch("xnlo_dram", (T, D), BF16)

        import os as _os
        kphase = _os.environ.get("KPHASE", "")
        cur = x_in
        for i, l in enumerate(layers):
            dst = x_out if i == len(layers) - 1 else xb[i % 2]
            emit_ln(bd, l, cur)
            if kphase == "ln":
                s_fin = bd.sem("fin")
                bd.do("gpsimd", lambda E_: E_.dma_start(out=x_out, in_=bd.dram["xn_dram"]), inc=(s_fin, 16))
                bd.wait("gpsimd", s_fin)
                break
            if l in (0, 1):
                emit_attn_l01(bd, l, FFN_CFGS[l])
                if kphase == "attn":
                    s_fin = bd.sem("fin")
                    bd.do("sync", lambda E_: E_.dma_start(out=x_out, in_=bd.dram["s_dram"]), inc=(s_fin, 16))
                    bd.wait("sync", s_fin)
                    break
                emit_ffn(bd, l, FFN_CFGS[l], cur, dst)
            elif l == 2:
                emit_attn_l2(bd, FFN_CFGS[2])
                emit_ffn(bd, 2, FFN_CFGS[2], cur, dst)
            else:
                emit_attn_l3(bd)
                emit_ffn_l3(bd, cur, dst)
            cur = dst
        bd.wait("sync", bd.sem("ffst"))
        with nc.Block() as block:
            bd.emit(block)
    return nc, bd


def _permute_l0_wqkv(w):
    w = np.array(w)
    perm = np.concatenate([np.arange(0, 64, 2), np.arange(1, 64, 2)])
    out = w.copy()
    for h in range(16):
        out[:, h * 64:(h + 1) * 64] = w[:, h * 64 + perm]
        out[:, 1024 + h * 64:1024 + (h + 1) * 64] = w[:, 1024 + h * 64 + perm]
    return out


def _b1c(b1, H):
    E = b1.shape[0]
    return np.ascontiguousarray(b1.reshape(E, H // 128, 128).transpose(0, 2, 1)).astype(np.float32)


def make_host_inputs(ins):
    sh = {}
    c = CONSTS
    sh["c_identb"] = c["identb"]
    sh["c_identf"] = c["identf"]
    sh["c_onesrf"] = np.ones((1, 128), np.float32)
    sh["c_onesrb"] = np.ones((1, 128), BF)
    sh["c_onescb"] = np.ones((128, 1), BF)
    zsel = np.zeros((16, 8 * 128), np.float32)
    for hp in range(8):
        zsel[2 * hp, hp * 128: hp * 128 + 64] = 1.0
        zsel[2 * hp + 1, hp * 128 + 64: hp * 128 + 128] = 1.0
    sh["c_zsel"] = zsel
    sh["ropec_r"] = np.ascontiguousarray(
        c["ropeC"].reshape(16, 128, 64).transpose(1, 0, 2).reshape(128, 1024))
    sh["ropes_r"] = np.ascontiguousarray(
        c["ropeS"].reshape(16, 128, 64).transpose(1, 0, 2).reshape(128, 1024))
    sh["l0_wqkv"] = _permute_l0_wqkv(ins["L0_attn_w_qkv"])
    sh["l0_wproj"] = ins["L0_attn_w_proj"]
    rw = np.zeros((D, 16), np.float32); rw[:, :8] = ins["L0_ffn_router_w"]
    sh["l0_rw"] = rw
    rb = np.full((1, 16), -1e30, np.float32); rb[0, :8] = ins["L0_ffn_router_b"]
    rb[0, 8:] = 0.0
    sh["l0_rb"] = rb
    sh["l0_w1"] = ins["L0_ffn_w1"]
    sh["l0_b1_c"] = _b1c(ins["L0_ffn_b1"], 1024)
    sh["l0_w2"] = ins["L0_ffn_w2"]
    sh["l0_b2"] = ins["L0_ffn_b2"]
    if "L1_attn_W_qkv" in ins:
        sh["l1_wqkv"] = ins["L1_attn_W_qkv"]
        sh["l1_wproj"] = ins["L1_attn_W_proj"]
        sh["l1_bproj"] = ins["L1_attn_b_proj"].reshape(1, D)
        etd = np.zeros((128, 2047), np.float32)
        etd[0:64] = ins["L1_attn_rel_pos_emb"].T
        etd[64:128] = ins["L1_attn_rel_pos_emb"].T
        sh["l1_etdup"] = etd
        sh["l1_rw"] = ins["L1_ffn_router_w"]
        sh["l1_rb"] = ins["L1_ffn_router_b"].reshape(1, 16)
        sh["l1_w1"] = ins["L1_ffn_w1"]
        sh["l1_b1_c"] = _b1c(ins["L1_ffn_b1"], 2048)
        sh["l1_w2"] = ins["L1_ffn_w2"]
        sh["l1_b2"] = ins["L1_ffn_b2"]
    sh["c_onescf"] = np.ones((128, 1), np.float32)
    for off in (0, 128, 256, 384):
        sh[f"cm{off}"] = CONSTS[f"cm{off}"]
    if "L3_attn_w_qkv" in ins:
        sh["l3_wqkv"] = ins["L3_attn_w_qkv"]
        sh["l3_wo"] = ins["L3_attn_w_o"]
        sh["l3_fw"] = ins["L3_ffn_weight"]
        sh["l3_fb"] = ins["L3_ffn_bias"].reshape(1, D)
    sh["c_utrif"] = CONSTS["utrif"]
    if "L2_attn_Wq" in ins:
        sh["l2_wq"] = ins["L2_attn_Wq"]
        sh["l2_wk"] = ins["L2_attn_Wk"]
        sh["l2_wv"] = ins["L2_attn_Wv"]
        sh["l2_wg"] = ins["L2_attn_Wg"]
        sh["l2_gkw1"] = ins["L2_attn_gk_w1"]
        sh["l2_gkw2"] = ins["L2_attn_gk_w2"]
        sh["l2_gkb2"] = ins["L2_attn_gk_b2"].reshape(1, 512)
        sh["l2_wo"] = ins["L2_attn_Wo"]
        sh["l2_gnw"] = ins["L2_attn_g_norm_w"].reshape(1, 256)
        rw2 = np.zeros((D, 16), np.float32); rw2[:, :8] = np.asarray(ins["L2_ffn_gate_w"]).T
        sh["l2_rw"] = rw2
        sh["l2_rb"] = np.zeros((1, 16), np.float32)
        sh["l2_w1"] = ins["L2_ffn_expert_w1"]
        sh["l2_w2"] = ins["L2_ffn_expert_w2"]
        sh["l2_gamma"] = ins["L2_ffn_ln_gamma"].reshape(1, D)
        sh["l2_beta"] = ins["L2_ffn_ln_beta"].reshape(1, D)
    per_core = []
    x = np.asarray(ins["L0_attn_x"])
    for core in range(NCORES):
        m = {"x_in": np.ascontiguousarray(x[core * BPC:(core + 1) * BPC].reshape(T, D))}
        cg = np.zeros((128, 8), np.float32)
        if core == 0:
            cg[0:8, 0:2] = 1.0
        m["c_c0g"] = cg
        per_core.append(m)
    return sh, per_core


_LAYERS = (0, 1, 2, 3)


def kernel(**inputs):
    import os
    layers = tuple(int(x) for x in os.environ.get("KLAYERS", "0123"))
    nc, bd = build_program(layers)
    sh, per_core = make_host_inputs(inputs)
    needed = set(bd.inputs.keys())
    in_maps = []
    for core in range(NCORES):
        m = {}
        for k in needed:
            if k == "x_out":
                continue
            if k in per_core[core]:
                m[k] = per_core[core][k]
            elif k in sh:
                m[k] = sh[k]
            else:
                raise KeyError(k)
        in_maps.append(m)
    res = run_bass_kernel_spmd(nc, in_maps, core_ids=list(range(NCORES)))
    out = np.concatenate([r["x_out"].reshape(BPC, S, D) for r in res.results], axis=0)
    return out.astype(np.float32)



def emit_attn_l2(bd: Build, ffn_cfg):
    nc = bd.nc
    xn_dram = bd.dram["xn_dram"]
    xnlo_dram = bd.dram["xnlo_dram"]
    s_dram = bd.dscratch("s_dram", (T, D), F32)
    wq = bd.din("l2_wq", (D, 512))
    wk = bd.din("l2_wk", (D, 512))
    wv = bd.din("l2_wv", (D, D))
    wg = bd.din("l2_wg", (D, D))
    gkw1 = bd.din("l2_gkw1", (D, 16))
    gkw2 = bd.din("l2_gkw2", (16, 512))
    gkb2 = bd.din("l2_gkb2", (1, 512))
    wo = bd.din("l2_wo", (D, D))
    gnw = bd.din("l2_gnw", (1, 256))
    rw = bd.din("l2_rw", (D, 16))
    rbias = bd.din("l2_rb", (1, 16))
    qdr = bd.dscratch("l2_q", (S, 512), BF16)
    kdr = bd.dscratch("l2_k", (S, 512), BF16)
    vdr = bd.dscratch("l2_v", (S, D), BF16)
    gdr = bd.dscratch("l2_g", (S, D), BF16)
    adr = bd.dscratch("l2_a", (S, 512), F32)
    otdr = bd.dscratch("l2_ot", (D, S), BF16)

    PA, PB, PC, PD = bd.sb["PA"], bd.sb["PB"], bd.sb["PC"], bd.sb["PD"]
    IDB, IDF = bd.sb["IDB"], bd.sb["IDF"]
    ONESRF, ONESRB, ONESCB = bd.sb["ONESRF"], bd.sb["ONESRB"], bd.sb["ONESCB"]
    ONESCF = bd.sb["ONESCF"]
    UTRF = bd.sb["UTRF"]
    RWsb, RBB = bd.sb["RWsb"], bd.sb["RBB"]
    s_w = bd.sem("atw")
    s_x = bd.sem("atx")
    s_t = bd.sem("att")
    s_v = bd.sem("atv")
    s_g = bd.sem("atg")
    s_st = bd.sem("atst")
    s_c = bd.sem("atc")
    sems = {"t": s_t, "v": s_v, "g": s_g, "x": s_x, "st": s_st, "w": s_w}

    with ExitStack() as lctx:
        RWF, RWLO = bd.sb["RWF"], bd.sb["RWLO"]
        GKW2 = lctx.enter_context(nc.sbuf_tensor("l2_gkw2f", [16, 512], F32))
        GKB2 = lctx.enter_context(nc.sbuf_tensor("l2_gkb2f", [1, 512], F32))
        bd.sb["GKW2F"] = GKW2
        bd.sb["GKB2F"] = GKB2
        GNWB = lctx.enter_context(nc.sbuf_tensor("l2_gnwb", [128, 256], F32))
        GNWR = lctx.enter_context(nc.sbuf_tensor("l2_gnwr", [1, 256], F32))
        WO = lctx.enter_context(nc.sbuf_tensor("l2_wo_s", [128, 8 * D], BF16))
        bd.do("sync", lambda E_: E_.dma_start(out=bd.sb["RBROW"][:], in_=rbias), inc=(s_c, 16))
        bd.do("sync", lambda E_: E_.dma_start(out=RWF[:].rearrange("p (c n) -> p c n", c=8),
              in_=rw.rearrange("(c p) n -> p c n", p=128)), inc=(s_c, 16))
        bd.do("sync", lambda E_: E_.dma_start(out=GNWR[:], in_=gnw), inc=(s_c, 16))
        bd.do("sync", lambda E_: E_.dma_start(out=GKW2[:], in_=gkw2), inc=(s_c, 16))
        bd.do("sync", lambda E_: E_.dma_start(out=GKB2[:], in_=gkb2), inc=(s_c, 16))
        bd.do("gpsimd", lambda E_: E_.dma_start(out=WO[:].rearrange("p (c n) -> p c n", c=8),
              in_=wo.rearrange("(c p) n -> p c n", p=128)), inc=(s_w, 16))
        bd.do("vector", lambda E_: E_.tensor_copy(RWsb[:], RWF[:]), inc=(s_v, 1),
              waits=[(s_c, None), (s_t, None), (s_g, None)])
        bd.do("vector", lambda E_: E_.tensor_tensor(RWLO[:], RWF[:], RWsb[:], op=ALU.subtract),
              inc=(s_v, 1), waits=[(s_v, None)])
        bd.do("tensor", lambda E_: E_.matmul(PC[:, 256:272], ONESRF[:], bd.sb["RBROW"][:], start=True, stop=True),
              inc=(s_t, 1), waits=[(s_c, None), (s_v, None)])
        bd.do("vector", lambda E_: E_.tensor_copy(RBB[:], PC[:, 256:272]), inc=(s_v, 1), waits=[(s_t, None)])
        bd.do("tensor", lambda E_: E_.matmul(PC[:, 272:272 + 256] if False else PB[:, 0:256], ONESRF[:], GNWR[:], start=True, stop=True),
              inc=(s_t, 1), waits=[(s_c, None), (s_v, None)])
        bd.do("vector", lambda E_: E_.tensor_copy(GNWB[:], PB[:, 0:256]), inc=(s_v, 1), waits=[(s_t, None)])

        for b in range(BPC):
            _l2_sample(bd, b, ffn_cfg, nc, sems, (PA, PB, PC, PD, IDB, IDF, ONESRF, ONESRB,
                       ONESCB, ONESCF, UTRF, RWsb, RBB, GKW2, GKB2, GNWB, WO,
                       xn_dram, xnlo_dram, s_dram, wq, wk, wv, wg, gkw1,
                       qdr, kdr, vdr, gdr, adr, otdr))
            bd.barrier(f"l2b{b}")
        del bd.sb["GKW2F"], bd.sb["GKB2F"]
    bd.barrier("attn2")


def _l2_sample(bd, b, ffn_cfg, nc, sems, shared):
    (PA, PB, PC, PD, IDB, IDF, ONESRF, ONESRB, ONESCB, ONESCF, UTRF, RWsb, RBB,
     GKW2, GKB2, GNWB, WO, xn_dram, xnlo_dram, s_dram, wq, wk, wv, wg, gkw1,
     qdr, kdr, vdr, gdr, adr, otdr) = shared
    s_w, s_x, s_t, s_v, s_g, s_st = (sems["w"], sems["x"], sems["t"], sems["v"], sems["g"], sems["st"])
    QSC = 128.0 ** -0.5

    with ExitStack() as bctx:
        XNT = bctx.enter_context(nc.sbuf_tensor(f"l2_xnt{b}", [128, 8 * S], BF16))
        for c in range(8):
            bd.do("sync", lambda E_, b=b, c=c: E_.dma_start(out=XNT[:, c * S:(c + 1) * S],
                  in_=xn_dram[b * S:(b + 1) * S, c * 128:(c + 1) * 128], transpose=True),
                  inc=(s_x, 16), waits=([(s_t, None), (s_v, None), (s_g, None)] if c == 0 else []))
        with ExitStack() as rctx:
            XNTL = rctx.enter_context(nc.sbuf_tensor(f"l2_xntl{b}", [128, 8 * S], BF16))
            for c in range(8):
                bd.do("sync", lambda E_, b=b, c=c: E_.dma_start(out=XNTL[:, c * S:(c + 1) * S],
                      in_=xnlo_dram[b * S:(b + 1) * S, c * 128:(c + 1) * 128], transpose=True),
                      inc=(s_x, 16))
            emit_router(bd, 2, b, ffn_cfg, XNT, XNTL, sems)
            bd.barrier(f"l2rt{b}")
        with ExitStack() as pctx:
            WQK = pctx.enter_context(nc.sbuf_tensor(f"l2_wqk{b}", [128, 8 * 1024], BF16))
            WVG = pctx.enter_context(nc.sbuf_tensor(f"l2_wvg{b}", [128, 8 * 2048], BF16))
            GKW1 = pctx.enter_context(nc.sbuf_tensor(f"l2_gkw1{b}", [128, 8 * 16], BF16))
            EVQ = pctx.enter_context(nc.sbuf_tensor(f"l2_evq{b}", [128, 2 * 1024], BF16))
            EVV = pctx.enter_context(nc.sbuf_tensor(f"l2_evv{b}", [128, 2 * 2048], BF16))
            EVA = pctx.enter_context(nc.sbuf_tensor(f"l2_eva{b}", [128, 2 * 512], F32))
            EVZ = pctx.enter_context(nc.sbuf_tensor(f"l2_evz{b}", [16, 2 * 128], F32))
            ASC = pctx.enter_context(nc.sbuf_tensor(f"l2_asc{b}", [128, 2 * 512], F32))
            s_pw = bd.sem("l2pw")
            bd.do("gpsimd", lambda E_: E_.dma_start(out=WQK[:, 0:8 * 512].rearrange("p (c n) -> p c n", c=8),
                  in_=wq.rearrange("(c p) n -> p c n", p=128)), inc=(s_pw, 16),
                  waits=[(s_t, None), (s_v, None), (s_g, None)])
            bd.do("gpsimd", lambda E_: E_.dma_start(out=WQK[:, 8 * 512:8 * 1024].rearrange("p (c n) -> p c n", c=8),
                  in_=wk.rearrange("(c p) n -> p c n", p=128)), inc=(s_pw, 16))
            bd.do("gpsimd", lambda E_: E_.dma_start(out=WVG[:, 0:8 * 1024].rearrange("p (c n) -> p c n", c=8),
                  in_=wv.rearrange("(c p) n -> p c n", p=128)), inc=(s_pw, 16))
            bd.do("gpsimd", lambda E_: E_.dma_start(out=WVG[:, 8 * 1024:8 * 2048].rearrange("p (c n) -> p c n", c=8),
                  in_=wg.rearrange("(c p) n -> p c n", p=128)), inc=(s_pw, 16))
            bd.do("gpsimd", lambda E_: E_.dma_start(out=GKW1[:].rearrange("p (c n) -> p c n", c=8),
                  in_=gkw1.rearrange("(c p) n -> p c n", p=128)), inc=(s_pw, 16))
            pw = s_pw.n
            s_ps = [bd.sem("l2ps0"), bd.sem("l2ps1")]
            for tt in range(NTB):
                pf = tt % 2
                tw = [(s_pw, pw), (s_x, None)]
                if tt > 0:
                    tw += [(s_v, None), (s_g, None)]
                first = True
                for nb in range(2):
                    for dc in range(8):
                        bd.do("tensor", lambda E_, tt=tt, nb=nb, dc=dc: E_.matmul(
                              PA[:, nb * 512:(nb + 1) * 512], XNT[:, dc * S + tt * 128: dc * S + tt * 128 + 128],
                              WQK[:, dc * 512 + (nb * 8 * 512 if False else 0) + nb * 8 * 512 // 1: 0] if False else
                              WQK[:, nb * 8 * 512 + dc * 512: nb * 8 * 512 + (dc + 1) * 512],
                              start=(dc == 0), stop=(dc == 7)), inc=(s_t, 1), waits=(tw if first else []))
                        first = False
                for nb in range(2):
                    for dc in range(8):
                        bd.do("tensor", lambda E_, tt=tt, nb=nb, dc=dc: E_.matmul(
                              PB[:, nb * 512:(nb + 1) * 512], XNT[:, dc * S + tt * 128: dc * S + tt * 128 + 128],
                              WVG[:, dc * 1024 + nb * 512: dc * 1024 + (nb + 1) * 512],
                              start=(dc == 0), stop=(dc == 7)), inc=(s_t, 1))
                for nb in range(2):
                    for dc in range(8):
                        bd.do("tensor", lambda E_, tt=tt, nb=nb, dc=dc: E_.matmul(
                              PA[:, 1024 + nb * 512:1024 + (nb + 1) * 512],
                              XNT[:, dc * S + tt * 128: dc * S + tt * 128 + 128],
                              WVG[:, 8 * 1024 + dc * 1024 + nb * 512: 8 * 1024 + dc * 1024 + (nb + 1) * 512],
                              start=(dc == 0), stop=(dc == 7)), inc=(s_t, 1))
                for dc in range(8):
                    bd.do("tensor", lambda E_, tt=tt, dc=dc: E_.matmul(
                          PC[:, 0:16], XNT[:, dc * S + tt * 128: dc * S + tt * 128 + 128],
                          GKW1[:, dc * 16:(dc + 1) * 16],
                          start=(dc == 0), stop=(dc == 7)), inc=(s_t, 1))
                mmd = s_t.n
                bd.do("scalar", lambda E_, pf=pf: E_.activation(EVQ[:, pf * 1024: pf * 1024 + 512], PA[:, 0:512],
                      AF.Copy, scale=QSC), inc=(s_g, 1), waits=[(s_t, mmd), (s_ps[pf], None)])
                bd.do("scalar", lambda E_, pf=pf: E_.copy(EVQ[:, pf * 1024 + 512: (pf + 1) * 1024], PA[:, 512:1024]),
                      inc=(s_g, 1))
                bd.do("scalar", lambda E_, pf=pf: E_.copy(EVV[:, pf * 2048: pf * 2048 + 1024], PB[:, 0:1024]),
                      inc=(s_g, 1))
                bd.do("scalar", lambda E_, pf=pf: E_.copy(EVV[:, pf * 2048 + 1024: (pf + 1) * 2048], PA[:, 1024:2048]),
                      inc=(s_g, 1))
                bd.do("tensor", lambda E_: E_.transpose(PC[0:16, 128:256], PC[:, 0:16], IDF[:]) if False else None, inc=None) if False else None
                bd.do("vector", lambda E_, pf=pf: E_.tensor_copy(ASC[:, pf * 512: pf * 512 + 16], PC[:, 0:16]),
                      inc=(s_v, 1), waits=[(s_t, mmd)])
                bd.do("tensor", lambda E_, pf=pf: E_.transpose(PC[0:16, 128:256], ASC[:, pf * 512: pf * 512 + 16], IDF[:]),
                      inc=(s_t, 1), waits=[(s_v, None)])
                bd.do("vector", lambda E_, pf=pf: E_.tensor_copy(EVZ[:, pf * 128:(pf + 1) * 128], PC[0:16, 128:256]),
                      inc=(s_v, 1), waits=[(s_t, None)])
                zt = s_v.n
                bd.do("vector", lambda E_, pf=pf: E_.tensor_copy(EVZ[:, pf * 128:(pf + 1) * 128], EVZ[:, pf * 128:(pf + 1) * 128]),
                      inc=(s_v, 1), waits=[(s_v, zt)]) if False else None
                GKW2F = bd.sb["GKW2F"]
                GKB2F = bd.sb["GKB2F"]
                ONESRFl = bd.sb["ONESRF"]
                bd.do("tensor", lambda E_, pf=pf, GKW2F=GKW2F: E_.matmul(PC[:, 0:512], EVZ[:, pf * 128:(pf + 1) * 128],
                      GKW2F[0:16, :], start=True, stop=False), inc=(s_t, 1), waits=[(s_v, None)])
                bd.do("tensor", lambda E_, ONESRFl=ONESRFl, GKB2F=GKB2F: E_.matmul(PC[:, 0:512], ONESRFl[:], GKB2F[:],
                      start=False, stop=True), inc=(s_t, 1))
                z2d = s_t.n
                bd.do("scalar", lambda E_, pf=pf: E_.activation(ASC[:, pf * 512:(pf + 1) * 512], PC[:, 0:512], AF.Abs),
                      inc=(s_g, 1), waits=[(s_t, z2d)])
                bd.do("scalar", lambda E_, pf=pf: E_.activation(ASC[:, pf * 512:(pf + 1) * 512],
                      ASC[:, pf * 512:(pf + 1) * 512], AF.Exp, scale=-1.0), inc=(s_g, 1), waits=[(s_g, None)])
                bd.do("scalar", lambda E_, pf=pf: E_.activation(ASC[:, pf * 512:(pf + 1) * 512],
                      ASC[:, pf * 512:(pf + 1) * 512], AF.Ln, bias=1.0), inc=(s_g, 1), waits=[(s_g, None)])
                bd.do("vector", lambda E_, pf=pf: E_.tensor_scalar(EVA[:, pf * 512:(pf + 1) * 512], PC[:, 0:512],
                      0.0, None, op0=ALU.min), inc=(s_v, 1), waits=[(s_g, None), (s_ps[pf], None)])
                bd.do("vector", lambda E_, pf=pf: E_.scalar_tensor_tensor(EVA[:, pf * 512:(pf + 1) * 512],
                      EVA[:, pf * 512:(pf + 1) * 512], 1.0, ASC[:, pf * 512:(pf + 1) * 512],
                      op0=ALU.mult, op1=ALU.subtract), inc=(s_v, 1), waits=[(s_v, None)])
                bd.do("vector", lambda E_, pf=pf: E_.tensor_scalar(EVA[:, pf * 512:(pf + 1) * 512],
                      EVA[:, pf * 512:(pf + 1) * 512], 1.0 / 16.0, None, op0=ALU.mult),
                      inc=(s_v, 1), waits=[(s_v, None)])
                evd_g, evd_v = s_g.n, s_v.n
                row = b * S + tt * 128
                for (dst, srcT, wid, ctr) in ((qdr, EVQ, 1024, s_ps[pf]),):
                    pass
                bd.do("sync", lambda E_, row=row, pf=pf: E_.dma_start(out=qdr[row - b * S:row - b * S + 128, :] if False else qdr[(row - b * S) + 0:0, :], in_=EVQ[:, 0:0]), inc=None) if False else None
                r0 = tt * 128
                bd.do("sync", lambda E_, r0=r0, pf=pf, s_ps=s_ps: E_.dma_start(out=qdr[r0:r0 + 128, :],
                      in_=EVQ[:, pf * 1024: pf * 1024 + 512]), inc=(s_ps[pf], 16),
                      waits=[(s_g, evd_g), (s_v, evd_v)])
                bd.do("sync", lambda E_, r0=r0, pf=pf, s_ps=s_ps: E_.dma_start(out=kdr[r0:r0 + 128, :],
                      in_=EVQ[:, pf * 1024 + 512: (pf + 1) * 1024]), inc=(s_ps[pf], 16))
                bd.do("sync", lambda E_, r0=r0, pf=pf, s_ps=s_ps: E_.dma_start(out=vdr[r0:r0 + 128, :],
                      in_=EVV[:, pf * 2048: pf * 2048 + 1024]), inc=(s_ps[pf], 16))
                bd.do("sync", lambda E_, r0=r0, pf=pf, s_ps=s_ps: E_.dma_start(out=gdr[r0:r0 + 128, :],
                      in_=EVV[:, pf * 2048 + 1024: (pf + 1) * 2048]), inc=(s_ps[pf], 16))
                bd.do("sync", lambda E_, r0=r0, pf=pf, s_ps=s_ps: E_.dma_start(out=adr[r0:r0 + 128, :],
                      in_=EVA[:, pf * 512:(pf + 1) * 512]), inc=(s_ps[pf], 16))
            bd.wait("sync", s_ps[0])
            bd.wait("sync", s_ps[1])
        bd.barrier(f"l2p{b}")
        for h in range(4):
            _l2_head(bd, b, h, nc, sems, (PA, PB, PC, PD, IDB, IDF, ONESCF, UTRF, GNWB,
                     qdr, kdr, vdr, gdr, adr, otdr))
        bd.barrier(f"l2s{b}")
        with ExitStack() as wctx:
            OT = wctx.enter_context(nc.sbuf_tensor(f"l2_otl{b}", [128, 8 * S], BF16))
            SOUT = wctx.enter_context(nc.sbuf_tensor(f"l2_sout{b}", [128, 2 * D], F32))
            s_ol = bd.sem("l2ol")
            bd.do("sync", lambda E_, b=b: E_.dma_start(out=OT[:].rearrange("p (c t) -> p c t", c=8),
                  in_=otdr.rearrange("(c p) t -> p c t", p=128)), inc=(s_ol, 16),
                  waits=[(s_t, None), (s_v, None)])
            olr = s_ol.n
            for tt in range(NTB):
                first = True
                for nb in range(2):
                    for dc in range(8):
                        w_ = [(s_ol, olr)] if first else []
                        if first and tt >= 2:
                            w_.append((s_v, None))
                        bd.do("tensor", lambda E_, tt=tt, nb=nb, dc=dc: E_.matmul(
                              PB[:, nb * 512:(nb + 1) * 512] if tt % 2 == 0 else PA[:, nb * 512:(nb + 1) * 512],
                              OT[:, dc * S + tt * 128: dc * S + tt * 128 + 128],
                              WO[:, dc * D + nb * 512: dc * D + nb * 512 + 512],
                              start=(dc == 0), stop=(dc == 7)), inc=(s_t, 1), waits=w_)
                        first = False
                pj = s_t.n
                sbf = tt % 2
                s_so = bd.sem(f"atso{sbf}")
                bd.do("vector", lambda E_, sbf=sbf, tt=tt: E_.tensor_copy(SOUT[:, sbf * D:(sbf + 1) * D],
                      PB[:, 0:D] if tt % 2 == 0 else PA[:, 0:D]), inc=(s_v, 1), waits=[(s_t, pj), (s_so, None)])
                bd.do("sync", lambda E_, b=b, tt=tt, sbf=sbf, s_so=s_so: E_.dma_start(
                      out=s_dram[b * S + tt * 128: b * S + tt * 128 + 128, :],
                      in_=SOUT[:, sbf * D:(sbf + 1) * D]), inc=(s_so, 16), waits=[(s_v, None)])
            bd.wait("sync", bd.sem("atso0"))
            bd.wait("sync", bd.sem("atso1"))


def _l2_head(bd, b, h, nc, sems, shared):
    (PA, PB, PC, PD, IDB, IDF, ONESCF, UTRF, GNWB, qdr, kdr, vdr, gdr, adr, otdr) = shared
    s_t, s_v, s_g, s_st = sems["t"], sems["v"], sems["g"], sems["st"]
    NC_ = 16
    with ExitStack() as hctx:
        QH = hctx.enter_context(nc.sbuf_tensor(f"l2h_q{b}{h}", [128, NC_ * 128], BF16))
        KH = hctx.enter_context(nc.sbuf_tensor(f"l2h_k{b}{h}", [128, NC_ * 128], BF16))
        VH = hctx.enter_context(nc.sbuf_tensor(f"l2h_v{b}{h}", [128, NC_ * 256], BF16))
        AH = hctx.enter_context(nc.sbuf_tensor(f"l2h_a{b}{h}", [128, NC_ * 128], F32))
        GH = hctx.enter_context(nc.sbuf_tensor(f"l2h_g{b}{h}", [128, 2 * 256], BF16))
        SST = hctx.enter_context(nc.sbuf_tensor(f"l2h_s{b}{h}", [128, 256], F32))
        SSB = hctx.enter_context(nc.sbuf_tensor(f"l2h_sb{b}{h}", [128, 256], BF16))
        LAM = hctx.enter_context(nc.sbuf_tensor(f"l2h_lam{b}{h}", [128, 2 * 128], F32))
        LAI = hctx.enter_context(nc.sbuf_tensor(f"l2h_lai{b}{h}", [128, 2 * 128], F32))
        QT = hctx.enter_context(nc.sbuf_tensor(f"l2h_qt{b}{h}", [128, 2 * 128], BF16))
        KT = hctx.enter_context(nc.sbuf_tensor(f"l2h_kt{b}{h}", [128, 2 * 128], BF16))
        KTM = hctx.enter_context(nc.sbuf_tensor(f"l2h_ktm{b}{h}", [128, 2 * 128], BF16))
        AMS = hctx.enter_context(nc.sbuf_tensor(f"l2h_am{b}{h}", [128, 2 * 128], BF16))
        GAM = hctx.enter_context(nc.sbuf_tensor(f"l2h_gam{b}{h}", [128, 4], F32))
        OEV = hctx.enter_context(nc.sbuf_tensor(f"l2h_oev{b}{h}", [128, 2 * 256], F32))
        OST = hctx.enter_context(nc.sbuf_tensor(f"l2h_ost{b}{h}", [128, 2 * 256], BF16))
        s_hl = bd.sem("l2hl")
        s_gl = [bd.sem("l2gl0"), bd.sem("l2gl1")]
        s_os = [bd.sem("l2os0"), bd.sem("l2os1")]
        b0 = b * S
        bd.do("sync", lambda E_: E_.dma_start(out=QH[:].rearrange("p (c n) -> p c n", n=128),
              in_=qdr.rearrange("(c p) n -> p c n", p=128)[:, :, h * 128:(h + 1) * 128]),
              inc=(s_hl, 16), waits=[(s_t, None), (s_v, None), (s_g, None)])
        bd.do("sync", lambda E_: E_.dma_start(out=KH[:].rearrange("p (c n) -> p c n", n=128),
              in_=kdr.rearrange("(c p) n -> p c n", p=128)[:, :, h * 128:(h + 1) * 128]),
              inc=(s_hl, 16))
        bd.do("sync", lambda E_: E_.dma_start(out=VH[:].rearrange("p (c n) -> p c n", n=256),
              in_=vdr.rearrange("(c p) n -> p c n", p=128)[:, :, h * 256:(h + 1) * 256]),
              inc=(s_hl, 16))
        bd.do("sync", lambda E_: E_.dma_start(out=AH[:].rearrange("p (c n) -> p c n", n=128),
              in_=adr.rearrange("(c p) n -> p c n", p=128)[:, :, h * 128:(h + 1) * 128]),
              inc=(s_hl, 16))
        hl = s_hl.n
        bd.do("vector", lambda E_: E_.memset(SST[:], 0.0), inc=(s_v, 1))
        bd.do("vector", lambda E_: E_.memset(SSB[:], 0.0), inc=(s_v, 1))
        sini = s_v.n
        for c in range(NC_):
            pf = c % 2
            qs = slice(c * 128, (c + 1) * 128)
            vs = slice(c * 256, (c + 1) * 256)
            lf = slice(pf * 128, (pf + 1) * 128)
            vf = slice(pf * 256, (pf + 1) * 256)
            bd.do("tensor", lambda E_, qs=qs: E_.matmul(PC[:, 0:128], UTRF[:], AH[:, qs], start=True, stop=True),
                  inc=(s_t, 1), waits=[(s_hl, hl), (s_g, None), (s_v, None)])
            bd.do("tensor", lambda E_, qs=qs: E_.matmul(PC[:, 192:193], AH[:, qs], ONESCF[:], start=True, stop=True),
                  inc=(s_t, 1))
            cumd = s_t.n
            bd.do("scalar", lambda E_, lf=lf: E_.activation(LAM[:, lf], PC[:, 0:128], AF.Exp), inc=(s_g, 1),
                  waits=[(s_t, cumd)])
            bd.do("scalar", lambda E_, lf=lf: E_.activation(LAI[:, lf], PC[:, 0:128], AF.Exp, scale=-1.0), inc=(s_g, 1))
            bd.do("scalar", lambda E_, pf=pf: E_.activation(GAM[:, pf:pf + 1], PC[:, 192:193], AF.Exp), inc=(s_g, 1))
            lamd = s_g.n
            bd.do("vector", lambda E_, qs=qs, lf=lf: E_.tensor_tensor(QT[:, lf], QH[:, qs], LAM[:, lf], op=ALU.mult),
                  inc=(s_v, 1), waits=[(s_g, lamd), (s_t, None)])
            bd.do("vector", lambda E_, qs=qs, lf=lf: E_.tensor_tensor(KT[:, lf], KH[:, qs], LAI[:, lf], op=ALU.mult),
                  inc=(s_v, 1), waits=[(s_v, None)])
            qkd = s_v.n
            bd.do("tensor", lambda E_, lf=lf: E_.transpose(PD[:, 0:128], QT[:, lf], IDB[:]), inc=(s_t, 1),
                  waits=[(s_v, qkd), (s_g, None)])
            bd.do("tensor", lambda E_, lf=lf: E_.transpose(PD[:, 128:256], KT[:, lf], IDB[:]), inc=(s_t, 1))
            tpd = s_t.n
            bd.do("scalar", lambda E_, lf=lf: E_.copy(QT[:, lf], PD[:, 0:128]), inc=(s_g, 1), waits=[(s_t, tpd)])
            bd.do("scalar", lambda E_, lf=lf: E_.copy(KTM[:, lf], PD[:, 128:256]), inc=(s_g, 1))
            tpe = s_g.n
            bd.do("tensor", lambda E_, lf=lf: E_.matmul(PA[:, 0:128], KTM[:, lf], QT[:, lf], start=True, stop=True),
                  inc=(s_t, 1), waits=[(s_g, tpe)])
            atd = s_t.n
            bd.do("vector", lambda E_, lf=lf: E_.tensor_tensor(AMS[:, lf], PA[:, 0:128], UTRF[:], op=ALU.mult),
                  inc=(s_v, 1), waits=[(s_t, atd)])
            amd = s_v.n
            bd.do("tensor", lambda E_, lf=lf, vs=vs: E_.matmul(PB[:, 0:256], AMS[:, lf], VH[:, vs],
                  start=True, stop=False), inc=(s_t, 1), waits=[(s_v, amd), (s_v, sini), (s_g, None)])
            bd.do("tensor", lambda E_, lf=lf: E_.matmul(PB[:, 0:256], QT[:, lf], SSB[:],
                  start=False, stop=True), inc=(s_t, 1))
            od = s_t.n
            bd.do("tensor", lambda E_, lf=lf, vs=vs: E_.matmul(PB[:, 512:768], KT[:, lf], VH[:, vs],
                  start=True, stop=True), inc=(s_t, 1))
            kvd = s_t.n
            bd.do("vector", lambda E_: E_.tensor_tensor(SST[:], SST[:], PB[:, 512:768], op=ALU.add),
                  inc=(s_v, 1), waits=[(s_t, kvd)])
            bd.do("vector", lambda E_, pf=pf: E_.tensor_scalar(SST[:], SST[:], GAM[:, pf:pf + 1], None, op0=ALU.mult),
                  inc=(s_v, 1), waits=[(s_v, None)])
            bd.do("vector", lambda E_: E_.tensor_copy(SSB[:], SST[:]), inc=(s_v, 1), waits=[(s_v, None)])
            bd.do("scalar", lambda E_, vf=vf, pf=pf: E_.activation(OEV[:, vf], PB[:, 0:256], AF.Square,
                  accum_out=GAM[:, 2 + pf:3 + pf]), inc=(s_g, 1), waits=[(s_t, od)])
            msd = s_g.n
            bd.do("vector", lambda E_, pf=pf: E_.tensor_scalar(GAM[:, 2 + pf:3 + pf], GAM[:, 2 + pf:3 + pf],
                  1.0 / 256.0, 1e-5, op0=ALU.mult, op1=ALU.add), inc=(s_v, 1), waits=[(s_g, msd)])
            bd.do("vector", lambda E_, pf=pf: E_.reciprocal(GAM[:, 2 + pf:3 + pf], GAM[:, 2 + pf:3 + pf]),
                  inc=(s_v, 1), waits=[(s_v, None)])
            rsd = s_v.n
            bd.do("scalar", lambda E_, pf=pf: E_.activation(GAM[:, 2 + pf:3 + pf], GAM[:, 2 + pf:3 + pf], AF.Sqrt),
                  inc=(s_g, 1), waits=[(s_v, rsd)])
            rsq = s_g.n
            bd.do("sync", lambda E_, c=c, pf=pf, b0=b0: E_.dma_start(out=GH[:, pf * 256:(pf + 1) * 256],
                  in_=gdr[c * 128:(c + 1) * 128, h * 256:(h + 1) * 256]), inc=(s_gl[pf], 16),
                  waits=[(s_v, None)])
            gld = s_gl[pf].n
            bd.do("scalar", lambda E_, pf=pf, vf=vf: E_.activation(GH[:, pf * 256:(pf + 1) * 256].bitcast(BF16) if False else GH[:, pf * 256:(pf + 1) * 256],
                  GH[:, pf * 256:(pf + 1) * 256], AF.Silu), inc=(s_g, 1), waits=[(s_gl[pf], gld)]) if False else None
            bd.do("scalar", lambda E_, pf=pf: E_.activation(OEV[:, pf * 256:(pf + 1) * 256] if False else OEV[:, vf], OEV[:, vf], AF.Identity), inc=None) if False else None
            bd.do("vector", lambda E_, vf=vf, pf=pf: E_.tensor_scalar(OEV[:, vf], PB[:, 0:256],
                  GAM[:, 2 + pf:3 + pf], None, op0=ALU.mult), inc=(s_v, 1), waits=[(s_g, rsq)])
            bd.do("vector", lambda E_, vf=vf: E_.tensor_tensor(OEV[:, vf], OEV[:, vf], GNWB[:], op=ALU.mult),
                  inc=(s_v, 1), waits=[(s_v, None)])
            sgd = s_v.n
            bd.do("scalar", lambda E_, pf=pf, vf=vf: E_.activation(GH[:, pf * 256:(pf + 1) * 256],
                  GH[:, pf * 256:(pf + 1) * 256], AF.Silu), inc=(s_g, 1), waits=[(s_gl[pf], gld), (s_v, sgd)])
            bd.do("vector", lambda E_, vf=vf, pf=pf: E_.tensor_tensor(OST[:, vf], OEV[:, vf],
                  GH[:, pf * 256:(pf + 1) * 256], op=ALU.mult), inc=(s_v, 1),
                  waits=[(s_g, None), (s_os[pf], None)])
            ofd = s_v.n
            bd.do("tensor", lambda E_, vf=vf: E_.transpose(PD[:, 256:384], OST[:, vf.start:vf.start + 128], IDB[:]),
                  inc=(s_t, 1), waits=[(s_v, ofd), (s_g, None)])
            bd.do("tensor", lambda E_, vf=vf: E_.transpose(PD[:, 384:512], OST[:, vf.start + 128:vf.stop], IDB[:]),
                  inc=(s_t, 1))
            otd = s_t.n
            bd.do("scalar", lambda E_, vf=vf: E_.copy(OST[:, vf], PD[:, 256:512]), inc=(s_g, 1), waits=[(s_t, otd)])
            ote = s_g.n
            bd.do("sync", lambda E_, c=c, vf=vf, pf=pf, s_os=s_os: E_.dma_start(
                  out=otdr[h * 256:(h + 1) * 256, c * 128:(c + 1) * 128].rearrange("(u p) t -> p u t", p=128),
                  in_=OST[:, vf].rearrange("p (u t) -> p u t", u=2)), inc=(s_os[pf], 16),
                  waits=[(s_g, ote), (s_os[pf], None)])
        bd.wait("sync", s_os[0])
        bd.wait("sync", s_os[1])



def emit_attn_l3(bd: Build):
    nc = bd.nc
    xn_dram = bd.dram["xn_dram"]
    s_dram = bd.dscratch("s_dram", (T, D), F32)
    qkt_dram = bd.dscratch("qkt_dram", (16, 128, S), BF16)
    zdram = bd.dscratch("z_bounce", (16, S), F32)
    wqkv = bd.din("l3_wqkv", (D, 1536))
    wo = bd.din("l3_wo", (D, D))
    PA, PB, PC, PD = bd.sb["PA"], bd.sb["PB"], bd.sb["PC"], bd.sb["PD"]
    IDB, IDF = bd.sb["IDB"], bd.sb["IDF"]
    ONESRF, ONESCB = bd.sb["ONESRF"], bd.sb["ONESCB"]
    ZSEL = bd.sb["ZSEL"]
    s_w = bd.sem("atw")
    s_x = bd.sem("atx")
    s_t = bd.sem("att")
    s_v = bd.sem("atv")
    s_g = bd.sem("atg")
    s_st = bd.sem("atst")
    s_c = bd.sem("atc")
    s_qp = bd.sem("atqp")
    sems = {"t": s_t, "v": s_v, "g": s_g, "x": s_x, "st": s_st, "w": s_w}

    with ExitStack() as lctx:
        WPRJ = lctx.enter_context(nc.sbuf_tensor("l3_wo_s", [128, 8 * D], BF16))
        ROPEC = lctx.enter_context(nc.sbuf_tensor("l3_ropec", [128, 16 * 64], F32))
        ROPES = lctx.enter_context(nc.sbuf_tensor("l3_ropes", [128, 16 * 64], F32))
        CMS = lctx.enter_context(nc.sbuf_tensor("l3_cms", [128, 4 * 512], BF16))
        bd.do("gpsimd", lambda E_: E_.dma_start(out=WPRJ[:].rearrange("p (c n) -> p c n", c=8),
              in_=wo.rearrange("(c p) n -> p c n", p=128)), inc=(s_w, 16))
        bd.do("sync", lambda E_: E_.dma_start(out=ROPEC[:], in_=bd.din("ropec_r", (128, 1024))), inc=(s_c, 16))
        bd.do("sync", lambda E_: E_.dma_start(out=ROPES[:], in_=bd.din("ropes_r", (128, 1024))), inc=(s_c, 16))
        for off in range(4):
            bd.do("sync", lambda E_, off=off: E_.dma_start(out=CMS[:, off * 512:(off + 1) * 512],
                  in_=bd.din(f"cm{off * 128}", (128, 512), BF16)), inc=(s_c, 16))
        w_ready, x_ready = s_w.n, s_c.n
        for b in range(BPC):
            _l3_sample(bd, b, nc, sems, (PA, PB, PC, PD, IDB, IDF, ONESRF, ONESCB, ZSEL,
                       WPRJ, ROPEC, ROPES, CMS, xn_dram, s_dram, qkt_dram, zdram, wqkv,
                       s_c, s_qp, w_ready, x_ready))
            bd.barrier(f"l3b{b}")
    bd.barrier("attn3")


def _l3_sample(bd, b, nc, sems, shared):
    (PA, PB, PC, PD, IDB, IDF, ONESRF, ONESCB, ZSEL, WPRJ, ROPEC, ROPES, CMS,
     xn_dram, s_dram, qkt_dram, zdram, wqkv, s_c, s_qp, w_ready, x_ready) = shared
    s_w, s_x, s_t, s_v, s_g, s_st = (sems["w"], sems["x"], sems["t"], sems["v"], sems["g"], sems["st"])
    NH = 16
    with ExitStack() as bctx:
        XNT = bctx.enter_context(nc.sbuf_tensor(f"l3_xnt{b}", [128, 8 * S], BF16))
        for c in range(8):
            bd.do("sync", lambda E_, b=b, c=c: E_.dma_start(out=XNT[:, c * S:(c + 1) * S],
                  in_=xn_dram[b * S:(b + 1) * S, c * 128:(c + 1) * 128], transpose=True),
                  inc=(s_x, 16), waits=([(s_t, None), (s_v, None), (s_g, None)] if c == 0 else []))
        xnt_ready = s_x.n
        VAUG = bctx.enter_context(nc.sbuf_tensor(f"l3_vaug{b}", [128, NTB * 4 * 65], BF16))
        with ExitStack() as pctx:
            WQKV = pctx.enter_context(nc.sbuf_tensor(f"l3_wqkv{b}", [128, 8 * 1536], BF16))
            NRM = pctx.enter_context(nc.sbuf_tensor(f"l3_nrm{b}", [128, 1536], F32))
            NST = pctx.enter_context(nc.sbuf_tensor(f"l3_nst{b}", [128, 64], F32))
            QRS = pctx.enter_context(nc.sbuf_tensor(f"l3_qrs{b}", [128, 2 * 1280], BF16))
            RTMP = pctx.enter_context(nc.sbuf_tensor(f"l3_rtmp{b}", [128, 2 * 1280], F32))
            PDS = pctx.enter_context(nc.sbuf_tensor(f"l3_pds{b}", [128, 2 * 1024], BF16))
            bd.do("gpsimd", lambda E_: E_.dma_start(out=WQKV[:].rearrange("p (c n) -> p c n", c=8),
                  in_=wqkv.rearrange("(c p) n -> p c n", p=128)), inc=(s_w, 16),
                  waits=[(s_t, None), (s_v, None), (s_g, None)])
            w2 = s_w.n
            bd.do("vector", lambda E_: E_.memset(VAUG[:], 1.0), inc=(s_v, 1))
            vaug_ones = s_v.n
            for tt in range(NTB):
                tw = [(s_w, w2), (s_c, x_ready), (s_x, xnt_ready)]
                if tt > 0:
                    tw += [(s_v, None), (s_g, None)]
                first = True
                for nb in range(3):
                    for dc in range(8):
                        bd.do("tensor", lambda E_, tt=tt, nb=nb, dc=dc: E_.matmul(
                              PA[:, nb * 512:(nb + 1) * 512], XNT[:, dc * S + tt * 128: dc * S + tt * 128 + 128],
                              WQKV[:, dc * 1536 + nb * 512: dc * 1536 + (nb + 1) * 512],
                              start=(dc == 0), stop=(dc == 7)), inc=(s_t, 1), waits=(tw if first else []))
                        first = False
                mm_done = s_t.n
                pf = tt % 2
                bd.do("scalar", lambda E_: E_.activation(NRM[:, 0:1024], PA[:, 0:1024], AF.Square),
                      inc=(s_g, 1), waits=[(s_t, mm_done), (s_v, None)])
                bd.do("scalar", lambda E_: E_.activation(NRM[:, 1024:1280], PA[:, 1024:1280], AF.Square),
                      inc=(s_g, 1))
                sqd = s_g.n
                bd.do("vector", lambda E_: E_.tensor_reduce(NST[:, 0:20], NRM[:, 0:1280].rearrange("p (h x) -> p h x", x=64),
                      axis=AX.X, op=ALU.add), inc=(s_v, 1), waits=[(s_g, sqd)])
                bd.do("scalar", lambda E_: E_.activation(NST[:, 32:52], NST[:, 0:20], AF.Sqrt),
                      inc=(s_g, 1), waits=[(s_v, None)])
                bd.do("vector", lambda E_: E_.tensor_scalar(NST[:, 32:52], NST[:, 32:52], 1e-10, None, op0=ALU.add),
                      inc=(s_v, 1), waits=[(s_g, None)])
                bd.do("vector", lambda E_: E_.reciprocal(NST[:, 0:20], NST[:, 32:52]),
                      inc=(s_v, 1), waits=[(s_v, None)])
                rcd = s_v.n
                bd.do("vector", lambda E_: E_.tensor_tensor(RTMP[:, 0:1280].rearrange("p (h x) -> p h x", x=64),
                      PA[:, 0:1280].rearrange("p (h x) -> p h x", x=64),
                      NST[:, 0:20].unsqueeze(2).to_broadcast([128, 20, 64]), op=ALU.mult),
                      inc=(s_v, 1), waits=[(s_v, rcd)])
                qnd = s_v.n
                rc = tt * 64
                bd.do("vector", lambda E_, rc=rc: E_.tensor_tensor(
                      RTMP[:, 1280:2560].rearrange("p (h x) -> p h x", x=64).__getitem__((slice(None), slice(0, 20), slice(0, 32))) if False else
                      RTMP[:, 1280:1280 + 1280].rearrange("p (h x) -> p h x", x=64)[:, :, 0:32],
                      RTMP[:, 0:1280].rearrange("p (h x) -> p h x", x=64)[:, :, 32:64],
                      ROPES[:, rc:rc + 64].unsqueeze(1).to_broadcast([128, 20, 64])[:, :, 0:32], op=ALU.mult),
                      inc=(s_v, 1), waits=[(s_v, qnd)])
                bd.do("vector", lambda E_, rc=rc: E_.tensor_tensor(
                      RTMP[:, 1280:2560].rearrange("p (h x) -> p h x", x=64)[:, :, 32:64],
                      RTMP[:, 0:1280].rearrange("p (h x) -> p h x", x=64)[:, :, 0:32],
                      ROPES[:, rc:rc + 64].unsqueeze(1).to_broadcast([128, 20, 64])[:, :, 32:64], op=ALU.mult),
                      inc=(s_v, 1), waits=[(s_v, None)])
                bd.do("vector", lambda E_, rc=rc: E_.tensor_tensor(
                      RTMP[:, 0:1280].rearrange("p (h x) -> p h x", x=64),
                      RTMP[:, 0:1280].rearrange("p (h x) -> p h x", x=64),
                      ROPEC[:, rc:rc + 64].unsqueeze(1).to_broadcast([128, 20, 64]), op=ALU.mult),
                      inc=(s_v, 1), waits=[(s_v, None)])
                pf1 = (tt % 2) * 1280
                bd.do("vector", lambda E_, pf1=pf1: E_.tensor_tensor(QRS[:, pf1:pf1 + 1280],
                      RTMP[:, 0:1280], RTMP[:, 1280:2560], op=ALU.add),
                      inc=(s_v, 1), waits=[(s_v, None)])
                qk_ready = s_v.n
                bd.do("scalar", lambda E_, tt=tt: E_.copy(
                      VAUG[:, tt * 260: tt * 260 + 260].rearrange("p (h c) -> p h c", c=65)[:, :, 0:64],
                      PA[:, 1280:1536].rearrange("p (h c) -> p h c", c=64)),
                      inc=(s_g, 1), waits=[(s_t, mm_done), (s_v, vaug_ones)])
                for r in range(2):
                    pbf = (tt * 2 + r) % 2
                    s_qs = bd.sem(f"atqs{pbf}")
                    nch = 8 if r == 0 else 2
                    for cdx in range(nch):
                        cg = r * 8 + cdx
                        tw2 = []
                        if cdx == 0:
                            tw2 = [(s_v, qk_ready), (s_g, None)]
                        bd.do("tensor", lambda E_, pf1=pf1, cg=cg, cdx=cdx: E_.transpose(
                              PD[:, cdx * 128:(cdx + 1) * 128], QRS[:, pf1 + cg * 128: pf1 + cg * 128 + 128], IDB[:]),
                              inc=(s_t, 1), waits=tw2)
                    bd.do("scalar", lambda E_, pbf=pbf, nch=nch: E_.copy(PDS[:, pbf * 1024: pbf * 1024 + nch * 128], PD[:, 0:nch * 128]),
                          inc=(s_g, 1), waits=[(s_t, None), (s_qs, None)])
                    bd.do("sync", lambda E_, tt=tt, r=r, pbf=pbf, nch=nch, s_qs=s_qs: E_.dma_start(
                          out=qkt_dram.rearrange("c p t -> p c t")[:, r * 8:r * 8 + nch, tt * 128:(tt + 1) * 128],
                          in_=PDS[:, pbf * 1024: pbf * 1024 + nch * 128].rearrange("p (c m) -> p c m", m=128)),
                          inc=(s_qs, 16), waits=[(s_g, None)])
                    if r == 1:
                        for g2 in range(4):
                            srcc = 8 + g2 // 2
                            srow = (g2 % 2) * 64
                            for half in range(2):
                                bd.do("sync", lambda E_, tt=tt, g2=g2, srcc=srcc, srow=srow, half=half, pbf=pbf, s_qs=s_qs: E_.dma_start(
                                      out=qkt_dram[10 + g2, half * 64:(half + 1) * 64, tt * 128:(tt + 1) * 128],
                                      in_=PDS[:, pbf * 1024 + (srcc - 8) * 128: pbf * 1024 + (srcc - 8) * 128 + 128][srow:srow + 64, :]),
                                      inc=(s_qs, 16), waits=[])
            bd.wait("sync", bd.sem("atqs0"))
            bd.wait("sync", bd.sem("atqs1"))
        bd.barrier(f"l3p{b}")
        _l3_scores(bd, b, nc, sems, (PA, PB, PC, PD, IDB, IDF, ONESRF, ONESCB, ZSEL,
                   WPRJ, CMS, VAUG, s_dram, qkt_dram, zdram, s_qp))


def _l3_scores(bd, b, nc, sems, shared):
    (PA, PB, PC, PD, IDB, IDF, ONESRF, ONESCB, ZSEL, WPRJ, CMS, VAUG,
     s_dram, qkt_dram, zdram, s_qp) = shared
    s_w, s_x, s_t, s_v, s_g, s_st = (sems["w"], sems["x"], sems["t"], sems["v"], sems["g"], sems["st"])
    scale = 1.0 / 8.0
    with ExitStack() as sctx:
        QP = sctx.enter_context(nc.sbuf_tensor(f"l3s_qp{b}", [128, 2 * 2 * S], BF16))
        KD = sctx.enter_context(nc.sbuf_tensor(f"l3s_kd{b}", [128, 4 * S], BF16))
        CTX = sctx.enter_context(nc.sbuf_tensor(f"l3s_ctx{b}", [128, 8 * 512], BF16))
        EXPS = sctx.enter_context(nc.sbuf_tensor(f"l3s_exps{b}", [128, 2 * 512], BF16))
        ZST = sctx.enter_context(nc.sbuf_tensor(f"l3s_zst{b}", [128, 512], F32))
        ZTQ = sctx.enter_context(nc.sbuf_tensor(f"l3s_ztq{b}", [16, 2 * 512], F32))
        SOUT = sctx.enter_context(nc.sbuf_tensor(f"l3s_sout{b}", [128, 2 * D], F32))
        s_kl = bd.sem("l3kl")
        for g2 in range(4):
            bd.do("sync", lambda E_, g2=g2: E_.dma_start(out=KD[:, g2 * S:(g2 + 1) * S], in_=qkt_dram[10 + g2]),
                  inc=(s_kl, 16), waits=([(s_t, None), (s_v, None), (s_g, None)] if g2 == 0 else []))
        klr = s_kl.n
        qp_last_use = [0, 0]

        def load_qpair(hp, buf):
            s_q2 = bd.sem(f"l3qp{buf}")
            bd.do("sync", lambda E_, hp=hp, buf=buf, s_q2=s_q2: E_.dma_start(
                  out=QP[:, buf * 2 * S: buf * 2 * S + S], in_=qkt_dram[hp]),
                  inc=(s_q2, 16), waits=[(s_t, qp_last_use[buf])])
            return s_q2, s_q2.n

        for qg in range(4):
            q0 = qg * 512
            kcmax = (q0 + 511) // 128
            for hp in range(8):
                buf = hp % 2
                s_q2, qpr = load_qpair(hp, buf)
                for hh in range(2):
                    h = hp * 2 + hh
                    g2 = h // 4
                    base = hh * 64
                    pbreg = hh * 512
                    odd = (hh == 1)
                    exp_after = {}
                    smm_after = {}

                    def emit_smm(kc, h=h, base=base, buf=buf, q0=q0, g2=g2, s_q2=s_q2, qpr=qpr):
                        ps = 1024 + (kc % 2) * 512
                        w_ = [(s_q2, qpr), (s_kl, klr)]
                        if kc >= 2:
                            w_.append((s_g, exp_after[kc - 2][0]))
                            w_.append((s_v, exp_after[kc - 2][1]))
                        bd.do("tensor", lambda E_, kc=kc, ps=ps: E_.matmul(
                              PA[:, ps:ps + 512],
                              KD[base:base + 64, g2 * S + kc * 128: g2 * S + kc * 128 + 128],
                              QP[base:base + 64, buf * 2 * S + q0: buf * 2 * S + q0 + 512],
                              start=True, stop=True), inc=(s_t, 1), waits=w_)
                        smm_after[kc] = s_t.n

                    def emit_exp(kc, q0=q0):
                        ps = 1024 + (kc % 2) * 512
                        eb = (kc % 2) * 512
                        off = kc * 128 - q0
                        bd.do("scalar", lambda E_, ps=ps, eb=eb: E_.activation(
                              EXPS[:, eb:eb + 512], PA[:, ps:ps + 512], AF.Exp, scale=scale),
                              inc=(s_g, 1), waits=[(s_t, smm_after[kc])])
                        if off >= 0:
                            moff = off // 128
                            bd.do("vector", lambda E_, eb=eb, moff=moff: E_.tensor_tensor(
                                  EXPS[:, eb:eb + 512], EXPS[:, eb:eb + 512],
                                  CMS[:, moff * 512:(moff + 1) * 512], op=ALU.mult),
                                  inc=(s_v, 1), waits=[(s_g, None)])
                            exp_after[kc] = (s_g.n, s_v.n)
                        else:
                            exp_after[kc] = (s_g.n, s_v.n)

                    def emit_pv(kc, h=h, odd=odd, pbreg=pbreg, q0=q0):
                        eb = (kc % 2) * 512
                        w_ = [(s_g, exp_after[kc][0]), (s_v, exp_after[kc][1])]
                        if kc == 0:
                            w_ += [(s_g, None), (s_v, None)]
                        lst = (kc == kcmax)
                        g2 = h // 4
                        if not odd:
                            bd.do("tensor", lambda E_, kc=kc, eb=eb, lst=lst: E_.matmul(
                                  PB[0:65, pbreg:pbreg + 512],
                                  VAUG[:, kc * 260 + g2 * 65: kc * 260 + g2 * 65 + 65],
                                  EXPS[:, eb:eb + 512], start=(kc == 0), stop=lst), inc=(s_t, 1), waits=w_)
                        else:
                            bd.do("tensor", lambda E_, kc=kc, eb=eb, lst=lst: E_.matmul(
                                  PB[64:128, pbreg:pbreg + 512],
                                  VAUG[:, kc * 260 + g2 * 65: kc * 260 + g2 * 65 + 64],
                                  EXPS[:, eb:eb + 512], start=(kc == 0), stop=lst), inc=(s_t, 1), waits=w_)
                            bd.do("tensor", lambda E_, kc=kc, eb=eb, lst=lst: E_.matmul(
                                  PC[0:1, 0:512], ONESCB[:], EXPS[:, eb:eb + 512],
                                  start=(kc == 0), stop=lst), inc=(s_t, 1))

                    def exp_guard(kc):
                        return exp_after[kc][0]

                    emit_smm(0)
                    for kc in range(kcmax + 1):
                        if kc < kcmax:
                            pass
                        if kc < kcmax:
                            emit_smm(kc + 1)
                        emit_exp(kc)
                        emit_pv(kc)
                    pv_done = s_t.n
                    s_zw = bd.sem(f"atzw{h % 2}")
                    if not odd:
                        bd.do("scalar", lambda E_, pbreg=pbreg: E_.copy(ZST[64:65, 0:512],
                              PB[64:65, pbreg:pbreg + 512]), inc=(s_g, 1), waits=[(s_t, pv_done), (s_zw, None)])
                        bd.do("sync", lambda E_, h=h, q0=q0, s_zw=s_zw: E_.dma_start(
                              out=zdram[h:h + 1, q0:q0 + 512], in_=ZST[64:65, 0:512]),
                              inc=(s_zw, 16), waits=[(s_g, None)])
                        bd.do("scalar", lambda E_, h=h, pbreg=pbreg: E_.copy(
                              CTX[0:64, (h // 2) * 512:(h // 2) * 512 + 512],
                              PB[0:64, pbreg:pbreg + 512]), inc=(s_g, 1), waits=[(s_g, None)])
                    else:
                        bd.do("scalar", lambda E_: E_.copy(ZST[0:1, 0:512], PC[0:1, 0:512]),
                              inc=(s_g, 1), waits=[(s_t, pv_done), (s_zw, None)])
                        bd.do("sync", lambda E_, h=h, q0=q0, s_zw=s_zw: E_.dma_start(
                              out=zdram[h:h + 1, q0:q0 + 512], in_=ZST[0:1, 0:512]),
                              inc=(s_zw, 16), waits=[(s_g, None)])
                        bd.do("scalar", lambda E_, h=h, pbreg=pbreg: E_.copy(
                              CTX[64:128, (h // 2) * 512:(h // 2) * 512 + 512],
                              PB[64:128, pbreg:pbreg + 512]), inc=(s_g, 1), waits=[(s_g, None)])
                qp_last_use[buf] = s_t.n
            allctx = s_g.n
            zq = (qg % 2) * 512
            bd.wait("sync", bd.sem("atzw0"))
            bd.wait("sync", bd.sem("atzw1"))
            s_zl = bd.sem("atzl")
            bd.do("sync", lambda E_, qg=qg, zq=zq: E_.dma_start(out=ZTQ[:, zq:zq + 512],
                  in_=zdram[:, qg * 512:(qg + 1) * 512]), inc=(s_zl, 16), waits=[(s_v, None)])
            zl = s_zl.n
            bd.do("vector", lambda E_, zq=zq: E_.reciprocal(ZTQ[:, zq:zq + 512], ZTQ[:, zq:zq + 512]),
                  inc=(s_v, 1), waits=[(s_zl, zl)])
            zi = s_v.n
            for hp in range(8):
                bd.do("tensor", lambda E_, hp=hp, zq=zq: E_.matmul(
                      PC[:, 0:512], ZSEL[0:16, hp * 128:(hp + 1) * 128], ZTQ[0:16, zq:zq + 512],
                      start=True, stop=True), inc=(s_t, 1),
                      waits=[(s_v, zi), (s_g, allctx), (s_v, None)])
                zb = s_t.n
                bd.do("vector", lambda E_, hp=hp: E_.scalar_tensor_tensor(
                      CTX[:, hp * 512:(hp + 1) * 512], CTX[:, hp * 512:(hp + 1) * 512], 1.0,
                      PC[:, 0:512], op0=ALU.mult, op1=ALU.mult), inc=(s_v, 1), waits=[(s_t, zb)])
            ctxn = s_v.n
            for sl in range(4):
                PRJ = PB if sl % 2 == 0 else PA
                first = True
                for c in range(8):
                    for nb in range(2):
                        w_ = []
                        if first:
                            w_ = [(s_v, ctxn)]
                            if sl >= 2:
                                w_.append((s_v, None))
                        bd.do("tensor", lambda E_, sl=sl, c=c, nb=nb, PRJ=PRJ: E_.matmul(
                              PRJ[:, nb * 512:(nb + 1) * 512], CTX[:, c * 512 + sl * 128: c * 512 + sl * 128 + 128],
                              WPRJ[:, c * D + nb * 512: c * D + nb * 512 + 512],
                              start=(c == 0), stop=(c == 7)), inc=(s_t, 1), waits=w_)
                        first = False
                pj = s_t.n
                sbf = sl % 2
                s_so = bd.sem(f"atso{sbf}")
                bd.do("vector", lambda E_, sbf=sbf, PRJ=PRJ: E_.tensor_copy(SOUT[:, sbf * D:(sbf + 1) * D], PRJ[:, 0:D]),
                      inc=(s_v, 1), waits=[(s_t, pj), (s_so, None)])
                bd.do("sync", lambda E_, b=b, qg=qg, sl=sl, sbf=sbf, s_so=s_so: E_.dma_start(
                      out=s_dram[b * S + qg * 512 + sl * 128: b * S + qg * 512 + sl * 128 + 128, :],
                      in_=SOUT[:, sbf * D:(sbf + 1) * D]), inc=(s_so, 16), waits=[(s_v, None)])
        bd.wait("sync", bd.sem("atso0"))
        bd.wait("sync", bd.sem("atso1"))
        bd.barrier(f"l3sc{b}")



def emit_ffn_l3(bd: Build, x_src, x_dst):
    nc = bd.nc
    xdt_dram = bd.dram["xdt_dram"]
    s_dram = bd.dram["s_dram"]
    w_d = bd.din("l3_fw", (D, D))
    bias_d = bd.din("l3_fb", (1, D))
    PA, PB, PC, PD = bd.sb["PA"], bd.sb["PB"], bd.sb["PC"], bd.sb["PD"]
    IDF = bd.sb["IDF"]
    ONESRF = bd.sb["ONESRF"]
    s_w = bd.sem("ffw")
    s_t = bd.sem("fft")
    s_v = bd.sem("ffv")
    s_g = bd.sem("ffg")
    s_cl2 = [bd.sem("ffcla"), bd.sem("ffclb")]
    s_ot = bd.sem("ffot")
    s_xl = bd.sem("ffxl")
    with ExitStack() as fctx:
        WDT = fctx.enter_context(nc.sbuf_tensor("l3f_wdt", [128, 8 * D], F32))
        BROW = fctx.enter_context(nc.sbuf_tensor("l3f_brow", [1, D], F32))
        CMBX = fctx.enter_context(nc.sbuf_tensor("l3f_cx", [128, D], F32))
        CMBS = fctx.enter_context(nc.sbuf_tensor("l3f_cs", [128, D], F32))
        CMBG = fctx.enter_context(nc.sbuf_tensor("l3f_cg", [128, D], F32))
        XDT = fctx.enter_context(nc.sbuf_tensor("l3f_xdt", [128, 2 * D], F32))
        with ExitStack() as qctx:
            WF = qctx.enter_context(nc.sbuf_tensor("l3f_wf", [128, 8 * D], F32))
            QS = qctx.enter_context(nc.sbuf_tensor("l3f_qs", [128, 2 * D], F32))
            QI = qctx.enter_context(nc.sbuf_tensor("l3f_qi", [128, D], I32))
            QM = qctx.enter_context(nc.sbuf_tensor("l3f_qm", [128, 3 * 32], F32))
            QSG = qctx.enter_context(nc.sbuf_tensor("l3f_qsg", [128, D], F32))
            bd.do("sync", lambda E_: E_.dma_start(out=WF[:].rearrange("p (c n) -> p c n", c=8),
                  in_=w_d.rearrange("(c p) n -> p c n", p=128)), inc=(s_ot, 16))
            bd.do("sync", lambda E_: E_.dma_start(out=BROW[:], in_=bias_d), inc=(s_ot, 16))
            wfr = s_ot.n
            for ch in range(8):
                ws = slice(ch * D, (ch + 1) * D)
                _quant_chain(bd, WF, ws, QS, QI, QM, QSG, s_v, s_g,
                             first_waits=[(s_ot, wfr)])
            s_qt = bd.sem("l3qt")
            for cc in range(8):
                for rr in range(8):
                    gslot = rr % 4
                    tw = []
                    if gslot == 0:
                        tw = [(s_v, None), (s_g, None)]
                    bd.do("tensor", lambda E_, cc=cc, rr=rr, gslot=gslot: E_.transpose(
                          PC[:, gslot * 128:(gslot + 1) * 128],
                          WF[:, rr * D + cc * 128: rr * D + cc * 128 + 128], IDF[:]),
                          inc=(s_t, 1), waits=tw)
                    if gslot == 3 or rr == 7:
                        pass
                    if gslot == 3:
                        bd.do("vector", lambda E_, cc=cc, rr=rr: E_.tensor_copy(
                              WDT[:, cc * D + (rr - 3) * 128: cc * D + (rr + 1) * 128], PC[:, 0:512]),
                              inc=(s_v, 1), waits=[(s_t, None)])
            bd.wait("tensor", s_v)
        for ti in range(NT):
            bf = ti % 2
            xls = slice(bf * D, (bf + 1) * D)
            lw = [(s_t, None)] if ti >= 2 else []
            bd.do("sync", lambda E_, ti=ti, xls=xls: E_.dma_start(
                  out=XDT[:, xls].rearrange("p (c m) -> p c m", c=8),
                  in_=xdt_dram.rearrange("(c p) t -> p c t", p=128)[:, :, ti * 128:(ti + 1) * 128]),
                  inc=(s_xl, 16), waits=lw)
            xlr = s_xl.n
            first = True
            for nb in range(2):
                for dc in range(8):
                    w_ = []
                    if first:
                        w_ = [(s_xl, xlr), (s_v, None)]
                    bd.do("tensor", lambda E_, nb=nb, dc=dc, xls=xls: E_.matmul(
                          PB[:, nb * 512:(nb + 1) * 512], XDT[:, xls.start + dc * 128: xls.start + (dc + 1) * 128],
                          WDT[:, dc * D + nb * 512: dc * D + nb * 512 + 512],
                          start=(dc == 0), stop=False), inc=(s_t, 1), waits=w_)
                    first = False
                bd.do("tensor", lambda E_, nb=nb: E_.matmul(PB[:, nb * 512:(nb + 1) * 512], ONESRF[:],
                      BROW[:, nb * 512:(nb + 1) * 512], start=False, stop=True), inc=(s_t, 1))
            mmd = s_t.n
            clw = [(s_v, None), (bd.sem("ffxo0"), None)] if ti >= 1 else []
            bd.do("sync", lambda E_, ti=ti: E_.dma_start(out=CMBX[:], in_=x_src[ti * 128:(ti + 1) * 128, :]),
                  inc=(s_cl2[0], 16), waits=clw)
            bd.do("sync", lambda E_, ti=ti: E_.dma_start(out=CMBS[:], in_=s_dram[ti * 128:(ti + 1) * 128, :]),
                  inc=(s_cl2[0], 16))
            clr = s_cl2[0].n
            bd.do("scalar", lambda E_: E_.activation(CMBG[:], PB[:, 0:D], AF.Silu),
                  inc=(s_g, 1), waits=[(s_t, mmd), (s_v, None)])
            bd.do("vector", lambda E_: E_.tensor_tensor(CMBS[:], CMBS[:], CMBG[:], op=ALU.mult),
                  inc=(s_v, 1), waits=[(s_g, None), (s_cl2[0], clr)])
            bd.do("vector", lambda E_: E_.tensor_tensor(CMBX[:], CMBX[:], CMBS[:], op=ALU.add),
                  inc=(s_v, 1), waits=[(s_v, None)])
            s_xo = bd.sem("ffxo0")
            bd.do("sync", lambda E_, ti=ti, s_xo=s_xo: E_.dma_start(out=x_dst[ti * 128:(ti + 1) * 128, :], in_=CMBX[:]),
                  inc=(s_xo, 16), waits=[(s_v, None)])
        bd.wait("sync", bd.sem("ffxo0"))
    bd.barrier("ffn3")


def _quant_chain(bd, BUF, ws, QS, QI, QM, QSG, s_v, s_g, first_waits=()):
    D_ = ws.stop - ws.start
    NB = D_ // 32
    bd.do("vector", lambda E_: E_.tensor_reduce(QM[:, 0:NB],
          BUF[:, ws].rearrange("p (b c) -> p b c", c=32), axis=AX.X, op=ALU.max,
          apply_absolute_value=True), inc=(s_v, 1), waits=list(first_waits) + [(s_v, None), (s_g, None)])
    bd.do("vector", lambda E_: E_.tensor_scalar(QM[:, 32:32 + NB], QM[:, 0:NB], 1.0 / 7.5, 1e-30,
          op0=ALU.mult, op1=ALU.max), inc=(s_v, 1), waits=[(s_v, None)])
    bd.do("vector", lambda E_: E_.reciprocal(QM[:, 64:64 + NB], QM[:, 32:32 + NB]), inc=(s_v, 1), waits=[(s_v, None)])
    bd.do("vector", lambda E_: E_.tensor_tensor(QS[:, 0:D_].rearrange("p (b c) -> p b c", c=32),
          BUF[:, ws].rearrange("p (b c) -> p b c", c=32),
          QM[:, 64:64 + NB].unsqueeze(2).to_broadcast([128, NB, 32]), op=ALU.mult),
          inc=(s_v, 1), waits=[(s_v, None)])
    bd.do("scalar", lambda E_: E_.activation(QSG[:, 0:D_], QS[:, 0:D_], AF.Sign), inc=(s_g, 1), waits=[(s_v, None)])
    bd.do("scalar", lambda E_: E_.activation(QS[:, D_:2 * D_], QS[:, 0:D_], AF.Abs), inc=(s_g, 1), waits=[(s_g, None)])
    bd.do("vector", lambda E_: E_.tensor_scalar(BUF[:, ws], QS[:, D_:2 * D_], 2.0, None, op0=ALU.is_ge),
          inc=(s_v, 1), waits=[(s_g, None)])
    bd.do("vector", lambda E_: E_.tensor_scalar(BUF[:, ws], BUF[:, ws], -4.0, 8.0, op0=ALU.mult, op1=ALU.add),
          inc=(s_v, 1), waits=[(s_v, None)])
    bd.do("vector", lambda E_: E_.tensor_scalar(QI[:, 0:D_], QS[:, D_:2 * D_], 4.0, None, op0=ALU.is_ge),
          inc=(s_v, 1), waits=[(s_v, None)])
    bd.do("vector", lambda E_: E_.scalar_tensor_tensor(BUF[:, ws], QI[:, 0:D_], -2.0, BUF[:, ws],
          op0=ALU.mult, op1=ALU.add), inc=(s_v, 1), waits=[(s_v, None)])
    bd.do("vector", lambda E_: E_.tensor_tensor(QS[:, D_:2 * D_], QS[:, D_:2 * D_], BUF[:, ws], op=ALU.mult),
          inc=(s_v, 1), waits=[(s_v, None)])
    bd.do("vector", lambda E_: E_.tensor_copy(QI[:, 0:D_], QS[:, D_:2 * D_]), inc=(s_v, 1), waits=[(s_v, None)])
    bd.do("vector", lambda E_: E_.tensor_copy(QS[:, D_:2 * D_], QI[:, 0:D_]), inc=(s_v, 1), waits=[(s_v, None)])
    bd.do("vector", lambda E_: E_.reciprocal(BUF[:, ws], BUF[:, ws]), inc=(s_v, 1), waits=[(s_v, None)])
    bd.do("vector", lambda E_: E_.tensor_tensor(QS[:, D_:2 * D_], QS[:, D_:2 * D_], BUF[:, ws], op=ALU.mult),
          inc=(s_v, 1), waits=[(s_v, None)])
    bd.do("vector", lambda E_: E_.tensor_tensor(QS[:, D_:2 * D_], QS[:, D_:2 * D_], QSG[:, 0:D_], op=ALU.mult),
          inc=(s_v, 1), waits=[(s_v, None)])
    bd.do("vector", lambda E_: E_.tensor_tensor(BUF[:, ws].rearrange("p (b c) -> p b c", c=32),
          QS[:, D_:2 * D_].rearrange("p (b c) -> p b c", c=32),
          QM[:, 32:32 + NB].unsqueeze(2).to_broadcast([128, NB, 32]), op=ALU.mult),
          inc=(s_v, 1), waits=[(s_v, None)])
